# version 17
# baseline (speedup 1.0000x reference)
import math
import numpy as np
from contextlib import ExitStack

import concourse.bass as bass
import concourse.mybir as mybir
from concourse.bass_utils import run_bass_kernel_spmd

F32 = mybir.dt.float32
BF16 = mybir.dt.bfloat16
AF = mybir.ActivationFunctionType
ALU = mybir.AluOpType
AX = mybir.AxisListType
ET = mybir.EngineType

T = 8192
D = 2048
NT = T // 128
VD = 256
EPS_RMS = 1e-6
EPS_LN = 1e-5
LAMBDA_INIT = 0.8 - 0.6 * math.exp(-0.3 * 6)
OUT_GAIN = 1.0 - LAMBDA_INIT
SCALE = 128.0 ** -0.5
LN104_64 = math.log(10000.0) / 64.0
QBLK = 256
NQB = T // QBLK


def set_dims(t):
    global T, NT, NQB
    T = t
    NT = t // 128
    NQB = t // QBLK


class Ctr:
    def __init__(self, sem):
        self.sem = sem
        self.n = 0

    def inc(self, instr, k=1):
        instr.then_inc(self.sem, k)
        self.n += k
        return instr


class SerialEng:

    def __init__(self, eng, ctr):
        self._e = eng
        self._c = ctr

    def __getattr__(self, name):
        f = getattr(self._e, name)
        if name in ("wait_ge", "nop", "drain", "sem", "then_inc"):
            return f

        def wrapped(*a, **k):
            ins = f(*a, **k)
            self._c.inc(ins)
            self._e.wait_ge(self._c.sem, self._c.n)
            return ins
        return wrapped


def build_program(debug=False, phases=4, skip=(), mock_cc=False):
    nc = bass.Bass("TRN2", target_bir_lowering=False)

    x_d = nc.dram_tensor("x", [T, D], F32, kind="ExternalInput").ap()
    wproj_d = nc.dram_tensor("wproj", [D, 2048], F32, kind="ExternalInput").ap()
    wa_d = nc.dram_tensor("wa", [D, 520], F32, kind="ExternalInput").ap()
    bcat_d = nc.dram_tensor("bcat", [512, D], F32, kind="ExternalInput").ap()
    br_d = nc.dram_tensor("br", [1, 8], F32, kind="ExternalInput").ap()
    lam_d = nc.dram_tensor("lam4", [4, 2, 64], F32, kind="ExternalInput").ap()
    y_out = nc.dram_tensor("y_out", [T // 4, D], F32, kind="ExternalOutput").ap()

    qk_t = nc.dram_tensor("qk_sc", [2, 4, 128, T], BF16)
    v_t = nc.dram_tensor("v_sc", [2, NT, 128, VD], BF16)
    g_t = nc.dram_tensor("g_sc", [2, NT, 128, VD], BF16)
    y_t = nc.dram_tensor("y_sc", [T, 512], BF16)
    cc_t = nc.dram_tensor("cc_sc", [4 * T, 512], BF16)
    qk_d, v_d, g_d, y_d = qk_t.ap(), v_t.ap(), g_t.ap(), y_t.ap()

    dbg = {}
    if debug:
        for nm, shp, dt in [("d_xn", [128, D], BF16), ("d_xnT", [128, D], BF16),
                            ("d_qn", [128, 1024], BF16), ("d_qr", [128, 1024], BF16),
                            ("d_st", [128, 8], F32), ("d_s8", [128, 24], F32),
                            ("d_vgs", [128, 1024], BF16)]:
            dbg[nm] = nc.dram_tensor(nm, shp, dt, kind="ExternalOutput").ap()
        dbg["d_qk"] = nc.dram_tensor("d_qk", [2, 4, 128, T], BF16, kind="ExternalOutput").ap()
        dbg["d_v"] = nc.dram_tensor("d_v", [2, NT, 128, VD], BF16, kind="ExternalOutput").ap()
        dbg["d_g"] = nc.dram_tensor("d_g", [2, NT, 128, VD], BF16, kind="ExternalOutput").ap()
        dbg["d_y"] = nc.dram_tensor("d_y", [T, 512], BF16, kind="ExternalOutput").ap()
        dbg["d_cc"] = nc.dram_tensor("d_cc", [4 * T, 512], BF16, kind="ExternalOutput").ap()
        dbg["d_p4x1"] = nc.dram_tensor("d_p4x1", [128, D], F32, kind="ExternalOutput").ap()
        dbg["d_p4xf"] = nc.dram_tensor("d_p4xf", [128, D], BF16, kind="ExternalOutput").ap()
        dbg["d_p4hm"] = nc.dram_tensor("d_p4hm", [128, 512], BF16, kind="ExternalOutput").ap()
        dbg["d_p4rr"] = nc.dram_tensor("d_p4rr", [128, 32], F32, kind="ExternalOutput").ap()
        dbg["d_p4br"] = nc.dram_tensor("d_p4br", [128, 8], F32, kind="ExternalOutput").ap()
        dbg["d_p4xfT"] = nc.dram_tensor("d_p4xfT", [128, D], BF16, kind="ExternalOutput").ap()
        dbg["d_p4wa"] = nc.dram_tensor("d_p4wa", [128, 1040], BF16, kind="ExternalOutput").ap()
        dbg["d_p4yt"] = nc.dram_tensor("d_p4yt", [128, D], BF16, kind="ExternalOutput").ap()

    est = ExitStack()
    with est:
        ident = est.enter_context(nc.sbuf_tensor("ident", [128, 128], BF16))
        consts = est.enter_context(nc.sbuf_tensor("consts", [128, 8], F32))
        maskA = est.enter_context(nc.sbuf_tensor("maskA", [128, 256], BF16))
        ones_col = est.enter_context(nc.sbuf_tensor("ones_col", [128, 1], BF16))
        lam_b = est.enter_context(nc.sbuf_tensor("lam_b", [128, 2], F32))
        br_b = est.enter_context(nc.sbuf_tensor("br_b", [128, 8], F32))
        wa_sb = est.enter_context(nc.sbuf_tensor("wa_sb", [128, 16 * 520], BF16))
        bc_sb = est.enter_context(nc.sbuf_tensor("bc_sb", [128, 4 * 2048], BF16))

        S = {}
        for nm in ["dma", "dmg", "dmh", "dmi", "dmx", "dmc", "dmq", "u4", "u5", "rb", "rbp", "rbs", "u1", "u2", "u3", "t1", "t2", "t3", "t4", "t5", "t6", "t7", "t8",
                   "t9", "t10", "t11", "t12", "t13", "t14", "t15", "t16", "t17",
                   "t18", "t19", "t20", "t21", "t22"]:
            S[nm] = Ctr(est.enter_context(nc.semaphore(nm)))

        if "p0a" not in skip:
            _p0a(nc, S, ident, maskA, ones_col, consts)
        cs_stack = ExitStack()
        cos_tm = cs_stack.enter_context(nc.sbuf_tensor("cos_tm", [128, NT * 64], BF16))
        sin_tm = cs_stack.enter_context(nc.sbuf_tensor("sin_tm", [128, NT * 64], BF16))
        if "p0b" not in skip:
            _p0b(nc, S, ident, cos_tm, sin_tm, consts, skip)
        wproj = cs_stack.enter_context(nc.sbuf_tensor("wproj_sb", [128, 16 * 2048], BF16))
        if "p0c" not in skip:
            _p0c(nc, S, dict(wproj_d=wproj_d, wa_d=wa_d, bcat_d=bcat_d, br_d=br_d,
                         lam_d=lam_d, wproj=wproj, wa_sb=wa_sb, bc_sb=bc_sb,
                         lam_b=lam_b, br_b=br_b))
        if phases >= 1:
            _p1(nc, S, dict(x_d=x_d, qk_d=qk_d, v_d=v_d, g_d=g_d, wproj=wproj,
                            cos_tm=cos_tm, sin_tm=sin_tm, ident=ident, consts=consts,
                            dbg=dbg if debug else None))
        cs_stack.close()
        xq_done = None
        if phases >= 4:
            ccq_t = nc.dram_tensor("ccq_sc", [4, T // 4, 512], BF16)
            xq_t = nc.dram_tensor("xq_sc", [T // 4, D], F32)
            with nc.Block() as blk:
                @blk.sync
                def _(sync0):
                    pid0 = nc.partition_id([ET.SP])
                    qoff0 = nc.snap((pid0 % 4) * (T // 4), min_val=0,
                                    max_val=3 * (T // 4))
                    S["dmq"].inc(sync0.dma_start(
                        xq_t.ap(), x_d[bass.ds(qoff0, T // 4), :]), 16)
            xq_done = S["dmq"].n
        if phases >= 2:
            _p2(nc, S, dict(qk_d=qk_d, v_d=v_d, g_d=g_d, y_d=y_d, maskA=maskA,
                            ones_col=ones_col, lam_b=lam_b, consts=consts))
        if phases >= 3:
            with nc.Block() as blk:
                @blk.gpsimd
                def _(g):
                    if mock_cc:
                        for rk in range(4):
                            S["t12"].inc(g.dma_start(
                                cc_t.ap()[rk * T:(rk + 1) * T, :], y_d), 16)
                        g.wait_ge(S["t12"].sem, S["t12"].n)
                    else:
                        S["t12"].inc(g.collective_compute(
                            "AllGather", ALU.bypass,
                            replica_groups=[[0, 1, 2, 3], [4, 5, 6, 7]],
                            ins=[y_t.ap().opt()], outs=[cc_t.ap().opt()]))
                        g.wait_ge(S["t12"].sem, S["t12"].n)
        if phases >= 4:
            _p4(nc, S, dict(x_d=x_d, cc_d=cc_t.ap(), y_out=y_out, wa_sb=wa_sb,
                            bc_sb=bc_sb, br_b=br_b, ident=ident, consts=consts,
                            maskA=maskA, ccq=ccq_t.ap(), xq=xq_t.ap(),
                            xq_done=xq_done, dbg=dbg if debug else None))

        if debug:
            with nc.Block() as blk:
                @blk.sync
                def _(sync):
                    S["dma"].inc(sync.dma_start(dbg["d_qk"], qk_d), 16)

                    S["dma"].inc(sync.dma_start(dbg["d_v"], v_d), 16)
                    S["dma"].inc(sync.dma_start(dbg["d_g"], g_d), 16)
                    if phases >= 2:
                        S["dma"].inc(sync.dma_start(dbg["d_y"], y_d), 16)
                    if phases >= 3:
                        S["dma"].inc(sync.dma_start(dbg["d_cc"], cc_t.ap()), 16)
                    sync.wait_ge(S["dma"].sem, S["dma"].n)

        if phases < 4:
            with nc.Block() as blk, ExitStack() as tst:
                zz = tst.enter_context(nc.sbuf_tensor("zz", [128, D], F32))

                @blk.gpsimd
                def _(g):
                    S["t13"].inc(g.memset(zz[:], 0.0))

                @blk.sync
                def _(sync):
                    sync.wait_ge(S["t13"].sem, S["t13"].n)
                    for i in range(T // 512):
                        S["dma"].inc(sync.dma_start(y_out[i * 128:(i + 1) * 128, :], zz[:]), 16)
                    sync.wait_ge(S["dma"].sem, S["dma"].n)

    return nc, dbg


def _p0a(nc, S, ident, maskA, ones_col, consts):
    with nc.Block() as blk:
        @blk.gpsimd
        def _(g):
            g = SerialEng(g, S["rbp"])
            g.memset(ident[:], 0.0)
            g.affine_select(out=ident[:], in_=ident[:], compare_op=ALU.not_equal,
                            fill=1.0, base=0, pattern=[[-1, 128]], channel_multiplier=1)
            g.memset(maskA[:], 1.0)
            g.affine_select(out=maskA[:], in_=maskA[:], compare_op=ALU.is_ge,
                            fill=0.0, base=0, pattern=[[1, 256]], channel_multiplier=-1)
            g.memset(ones_col[:], 1.0)
            g.memset(consts[:, 0:1], EPS_LN)
            g.memset(consts[:, 1:2], EPS_RMS)
            g.memset(consts[:, 2:3], math.pi / 2)
            g.memset(consts[:, 3:4], math.log(OUT_GAIN))
            g.memset(consts[:, 4:5], 1.0)


def _p0b(nc, S, ident, cos_tm, sin_tm, consts, skip=()):
    with nc.Block() as blk, ExitStack() as tst:
        p_tp2 = [tst.enter_context(nc.psum_tensor(f"p_tp0{i}", [128, 1024], BF16))
                 for i in range(2)]
        iot64 = tst.enter_context(nc.sbuf_tensor("iot64", [64, 1], F32))
        invf = tst.enter_context(nc.sbuf_tensor("invf", [64, 1], F32))
        cn = tst.enter_context(nc.sbuf_tensor("cn", [64, 1], F32))
        sn = tst.enter_context(nc.sbuf_tensor("sn", [64, 1], F32))
        nsn = tst.enter_context(nc.sbuf_tensor("nsn", [64, 1], F32))
        tsc = tst.enter_context(nc.sbuf_tensor("tsc", [64, 2], F32))
        kvec = tst.enter_context(nc.sbuf_tensor("kvec", [64, 1], F32))
        stp = tst.enter_context(nc.sbuf_tensor("stp", [64, 1], F32))
        onec = tst.enter_context(nc.sbuf_tensor("onec", [64, 1], F32))
        car = tst.enter_context(nc.sbuf_tensor("car", [64, 64], F32))
        sar = tst.enter_context(nc.sbuf_tensor("sar", [64, 64], F32))
        caa = tst.enter_context(nc.sbuf_tensor("caa", [64, T // 64], F32))
        saa = tst.enter_context(nc.sbuf_tensor("saa", [64, T // 64], F32))
        tmp8 = tst.enter_context(nc.sbuf_tensor("tmp8", [64, T], F32))
        cbf = tst.enter_context(nc.sbuf_tensor("cbf", [128, T], BF16))
        sbf = tst.enter_context(nc.sbuf_tensor("sbf", [128, T], BF16))
        cf32 = tst.enter_context(nc.sbuf_tensor("cf32", [64, T], F32))
        sf32 = tst.enter_context(nc.sbuf_tensor("sf32", [64, T], F32))

        @blk.gpsimd
        def _(g):
            g = SerialEng(g, S["rbp"])
            g.iota(iot64[:], [[0, 1]], channel_multiplier=1,
                   allow_small_or_imprecise_dtypes=True)
            g.memset(cbf[:], 0.0)
            g.memset(sbf[:], 0.0)
            g.memset(kvec[:], 0.0)
            g.memset(onec[:], 1.0)
            kjs = [int(64.0 * 10000.0 ** (-j / 64.0) / (2 * math.pi)) for j in range(64)]
            m = 1
            while any(k >= m for k in kjs):
                jmax = max(j for j in range(64) if kjs[j] >= m)
                g.affine_select(out=stp[:], in_=onec[:], compare_op=ALU.is_ge,
                                fill=0.0, base=jmax, pattern=[[0, 1]],
                                channel_multiplier=-1)
                g.tensor_add(kvec[:], kvec[:], stp[:])
                m += 1
            gi = g.memset(nsn[:], 0.0)
            S["t1"].inc(g._e.memset(tsc[:, 0:1], 0.0))

        @blk.scalar
        def _(s):
            s.wait_ge(S["t1"].sem, S["t1"].n)
            S["t2"].inc(s.activation(invf[:], iot64[:], AF.Exp, scale=-LN104_64))

        def taylor_sincos(v, phi, out_c, out_s, u, hi):
            v.tensor_mul(u, phi, phi)
            v.memset(out_s, -1.0 / 121645100408832000.0)
            for ck in [1.0 / 355687428096000.0, -1.0 / 1307674368000.0,
                       1.0 / 6227020800.0, -1.0 / 39916800.0, 1.0 / 362880.0,
                       -1.0 / 5040.0, 1.0 / 120.0, -1.0 / 6.0, 1.0]:
                v.tensor_mul(out_s, out_s, u)
                v.tensor_scalar_add(out_s, out_s, ck)
            v.tensor_mul(out_s, out_s, phi)
            v.memset(out_c, 1.0 / 6402373705728000.0)
            for ck in [-1.0 / 87178291200.0, 1.0 / 479001600.0, -1.0 / 3628800.0,
                       1.0 / 40320.0, -1.0 / 720.0, 1.0 / 24.0, -1.0 / 2.0, 1.0]:
                v.tensor_mul(out_c, out_c, u)
                v.tensor_scalar_add(out_c, out_c, ck)
            _ = hi

        @blk.vector
        def _(v):
            vraw = v
            v = SerialEng(v, S["rb"])
            vraw.wait_ge(S["t2"].sem, S["t2"].n)
            NA = T // 64
            u = tsc[:, 0:1]
            taylor_sincos(v, invf[:], cn[:], sn[:], u, None)
            v.tensor_scalar_mul(nsn[:], sn[:], -1.0)
            v.memset(car[:, 0:1], 1.0)
            v.memset(sar[:, 0:1], 0.0)
            for r in range(63):
                v.tensor_scalar_mul(car[:, r + 1:r + 2], car[:, r:r + 1], cn[:])
                v.scalar_tensor_tensor(car[:, r + 1:r + 2], sar[:, r:r + 1], nsn[:],
                                       car[:, r + 1:r + 2], ALU.mult, ALU.add)
                v.tensor_scalar_mul(sar[:, r + 1:r + 2], sar[:, r:r + 1], cn[:])
                v.scalar_tensor_tensor(sar[:, r + 1:r + 2], car[:, r:r + 1], sn[:],
                                       sar[:, r + 1:r + 2], ALU.mult, ALU.add)
            v.tensor_scalar(tsc[:, 1:2], kvec[:], -2.0 * math.pi, -math.pi,
                            ALU.mult, ALU.add)
            v.scalar_tensor_tensor(tsc[:, 1:2], invf[:], 64.0, tsc[:, 1:2],
                                   ALU.mult, ALU.add)
            taylor_sincos(v, tsc[:, 1:2], cn[:], sn[:], u, None)
            v.tensor_scalar_mul(cn[:], cn[:], -1.0)
            v.tensor_scalar_mul(nsn[:], sn[:], 1.0)
            v.tensor_scalar_mul(sn[:], sn[:], -1.0)
            v.memset(caa[:, 0:1], 1.0)
            v.memset(saa[:, 0:1], 0.0)
            for a in range(NA - 1):
                v.tensor_scalar_mul(caa[:, a + 1:a + 2], caa[:, a:a + 1], cn[:])
                v.scalar_tensor_tensor(caa[:, a + 1:a + 2], saa[:, a:a + 1], nsn[:],
                                       caa[:, a + 1:a + 2], ALU.mult, ALU.add)
                v.tensor_scalar_mul(saa[:, a + 1:a + 2], saa[:, a:a + 1], cn[:])
                v.scalar_tensor_tensor(saa[:, a + 1:a + 2], caa[:, a:a + 1], sn[:],
                                       saa[:, a + 1:a + 2], ALU.mult, ALU.add)
            cavw = caa[:].rearrange("p (a o) -> p a o", o=1).broadcast_to([64, NA, 64])
            savw = saa[:].rearrange("p (a o) -> p a o", o=1).broadcast_to([64, NA, 64])
            crvw = car[:].rearrange("p (o r) -> p o r", o=1).broadcast_to([64, NA, 64])
            srvw = sar[:].rearrange("p (o r) -> p o r", o=1).broadcast_to([64, NA, 64])
            cfv = cf32[:].rearrange("p (a r) -> p a r", r=64)
            sfv = sf32[:].rearrange("p (a r) -> p a r", r=64)
            v.tensor_mul(cfv, cavw, crvw)
            v.tensor_mul(tmp8[:].rearrange("p (a r) -> p a r", r=64), savw, srvw)
            v.tensor_sub(cf32[:], cf32[:], tmp8[:])
            v.tensor_mul(sfv, savw, crvw)
            v.tensor_mul(tmp8[:].rearrange("p (a r) -> p a r", r=64), cavw, srvw)
            v.tensor_add(sf32[:], sf32[:], tmp8[:])
            v.tensor_copy(cbf[0:64, :], cf32[:])
            v.tensor_copy(sbf[0:64, :], sf32[:])
            S["t3"].inc(vraw.tensor_copy(tsc[:, 0:1], tsc[:, 0:1]))

        if "p0b_tp" not in skip:
            @blk.tensor
            def _(t):
                t.wait_ge(S["t3"].sem, S["t3"].n)
                for tt in range(NT):
                    pt = p_tp2[tt % 2]
                    if tt >= 2:
                        t.wait_ge(S["t4"].sem, tt - 1)
                    t.transpose(pt[:, 0:128], cbf[:, tt * 128:(tt + 1) * 128], ident[:])
                    S["t5"].inc(t.transpose(pt[:, 128:256],
                                            sbf[:, tt * 128:(tt + 1) * 128], ident[:]))

            @blk.vector
            def _(v):
                for tt in range(NT):
                    pt = p_tp2[tt % 2]
                    v.wait_ge(S["t5"].sem, tt + 1)
                    v.tensor_copy(cos_tm[:, tt * 64:(tt + 1) * 64], pt[:, 0:64])
                    S["t4"].inc(v.tensor_copy(sin_tm[:, tt * 64:(tt + 1) * 64],
                                              pt[:, 128:192]))


def _p0c(nc, S, env):
    wproj_d, wa_d, bcat_d = env["wproj_d"], env["wa_d"], env["bcat_d"]
    br_d, lam_d = env["br_d"], env["lam_d"]
    wproj, wa_sb, bc_sb = env["wproj"], env["wa_sb"], env["bc_sb"]
    lam_b, br_b = env["lam_b"], env["br_b"]
    d0, w6 = S["dma"].n, S["t6"].n

    with nc.Block() as blk, ExitStack() as tst:
        p_bc = tst.enter_context(nc.psum_tensor("p_bc", [128, 512], F32))
        wstg = [tst.enter_context(nc.sbuf_tensor(f"wstg{i}", [128, 16 * 520], F32))
                for i in range(2)]
        lamq = tst.enter_context(nc.sbuf_tensor("lamq", [2, 4 * 64], F32))
        lamc = tst.enter_context(nc.sbuf_tensor("lamc", [2, 8], F32))
        brs = tst.enter_context(nc.sbuf_tensor("brs", [1, 8], F32))
        onerow = tst.enter_context(nc.sbuf_tensor("onerow", [2, 128], F32))

        @blk.sync
        def _(sync):
            for gg in range(4):
                if gg >= 2:
                    sync.wait_ge(S["t6"].sem, w6 + gg - 1)
                S["dma"].inc(sync.dma_start(
                    wstg[gg % 2][:, 0:8192].rearrange("p (c f) -> p c f", c=4),
                    wproj_d.rearrange("(c p) f -> p c f", p=128)[:, gg * 4:(gg + 1) * 4, :]), 16)
            sync.wait_ge(S["t6"].sem, w6 + 4)
            S["dma"].inc(sync.dma_start(
                wstg[0][:, 0:16 * 520].rearrange("p (c f) -> p c f", c=16),
                wa_d.rearrange("(c p) f -> p c f", p=128)), 16)
            S["dma"].inc(sync.dma_start(
                wstg[1][:, 0:8192].rearrange("p (c f) -> p c f", c=4),
                bcat_d.rearrange("(c p) f -> p c f", p=128)), 16)
            S["dma"].inc(sync.dma_start(brs[:], br_d), 16)
            S["dma"].inc(sync.dma_start(
                lamq[:].rearrange("p (c f) -> p c f", c=4),
                lam_d.rearrange("c p f -> p c f")), 16)

        @blk.vector
        def _(v):
            for gg in range(4):
                v.wait_ge(S["dma"].sem, d0 + (gg + 1) * 16)
                S["t6"].inc(v.tensor_copy(
                    wproj[:, gg * 4 * 2048:(gg + 1) * 4 * 2048], wstg[gg % 2][:, 0:8192]))
            v.wait_ge(S["dma"].sem, d0 + 8 * 16)
            v.tensor_copy(wa_sb[:], wstg[0][:, 0:16 * 520])
            v.tensor_copy(bc_sb[:], wstg[1][:, 0:8192])
            sv = SerialEng(v, S["rb"])
            sv.tensor_mul(lamq[:, 0:64], lamq[:, 0:64], lamq[:, 64:128])
            sv.tensor_mul(lamq[:, 128:192], lamq[:, 128:192], lamq[:, 192:256])
            sv.reduce_sum(lamc[:, 0:1], lamq[:, 0:64], axis=AX.X)
            sv.reduce_sum(lamc[:, 1:2], lamq[:, 128:192], axis=AX.X)
            S["t7"].inc(v.tensor_copy(lamc[:, 0:2], lamc[:, 0:2]))

        @blk.scalar
        def _(s):
            s.wait_ge(S["t7"].sem, S["t7"].n)
            ss = SerialEng(s, S["rbs"])
            ss.activation(lamc[:, 2:3], lamc[:, 0:1], AF.Exp)
            ss.activation(lamc[:, 3:4], lamc[:, 1:2], AF.Exp)
            S["t8"].inc(s.activation(lamc[:, 2:4], lamc[:, 2:4], AF.Identity))

        @blk.gpsimd
        def _(g):
            graw = g
            g = SerialEng(g, S["rbp"])
            g.memset(onerow[:], 1.0)
            graw.wait_ge(S["t8"].sem, S["t8"].n)
            g.tensor_sub(lamc[:, 4:5], lamc[:, 2:3], lamc[:, 3:4])
            g.tensor_scalar_add(lamc[:, 4:5], lamc[:, 4:5], LAMBDA_INIT)
            g.tensor_copy(lamc[:, 6:8], lamc[:, 4:5].broadcast_to([2, 2]))
            g.affine_select(
                out=lamc[:, 6:8], in_=lamc[:, 6:8], compare_op=ALU.is_equal,
                fill=0.0, base=0, pattern=[[-1, 2]], channel_multiplier=1)
            S["t9"].inc(graw.tensor_copy(lamc[:, 6:8], lamc[:, 6:8]))

        @blk.tensor
        def _(t):
            t.wait_ge(S["t9"].sem, S["t9"].n)
            t.matmul(p_bc[:, 0:2], onerow[:], lamc[:, 6:8], start=True, stop=True)
            S["t10"].inc(t.matmul(p_bc[:, 8:16], onerow[0:1, :], brs[:],
                                  start=True, stop=True))

        @blk.vector
        def _(v):
            v.wait_ge(S["t10"].sem, S["t10"].n)
            v.tensor_copy(lam_b[:], p_bc[:, 0:2])
            S["t11"].inc(v.tensor_copy(br_b[:], p_bc[:, 8:16]))


def _p1(nc, S, env):
    x_d, qk_d, v_d, g_d = env["x_d"], env["qk_d"], env["v_d"], env["g_d"]
    wproj, cos_tm, sin_tm, ident = env["wproj"], env["cos_tm"], env["sin_tm"], env["ident"]
    consts = env["consts"]
    dbg_aps = env.get("dbg")

    with nc.Block() as blk, ExitStack() as tst:
        p_tpA = tst.enter_context(nc.psum_tensor("p1_tpA", [128, 1024], BF16))
        p_tpB = tst.enter_context(nc.psum_tensor("p1_tpB", [128, 1024], BF16))
        p_prj = [tst.enter_context(nc.psum_tensor(f"p1_prj{i}", [128, 512], F32))
                 for i in range(4)]
        xf32 = [tst.enter_context(nc.sbuf_tensor(f"p1_x{i}", [128, D], F32)) for i in range(3)]
        xn = [tst.enter_context(nc.sbuf_tensor(f"p1_xn{i}", [128, D], BF16)) for i in range(2)]
        xnT = [tst.enter_context(nc.sbuf_tensor(f"p1_xnT{i}", [128, D], BF16)) for i in range(2)]
        junk = tst.enter_context(nc.sbuf_tensor("p1_junk", [128, D], BF16))
        stats = [tst.enter_context(nc.sbuf_tensor(f"p1_st{i}", [128, 8], F32)) for i in range(2)]
        q2 = [tst.enter_context(nc.sbuf_tensor(f"p1_q2{i}", [128, 1024], F32)) for i in range(2)]
        s8 = [tst.enter_context(nc.sbuf_tensor(f"p1_s8{i}", [128, 24], F32)) for i in range(2)]
        qn = [tst.enter_context(nc.sbuf_tensor(f"p1_qn{i}", [128, 1024], BF16)) for i in range(2)]
        qrot = [tst.enter_context(nc.sbuf_tensor(f"p1_qr{i}", [128, 1024], BF16)) for i in range(2)]
        rsc = tst.enter_context(nc.sbuf_tensor("p1_rsc", [128, 1024], BF16))
        qkTs = [tst.enter_context(nc.sbuf_tensor(f"p1_qkT{i}", [128, 1024], BF16)) for i in range(2)]
        vgs = [tst.enter_context(nc.sbuf_tensor(f"p1_vg{i}", [128, 1024], BF16)) for i in range(2)]
        qc = tst.enter_context(nc.sbuf_tensor("p1_qc", [128, 1024], F32))
        gtmp = tst.enter_context(nc.sbuf_tensor("p1_gt", [128, 512], F32))
        esb = tst.enter_context(nc.sbuf_tensor("p1_es", [128, 512], F32))

        base = {k: S[k].n for k in S}

        def spill(sync, t):
            pr = t % 2
            snm = "dmg" if t % 2 == 0 else "dmi"
            tnm = "dmc" if t % 2 == 0 else "dmq"
            sync.wait_ge(S[tnm].sem, base[tnm] + 128 * (t // 2 + 1))
            for i in range(8):
                hi, isq = i // 2, (i % 2 == 0)
                pair, slot = hi // 2, (hi % 2) if isq else 2 + (hi % 2)
                S[snm].inc(sync.dma_start(
                    qk_d[pair, slot, :, t * 128:(t + 1) * 128],
                    qkTs[pr][:, i * 128:(i + 1) * 128]), 16)
            sync.wait_ge(S["t5"].sem, base["t5"] + t + 1)
            sync.wait_ge(S["u4"].sem, base["u4"] + t + 1)
            for p in range(2):
                S[snm].inc(sync.dma_start(
                    v_d[p, t, :, :], vgs[pr][:, p * 256:(p + 1) * 256]), 16)
                S[snm].inc(sync.dma_start(
                    g_d[p, t, :, :], vgs[pr][:, 512 + p * 256:512 + (p + 1) * 256]), 16)

        @blk.sync
        def _(sync):
            for t in range(NT + 3):
                if t < NT:
                    if t >= 3:
                        sync.wait_ge(S["t15"].sem, base["t15"] + t - 2)
                        sync.wait_ge(S["t14"].sem, base["t14"] + t - 2)
                    S[["dma", "dmh", "dmx"][t % 3]].inc(
                        sync.dma_start(xf32[t % 3][:], x_d[t * 128:(t + 1) * 128, :]), 16)
                if 0 <= t - 2 < NT:
                    w = t - 2
                    tnm = "dmc" if w % 2 == 0 else "dmq"
                    sync.wait_ge(S["t6"].sem, base["t6"] + w + 1)
                    if w >= 2:
                        c3 = "dmg" if (w - 2) % 2 == 0 else "dmi"
                        sync.wait_ge(S[c3].sem, base[c3] + 192 * ((w - 2) // 2 + 1))
                    for i in range(8):
                        S[tnm].inc(sync.dma_start_transpose(
                            qkTs[w % 2][:, i * 128:(i + 1) * 128],
                            qrot[w % 2][:, i * 128:(i + 1) * 128]), 16)
                if t >= 3:
                    spill(sync, t - 3)
            sync.wait_ge(S["dmg"].sem, S["dmg"].n)
            sync.wait_ge(S["dmi"].sem, S["dmi"].n)
            if dbg_aps:
                pr_l = (NT - 1) % 2
                for nm, tens in [("d_xn", xn[pr_l]), ("d_xnT", xnT[pr_l]),
                                 ("d_qn", qn[pr_l]), ("d_qr", qrot[pr_l]),
                                 ("d_st", stats[pr_l]), ("d_s8", s8[pr_l]),
                                 ("d_vgs", vgs[pr_l])]:
                    S["dmg"].inc(sync.dma_start(dbg_aps[nm], tens[:]), 16)
            sync.wait_ge(S["dmg"].sem, S["dmg"].n)

        @blk.vector
        def _(v):
            for t in range(NT + 2):
                if t < NT:
                    pr3 = t % 3
                    st = stats[t % 2]
                    lnm = ["dma", "dmh", "dmx"][t % 3]
                    v.wait_ge(S[lnm].sem, base[lnm] + 16 * (t // 3 + 1))
                    S["rb"].inc(v.reduce_sum(st[:, 0:1], xf32[pr3][:], axis=AX.X))
                    v.wait_ge(S["rb"].sem, S["rb"].n)
                    S["t14"].inc(v.tensor_scalar_mul(st[:, 2:3], st[:, 0:1], 1.0 / D))
                    v.wait_ge(S["t16"].sem, base["t16"] + t + 1)
                    S["rb"].inc(v.tensor_mul(st[:, 3:4], st[:, 2:3], st[:, 2:3]))
                    v.wait_ge(S["rb"].sem, S["rb"].n)
                    S["t17"].inc(v.scalar_tensor_tensor(st[:, 4:5], st[:, 1:2], 1.0 / D,
                                                        st[:, 3:4], ALU.mult, ALU.subtract))
                    v.wait_ge(S["t18"].sem, base["t18"] + t + 1)
                    S["t19"].inc(v.scalar_tensor_tensor(st[:, 6:7], st[:, 2:3], -1.0,
                                                        st[:, 5:6], ALU.mult, ALU.mult))
                    v.wait_ge(S["t20"].sem, base["t20"] + 2 * t + 1)
                    S["t21"].inc(v.tensor_copy(xnT[t % 2][:, 0:1024], p_tpA[:]))
                    v.wait_ge(S["t20"].sem, base["t20"] + 2 * t + 2)
                    S["t21"].inc(v.tensor_copy(xnT[t % 2][:, 1024:2048], p_tpB[:]))
                if 0 <= t - 1 < NT:
                    u = t - 1
                    pu = u % 2
                    v.wait_ge(S["t1"].sem, base["t1"] + u + 1)
                    S["rb"].inc(v.reduce_sum(
                        s8[pu][:, 0:8], q2[pu][:].rearrange("p (s x) -> p s x", s=8),
                        axis=AX.X))
                    v.wait_ge(S["rb"].sem, S["rb"].n)
                    S["t2"].inc(v.tensor_scalar(s8[pu][:, 8:16], s8[pu][:, 0:8],
                                                1.0 / 128.0, EPS_RMS, ALU.mult, ALU.add))
                    v.wait_ge(S["t3"].sem, base["t3"] + u + 1)
                    v.tensor_tensor(
                        qn[pu][:, 0:512].rearrange("p (s x) -> p s x", s=4),
                        qc[:, 0:512].rearrange("p (s x) -> p s x", s=4),
                        s8[pu][:, 16:20].rearrange("p (s o) -> p s o", o=1
                                                   ).broadcast_to([128, 4, 128]),
                        ALU.mult)
                    S["t4"].inc(v.tensor_tensor(
                        qn[pu][:, 512:1024].rearrange("p (s x) -> p s x", s=4),
                        qc[:, 512:1024].rearrange("p (s x) -> p s x", s=4),
                        s8[pu][:, 20:24].rearrange("p (s o) -> p s o", o=1
                                                   ).broadcast_to([128, 4, 128]),
                        ALU.mult))
                    v.wait_ge(S["t5"].sem, base["t5"] + u + 1)
                    S["rb"].inc(v.tensor_scalar_add(esb[:], esb[:], 1.0))
                    v.wait_ge(S["rb"].sem, S["rb"].n)
                    S["rb"].inc(v.reciprocal(esb[:], esb[:]))
                    v.wait_ge(S["rb"].sem, S["rb"].n)
                    S["u4"].inc(v.tensor_mul(vgs[pu][:, 512:1024], gtmp[:], esb[:]))

        @blk.scalar
        def _(s):
            for t in range(NT + 1):
                if t < NT:
                    pr3 = t % 3
                    st = stats[t % 2]
                    lnm = ["dma", "dmh", "dmx"][t % 3]
                    s.wait_ge(S[lnm].sem, base[lnm] + 16 * (t // 3 + 1))
                    S["t16"].inc(s.activation(junk[:], xf32[pr3][:], AF.Square,
                                              accum_out=st[:, 1:2]))
                    s.wait_ge(S["t17"].sem, base["t17"] + t + 1)
                    S["rbs"].inc(s.activation(st[:, 7:8], st[:, 4:5], AF.Ln,
                                              bias=consts[:, 0:1]))
                    s.wait_ge(S["rbs"].sem, S["rbs"].n)
                    S["t18"].inc(s.activation(st[:, 5:6], st[:, 7:8], AF.Exp, scale=-0.5))
                    s.wait_ge(S["t19"].sem, base["t19"] + t + 1)
                    if t >= 2:
                        s.wait_ge(S["t20"].sem, base["t20"] + 2 * (t - 2) + 2)
                    S["t15"].inc(s.activation(xn[t % 2][:], xf32[pr3][:], AF.Identity,
                                              bias=st[:, 6:7], scale=st[:, 5:6]))
                if 0 <= t - 1 < NT:
                    u = t - 1
                    pu = u % 2
                    s.wait_ge(S["t22"].sem, base["t22"] + u + 1)
                    if u >= 1:
                        s.wait_ge(S["t4"].sem, base["t4"] + u)
                        s.wait_ge(S["u4"].sem, base["u4"] + u)
                    if u >= 2:
                        c2 = "dmg" if u % 2 == 0 else "dmi"
                        s.wait_ge(S[c2].sem, base[c2] + 192 * ((u - 2) // 2 + 1))
                    s.activation(qc[:, 0:512], p_prj[0][:], AF.Copy)
                    s.activation(qc[:, 512:1024], p_prj[1][:], AF.Copy)
                    s.activation(vgs[pu][:, 0:512], p_prj[2][:], AF.Copy)
                    s.activation(gtmp[:], p_prj[3][:], AF.Copy)
                    S["t5"].inc(s.activation(esb[:], p_prj[3][:], AF.Exp, scale=-1.0))
                    s.activation(q2[pu][:, 0:512], qc[:, 0:512], AF.Square)
                    S["t1"].inc(s.activation(q2[pu][:, 512:1024], qc[:, 512:1024],
                                             AF.Square))
                    s.wait_ge(S["t2"].sem, base["t2"] + u + 1)
                    S["rbs"].inc(s.activation(s8[pu][:, 16:24], s8[pu][:, 8:16], AF.Ln))
                    s.wait_ge(S["rbs"].sem, S["rbs"].n)
                    S["t3"].inc(s.activation(s8[pu][:, 16:24], s8[pu][:, 16:24], AF.Exp,
                                             scale=-0.5))

        @blk.tensor
        def _(t_):
            for t in range(NT + 2):
                if t < NT:
                    t_.wait_ge(S["t15"].sem, base["t15"] + t + 1)
                    if t >= 1:
                        t_.wait_ge(S["t21"].sem, base["t21"] + 2 * (t - 1) + 1)
                    for i in range(8):
                        ins = t_.transpose(p_tpA[:, i * 128:(i + 1) * 128],
                                           xn[t % 2][:, i * 128:(i + 1) * 128], ident[:])
                        if i == 7:
                            S["t20"].inc(ins)
                    if t >= 1:
                        t_.wait_ge(S["t21"].sem, base["t21"] + 2 * (t - 1) + 2)
                    for i in range(8):
                        ins = t_.transpose(p_tpB[:, i * 128:(i + 1) * 128],
                                           xn[t % 2][:, (8 + i) * 128:(9 + i) * 128], ident[:])
                        if i == 7:
                            S["t20"].inc(ins)
                    t_.wait_ge(S["t21"].sem, base["t21"] + 2 * t + 2)
                    if t >= 1:
                        t_.wait_ge(S["t5"].sem, base["t5"] + t)
                    for c in range(16):
                        for sx in range(4):
                            ins = t_.matmul(
                                p_prj[sx][:], xnT[t % 2][:, c * 128:(c + 1) * 128],
                                wproj[:, c * 2048 + sx * 512:c * 2048 + (sx + 1) * 512],
                                start=(c == 0), stop=(c == 15))
                            if c == 15 and sx == 3:
                                S["t22"].inc(ins)

        @blk.gpsimd
        def _(g):
            for t in range(1, NT + 1):
                u = t - 1
                pu = u % 2
                g.wait_ge(S["t4"].sem, base["t4"] + u + 1)
                if u >= 2:
                    tn2 = "dmc" if (u - 2) % 2 == 0 else "dmq"
                    g.wait_ge(S[tn2].sem, base[tn2] + 128 * ((u - 2) // 2 + 1))
                qv = qn[pu][:].rearrange("p (s h x) -> p s h x", s=8, h=2)
                rv = qrot[pu][:].rearrange("p (s h x) -> p s h x", s=8, h=2)
                sc = rsc[:, 0:512].rearrange("p (s o x) -> p s o x", s=8, o=1)
                sc2 = rsc[:, 512:1024].rearrange("p (s o x) -> p s o x", s=8, o=1)
                c_t = cos_tm[:, u * 64:(u + 1) * 64].rearrange(
                    "p (a o x) -> p a o x", a=1, o=1).broadcast_to([128, 8, 1, 64])
                s_t = sin_tm[:, u * 64:(u + 1) * 64].rearrange(
                    "p (a o x) -> p a o x", a=1, o=1).broadcast_to([128, 8, 1, 64])
                g.tensor_mul(rv[:, :, 0:1, :], qv[:, :, 0:1, :], c_t)
                g.tensor_mul(sc, qv[:, :, 1:2, :], s_t)
                S["rbp"].inc(g.tensor_mul(sc2, qv[:, :, 0:1, :], s_t))
                g.wait_ge(S["rbp"].sem, S["rbp"].n)
                g.tensor_add(rv[:, :, 0:1, :], rv[:, :, 0:1, :], sc)
                S["rbp"].inc(g.tensor_mul(rv[:, :, 1:2, :], qv[:, :, 1:2, :], c_t))
                g.wait_ge(S["rbp"].sem, S["rbp"].n)
                S["t6"].inc(g.tensor_sub(rv[:, :, 1:2, :], rv[:, :, 1:2, :], sc2))


def _p2(nc, S, env):
    qk_d, v_d, g_d, y_d = env["qk_d"], env["v_d"], env["g_d"], env["y_d"]
    maskA, ones_col, lam_b = env["maskA"], env["ones_col"], env["lam_b"]
    consts = env["consts"]

    with ExitStack() as tst:
        qs_ = [tst.enter_context(nc.sbuf_tensor(f"p2_q{i}", [128, T], BF16)) for i in range(2)]
        ks_ = [tst.enter_context(nc.sbuf_tensor(f"p2_k{i}", [128, T], BF16)) for i in range(2)]
        v_sb = tst.enter_context(nc.sbuf_tensor("p2_v", [128, (T // 128) * VD], BF16))
        g_sb = tst.enter_context(nc.sbuf_tensor("p2_g", [128, (T // 128) * VD], BF16))
        P2b = [tst.enter_context(nc.sbuf_tensor(f"p2_P{i}", [128, 512], BF16)) for i in range(2)]
        eR = tst.enter_context(nc.sbuf_tensor("p2_eR", [128, 8], F32))
        ey = tst.enter_context(nc.sbuf_tensor("p2_ey", [128, 256], F32))
        ey2 = tst.enter_context(nc.sbuf_tensor("p2_ey2", [128, 256], F32))
        eyg = tst.enter_context(nc.sbuf_tensor("p2_eyg", [128, 512], F32))
        esq = tst.enter_context(nc.sbuf_tensor("p2_esq", [128, 256], BF16))
        es = tst.enter_context(nc.sbuf_tensor("p2_es", [128, 8], F32))
        ybf = [tst.enter_context(nc.sbuf_tensor(f"p2_ybf{i}", [128, 256], BF16)) for i in range(2)]

        for pr in range(2):
            with nc.Block() as blk, ExitStack() as pst:
                p_s = [pst.enter_context(nc.psum_tensor(f"p2_s{pr}{i}", [128, 512], F32))
                       for i in range(2)]
                p_o = [pst.enter_context(nc.psum_tensor(f"p2_o{pr}{i}", [128, 512], F32))
                       for i in range(4)]
                p_z = [pst.enter_context(nc.psum_tensor(f"p2_z{pr}{i}", [128, 4], F32))
                       for i in range(2)]
                base = {k: S[k].n for k in S}

                iters = []
                for j in range(NQB):
                    for kt in range(2 * j + 2):
                        kind = "full" if kt < 2 * j else ("diag0" if kt == 2 * j else "half")
                        iters.append((j, kt, kind))
                n_it = len(iters)
                cum_it = {}
                c = 0
                for (j, kt, kind) in iters:
                    c += 1
                    if kt == 2 * j + 1:
                        cum_it[j] = c

                @blk.sync
                def _(sync):
                    S["dma"].inc(sync.dma_start(ks_[0][:], qk_d[pr, 2]), 16)
                    S["dma"].inc(sync.dma_start(ks_[1][:], qk_d[pr, 3]), 16)
                    S["dma"].inc(sync.dma_start(qs_[0][:], qk_d[pr, 0]), 16)
                    S["dma"].inc(sync.dma_start(qs_[1][:], qk_d[pr, 1]), 16)
                    S["dmx"].inc(sync.dma_start(
                        v_sb[:].rearrange("p (t f) -> p t f", f=VD),
                        v_d[pr].rearrange("t p f -> p t f")), 16)
                    S["dmc"].inc(sync.dma_start(
                        g_sb[:].rearrange("p (t f) -> p t f", f=VD),
                        g_d[pr].rearrange("t p f -> p t f")), 16)
                    for j in range(NQB):
                        for qs in range(2):
                            u = 2 * j + qs
                            sync.wait_ge(S["t20"].sem, base["t20"] + u + 1)
                            S["dmg" if u % 2 == 0 else "dmi"].inc(sync.dma_start(
                                y_d[j * 256 + qs * 128: j * 256 + (qs + 1) * 128,
                                    pr * 256:(pr + 1) * 256], ybf[u % 2][:]), 16)
                    sync.wait_ge(S["dmg"].sem, S["dmg"].n)
                    sync.wait_ge(S["dmi"].sem, S["dmi"].n)

                @blk.tensor
                def _(t_):
                    t_.wait_ge(S["dma"].sem, base["dma"] + 64)
                    for i, (j, kt, kind) in enumerate(iters):
                        sb = p_s[i % 2]
                        if i >= 2:
                            t_.wait_ge(S["t15"].sem, base["t15"] + i - 1)
                        if kind == "half":
                            qsl = slice(j * 256 + 128, j * 256 + 256)
                            t_.matmul(sb[:, 0:128], ks_[0][:, kt * 128:(kt + 1) * 128],
                                      qs_[0][:, qsl], start=True, stop=True,
                                      skip_group_check=True)
                            ins = t_.matmul(sb[:, 128:256], ks_[1][:, kt * 128:(kt + 1) * 128],
                                            qs_[1][:, qsl], start=True, stop=True,
                                            skip_group_check=True)
                        else:
                            qsl = slice(j * 256, j * 256 + 256)
                            t_.matmul(sb[:, 0:256], ks_[0][:, kt * 128:(kt + 1) * 128],
                                      qs_[0][:, qsl], start=True, stop=True,
                                      skip_group_check=True)
                            ins = t_.matmul(sb[:, 256:512], ks_[1][:, kt * 128:(kt + 1) * 128],
                                            qs_[1][:, qsl], start=True, stop=True,
                                            skip_group_check=True)
                        S["t14"].inc(ins)
                        if i >= 1:
                            _p2_pv(nc, t_, S, base, iters[i - 1], i - 1, P2b, p_o, p_z,
                                   v_sb, ones_col)
                    _p2_pv(nc, t_, S, base, iters[n_it - 1], n_it - 1, P2b, p_o, p_z,
                           v_sb, ones_col)

                @blk.scalar
                def _(s):
                    da = 0
                    for i, (j, kt, kind) in enumerate(iters):
                        s.wait_ge(S["t14"].sem, base["t14"] + i + 1)
                        if i >= 2:
                            s.wait_ge(S["t16"].sem, base["t16"] + i - 1)
                        w = 256 if kind == "half" else 512
                        ins = s.activation(P2b[i % 2][:, 0:w], p_s[i % 2][:, 0:w],
                                           AF.Exp, scale=SCALE)
                        if kind == "full":
                            S["t15"].inc(ins)
                        else:
                            S["t17"].inc(ins)
                            da += 1
                        if kt == min(5, 2 * j + 1) and j >= 1:
                            for qs in range(2):
                                u = 2 * (j - 1) + qs
                                s.wait_ge(S["t18"].sem, base["t18"] + u + 1)
                                s.activation(esq[:], eyg[:, qs * 256:(qs + 1) * 256],
                                             AF.Square, accum_out=es[:, qs:qs + 1])
                                S["rbs"].inc(s.activation(
                                    es[:, 2 + qs:3 + qs], es[:, qs:qs + 1], AF.Ln,
                                    scale=1.0 / 256.0, bias=consts[:, 1:2]))
                                s.wait_ge(S["rbs"].sem, S["rbs"].n)
                                S["t19"].inc(s.activation(
                                    es[:, 4 + qs:5 + qs], es[:, 2 + qs:3 + qs], AF.Exp,
                                    scale=-0.5, bias=consts[:, 3:4]))
                    for qs in range(2):
                        u = 2 * (NQB - 1) + qs
                        s.wait_ge(S["t18"].sem, base["t18"] + u + 1)
                        s.activation(esq[:], eyg[:, qs * 256:(qs + 1) * 256],
                                     AF.Square, accum_out=es[:, qs:qs + 1])
                        S["rbs"].inc(s.activation(
                            es[:, 2 + qs:3 + qs], es[:, qs:qs + 1], AF.Ln,
                            scale=1.0 / 256.0, bias=consts[:, 1:2]))
                        s.wait_ge(S["rbs"].sem, S["rbs"].n)
                        S["t19"].inc(s.activation(
                            es[:, 4 + qs:5 + qs], es[:, 2 + qs:3 + qs], AF.Exp,
                            scale=-0.5, bias=consts[:, 3:4]))

                @blk.vector
                def _(v):
                    da = 0
                    for i, (j, kt, kind) in enumerate(iters):
                        if kind == "diag0":
                            da += 1
                            v.wait_ge(S["t17"].sem, base["t17"] + da)
                            S["t15"].inc(v.tensor_mul(
                                P2b[i % 2][:].rearrange("p (s x) -> p s x", s=2),
                                P2b[i % 2][:].rearrange("p (s x) -> p s x", s=2),
                                maskA[:].rearrange("p (o x) -> p o x", o=1
                                                   ).broadcast_to([128, 2, 256])))
                        elif kind == "half":
                            da += 1
                            v.wait_ge(S["t17"].sem, base["t17"] + da)
                            S["t15"].inc(v.tensor_mul(
                                P2b[i % 2][:, 0:256].rearrange("p (s x) -> p s x", s=2),
                                P2b[i % 2][:, 0:256].rearrange("p (s x) -> p s x", s=2),
                                maskA[:, 0:128].rearrange("p (o x) -> p o x", o=1
                                                          ).broadcast_to([128, 2, 128])))
                            _p2_epi(nc, v, S, base, j, pr, cum_it[j], p_o, p_z,
                                    eR, ey, ey2, eyg, es, ybf, g_sb, lam_b)

                @blk.gpsimd
                def _(g):
                    pass

    return


def _p2_pv(nc, t_, S, base, it, i, P2b, p_o, p_z, v_sb, ones_col):
    j, kt, kind = it
    og = 2 * (j % 2)
    if i == 0:
        t_.wait_ge(S["dmx"].sem, base["dmx"] + 16)
    if kt == 0 and j >= 2:
        t_.wait_ge(S["t21"].sem, base["t21"] + j - 1)
    t_.wait_ge(S["t15"].sem, base["t15"] + i + 1)
    combos = [(0, 1), (1, 1)] if kind == "half" else [(0, 0), (0, 1), (1, 0), (1, 1)]
    last = None
    for (st, qs) in combos:
        off = st * 128 if kind == "half" else st * 256 + qs * 128
        psl = P2b[i % 2][:, off:off + 128]
        t_.matmul(p_o[og + st][:, qs * 256:(qs + 1) * 256], psl,
                  v_sb[:, kt * VD:(kt + 1) * VD],
                  start=(kt == 0 and qs == 0), stop=False, skip_group_check=True)
        last = t_.matmul(p_z[j % 2][:, st * 2 + qs:st * 2 + qs + 1], psl, ones_col[:],
                         start=(kt == 0 and st == 0 and qs == 0), stop=False,
                         skip_group_check=True)
    S["t16"].inc(last)


def _p2_epi(nc, v, S, base, j, pr, cum, p_o, p_z, eR, ey, ey2, eyg, es, ybf,
            g_sb, lam_b):
    og = 2 * (j % 2)
    if j == 0:
        v.wait_ge(S["dmc"].sem, base["dmc"] + 16)
    v.wait_ge(S["t16"].sem, base["t16"] + cum)
    S["rb"].inc(v.reciprocal(eR[:, 0:4], p_z[j % 2][:]))
    v.wait_ge(S["rb"].sem, S["rb"].n)
    S["rb"].inc(v.tensor_scalar_mul(eR[:, 4:6], eR[:, 2:4], lam_b[:, pr:pr + 1]))
    v.wait_ge(S["rb"].sem, S["rb"].n)
    for qs in range(2):
        S["rb"].inc(v.tensor_scalar_mul(
            ey[:], p_o[og + 1][:, qs * 256:(qs + 1) * 256], eR[:, 4 + qs:5 + qs]))
        v.wait_ge(S["rb"].sem, S["rb"].n)
        S["rb"].inc(v.scalar_tensor_tensor(
            ey2[:], p_o[og][:, qs * 256:(qs + 1) * 256], eR[:, qs:qs + 1], ey[:],
            ALU.mult, ALU.subtract))
        v.wait_ge(S["rb"].sem, S["rb"].n)
        S["t18"].inc(v.tensor_mul(
            eyg[:, qs * 256:(qs + 1) * 256], ey2[:],
            g_sb[:, (2 * j + qs) * VD:(2 * j + qs + 1) * VD]))
        v.wait_ge(S["rb"].sem, S["rb"].n)
    for qs in range(2):
        u = 2 * j + qs
        v.wait_ge(S["t19"].sem, base["t19"] + u + 1)
        if u >= 2:
            cu = "dmg" if u % 2 == 0 else "dmi"
            v.wait_ge(S[cu].sem, base[cu] + 16 * ((u - 2) // 2 + 1))
        S["t20"].inc(v.tensor_scalar_mul(ybf[u % 2][:], eyg[:, qs * 256:(qs + 1) * 256],
                                         es[:, 4 + qs:5 + qs]))
    S["t21"].inc(v.tensor_copy(eR[:, 6:7], eR[:, 0:1]))


def _p4(nc, S, env):
    x_d, cc_d, y_out = env["x_d"], env["cc_d"], env["y_out"]
    wa_sb, bc_sb, br_b, ident = env["wa_sb"], env["bc_sb"], env["br_b"], env["ident"]
    consts, maskA = env["consts"], env["maskA"]
    ccq, xq, xq_done = env["ccq"], env["xq"], env["xq_done"]
    dbg_aps = env.get("dbg")
    NT4 = T // 512
    QT = T // 4

    with nc.Block() as blk, ExitStack() as tst:
        p_tpA = tst.enter_context(nc.psum_tensor("p4_tpA", [128, 1024], BF16))
        p_tpB = tst.enter_context(nc.psum_tensor("p4_tpB", [128, 1024], BF16))
        p_h = tst.enter_context(nc.psum_tensor("p4_h", [128, 512], F32))
        p_lg = tst.enter_context(nc.psum_tensor("p4_lg", [128, 8], F32))
        p_out = tst.enter_context(nc.psum_tensor("p4_out", [128, 2048], F32))
        ytile = [tst.enter_context(nc.sbuf_tensor(f"p4_y{i}", [128, D], BF16)) for i in range(2)]
        xtile = [tst.enter_context(nc.sbuf_tensor(f"p4_x{i}", [128, D], F32)) for i in range(2)]
        lny = tst.enter_context(nc.sbuf_tensor("p4_lny", [128, D], F32))
        x1 = [tst.enter_context(nc.sbuf_tensor(f"p4_x1{i}", [128, D], F32)) for i in range(4)]
        xf = [tst.enter_context(nc.sbuf_tensor(f"p4_xf{i}", [128, D], BF16)) for i in range(2)]
        xfT = tst.enter_context(nc.sbuf_tensor("p4_xfT", [128, D], BF16))
        hmT = tst.enter_context(nc.sbuf_tensor("p4_hmT", [128, 512], BF16))
        hm = tst.enter_context(nc.sbuf_tensor("p4_hm", [128, 512], BF16))
        junk = tst.enter_context(nc.sbuf_tensor("p4_junk", [128, D], BF16))
        st = [tst.enter_context(nc.sbuf_tensor(f"p4_st{i}", [128, 8], F32)) for i in range(2)]
        rr1 = [tst.enter_context(nc.sbuf_tensor(f"p4_rr1{i}", [128, 8], F32)) for i in range(2)]
        rr = tst.enter_context(nc.sbuf_tensor("p4_rr", [128, 32], F32))
        lnf = tst.enter_context(nc.sbuf_tensor("p4_lnf", [128, D], F32))
        sqa = tst.enter_context(nc.sbuf_tensor("p4_sqa", [128, D], F32))
        sqb = tst.enter_context(nc.sbuf_tensor("p4_sqb", [128, D], F32))
        osb = tst.enter_context(nc.sbuf_tensor("p4_osb", [128, D], F32))
        fo = [tst.enter_context(nc.sbuf_tensor(f"p4_fo{i}", [128, D], F32)) for i in range(2)]
        base = {k: S[k].n for k in S}

        def tgt_tpA(t):
            return base["t13"] + 2 * t + 1

        def tgt_tpB(t):
            return base["t13"] + 2 * t + 2

        def tgt_evB(t):
            return base["t22"] + 2 * t + 2

        NIT = NT4 + 4

        @blk.sync
        def _(sync):
            pid = nc.partition_id([ET.SP])
            qoff = nc.snap((pid % 4) * QT, min_val=0, max_val=3 * QT)
            for rk in range(4):
                S["dmc"].inc(sync.dma_start(
                    ccq[rk], cc_d[rk * T:(rk + 1) * T, :][bass.ds(qoff, QT), :]), 16)
            sync.wait_ge(S["dmc"].sem, S["dmc"].n)
            for t in range(NIT):
                if 0 <= t - 2 < NT4:
                    u = t - 2
                    sync.wait_ge(S["t19"].sem, base["t19"] + u + 1)
                    if u >= 1:
                        sync.wait_ge(S["t20"].sem, base["t20"] + u)
                    for i in range(4):
                        S["dmc"].inc(sync.dma_start_transpose(
                            hmT[:, i * 128:(i + 1) * 128],
                            hm[:, i * 128:(i + 1) * 128]), 16)
                if 0 <= t - 4 < NT4:
                    w = t - 4
                    sync.wait_ge(S["u3"].sem, base["u3"] + w + 1)
                    S["dmg" if w % 2 == 0 else "dmi"].inc(sync.dma_start(
                        y_out[w * 128:(w + 1) * 128, :], fo[w % 2][:]), 16)
            if dbg_aps:
                tl = NT4 - 1
                for nm, tens in [("d_p4x1", x1[tl % 4]), ("d_p4xf", xf[tl % 2]),
                                 ("d_p4hm", hm), ("d_p4rr", rr), ("d_p4yt", ytile[tl % 2]),
                                 ("d_p4br", br_b), ("d_p4xfT", xfT),
                                 ("d_p4wa", wa_sb[:, 0:1040])]:
                    S["dmg"].inc(sync.dma_start(dbg_aps[nm], tens[:]), 16)
            sync.wait_ge(S["dmg"].sem, S["dmg"].n)
            sync.wait_ge(S["dmi"].sem, S["dmi"].n)

        @blk.vector
        def _(v):
            for t in range(NIT):
                if 0 <= t - 1 < NT4:
                    w1_ = t - 1
                    v.wait_ge(S["t13"].sem, tgt_tpA(w1_))
                    S["t22"].inc(v.tensor_copy(xfT[:, 0:1024], p_tpA[:]))
                    v.wait_ge(S["t13"].sem, tgt_tpB(w1_))
                    S["t22"].inc(v.tensor_copy(xfT[:, 1024:2048], p_tpB[:]))
                if t < NT4:
                    pr = t % 2
                    lnm = "dma" if t % 2 == 0 else "dmh"
                    v.wait_ge(S[lnm].sem, base[lnm] + 80 * ((t // 2) + 1))
                    S["rb"].inc(v.reduce_sum(st[pr][:, 0:1], ytile[pr][:], axis=AX.X))
                    v.wait_ge(S["rb"].sem, S["rb"].n)
                    S["rb"].inc(v.tensor_scalar_mul(st[pr][:, 2:3], st[pr][:, 0:1], 1.0 / D))
                    v.wait_ge(S["rb"].sem, S["rb"].n)
                    v.wait_ge(S["t1"].sem, base["t1"] + t + 1)
                    S["rb"].inc(v.tensor_mul(st[pr][:, 3:4], st[pr][:, 2:3], st[pr][:, 2:3]))
                    v.wait_ge(S["rb"].sem, S["rb"].n)
                    S["t2"].inc(v.scalar_tensor_tensor(st[pr][:, 4:5], st[pr][:, 1:2],
                                                       1.0 / D, st[pr][:, 3:4],
                                                       ALU.mult, ALU.subtract))
                    v.wait_ge(S["t3"].sem, base["t3"] + t + 1)
                    S["t4"].inc(v.scalar_tensor_tensor(st[pr][:, 6:7], st[pr][:, 2:3],
                                                       -1.0, st[pr][:, 5:6],
                                                       ALU.mult, ALU.mult))
                    v.wait_ge(S["t5"].sem, base["t5"] + t + 1)
                    if t >= 3:
                        v.wait_ge(S["u3"].sem, base["u3"] + t - 3)
                    S["t6"].inc(v.tensor_add(x1[t % 4][:], xtile[pr][:], lny[:]))
                    S["rb"].inc(v.reduce_sum(rr1[pr][:, 0:1], x1[t % 4][:], axis=AX.X))
                    v.wait_ge(S["rb"].sem, S["rb"].n)
                    S["rb"].inc(v.tensor_scalar_mul(rr1[pr][:, 2:3], rr1[pr][:, 0:1], 1.0 / D))
                    v.wait_ge(S["rb"].sem, S["rb"].n)
                    v.wait_ge(S["t7"].sem, base["t7"] + t + 1)
                    S["rb"].inc(v.tensor_mul(rr1[pr][:, 3:4], rr1[pr][:, 2:3], rr1[pr][:, 2:3]))
                    v.wait_ge(S["rb"].sem, S["rb"].n)
                    S["t8"].inc(v.scalar_tensor_tensor(rr1[pr][:, 4:5], rr1[pr][:, 1:2],
                                                       1.0 / D, rr1[pr][:, 3:4],
                                                       ALU.mult, ALU.subtract))
                    v.wait_ge(S["t9"].sem, base["t9"] + t + 1)
                    S["t10"].inc(v.scalar_tensor_tensor(rr1[pr][:, 6:7], rr1[pr][:, 2:3],
                                                        -1.0, rr1[pr][:, 5:6],
                                                        ALU.mult, ALU.mult))
                if 0 <= t - 1 < NT4:
                    u = t - 1
                    v.wait_ge(S["t16"].sem, base["t16"] + u + 1)
                    S["rb"].inc(v.tensor_add(rr[:, 8:16], p_lg[:], br_b[:]))
                    v.wait_ge(S["rb"].sem, S["rb"].n)
                    S["rb"].inc(v.reduce_max(rr[:, 16:17], rr[:, 8:16], axis=AX.X))
                    v.wait_ge(S["rb"].sem, S["rb"].n)
                    S["t17"].inc(v.tensor_scalar_mul(rr[:, 17:18], rr[:, 16:17], -1.0))
                    v.wait_ge(S["t18"].sem, base["t18"] + u + 1)
                    S["rb"].inc(v.reciprocal(rr[:, 19:20], rr[:, 18:19]))
                    v.wait_ge(S["rb"].sem, S["rb"].n)
                    S["rb"].inc(v.tensor_single_scalar(rr[:, 24:32], rr[:, 8:16],
                                                       rr[:, 16:17], ALU.is_ge))
                    v.wait_ge(S["rb"].sem, S["rb"].n)
                    S["rb"].inc(v.tensor_scalar(rr[:, 24:32], rr[:, 24:32], rr[:, 19:20],
                                                1.0 / 64.0, ALU.mult, ALU.mult))
                    v.wait_ge(S["rb"].sem, S["rb"].n)
                    if u >= 1:
                        v.wait_ge(S["dmc"].sem, base["dmc"] + 64 + 64 * u)
                    S["t19"].inc(v.tensor_mul(
                        hm[:].rearrange("p (e r) -> p e r", e=8),
                        p_h[:].rearrange("p (e r) -> p e r", e=8),
                        rr[:, 24:32].rearrange("p (e o) -> p e o", o=1
                                               ).broadcast_to([128, 8, 64])))
                if 0 <= t - 3 < NT4:
                    w = t - 3
                    v.wait_ge(S["u5"].sem, base["u5"] + w + 1)
                    v.wait_ge(S["t21"].sem, base["t21"] + w + 1)
                    S["rb"].inc(v.tensor_scalar_mul(rr[:, 3:4], rr[:, 0:1], 1.0 / D))
                    v.wait_ge(S["rb"].sem, S["rb"].n)
                    S["rb"].inc(v.tensor_mul(rr[:, 20:21], rr[:, 3:4], rr[:, 3:4]))
                    v.wait_ge(S["rb"].sem, S["rb"].n)
                    S["u1"].inc(v.scalar_tensor_tensor(rr[:, 4:5], rr[:, 1:2], 1.0 / D,
                                                       rr[:, 20:21], ALU.mult, ALU.subtract))
                    v.wait_ge(S["u2"].sem, base["u2"] + w + 1)
                    S["rb"].inc(v.scalar_tensor_tensor(rr[:, 6:7], rr[:, 3:4], -1.0,
                                                       rr[:, 5:6], ALU.mult, ALU.mult))
                    v.wait_ge(S["rb"].sem, S["rb"].n)
                    if w >= 1:
                        v.wait_ge(S["u3"].sem, base["u3"] + w)
                    S["t15"].inc(v.tensor_scalar(lnf[:], osb[:], rr[:, 5:6], rr[:, 6:7],
                                                 ALU.mult, ALU.add))

        @blk.scalar
        def _(s):
            def loads(s_):
                lnm = "dma" if s_ % 2 == 0 else "dmh"
                for rk in range(4):
                    S[lnm].inc(s.dma_start(
                        ytile[s_ % 2][:, rk * 512:(rk + 1) * 512],
                        ccq[rk, s_ * 128:(s_ + 1) * 128, :]), 16)
                S[lnm].inc(s.dma_start(
                    xtile[s_ % 2][:], xq[s_ * 128:(s_ + 1) * 128, :]), 16)

            for t in range(NIT):
                if t == 0:
                    s.wait_ge(S["dmc"].sem, base["dmc"] + 64)
                    s.wait_ge(S["dmq"].sem, xq_done)
                    for s_ in (0, 1):
                        if s_ < NT4:
                            loads(s_)
                if 0 <= t - 3 < NT4:
                    w = t - 3
                    if w >= 1:
                        s.wait_ge(S["t15"].sem, base["t15"] + w)
                    S["u5"].inc(s.activation(osb[:], p_out[:], AF.Abs,
                                             accum_out=rr[:, 0:1]))
                if t < NT4:
                    pr = t % 2
                    lnm = "dma" if t % 2 == 0 else "dmh"
                    s.wait_ge(S[lnm].sem, base[lnm] + 80 * ((t // 2) + 1))
                    S["t1"].inc(s.activation(junk[:], ytile[pr][:], AF.Square,
                                             accum_out=st[pr][:, 1:2]))
                    s.wait_ge(S["t2"].sem, base["t2"] + t + 1)
                    S["rbs"].inc(s.activation(st[pr][:, 7:8], st[pr][:, 4:5], AF.Ln,
                                              bias=consts[:, 0:1]))
                    s.wait_ge(S["rbs"].sem, S["rbs"].n)
                    S["t3"].inc(s.activation(st[pr][:, 5:6], st[pr][:, 7:8], AF.Exp,
                                             scale=-0.5))
                    s.wait_ge(S["t4"].sem, base["t4"] + t + 1)
                    if t >= 1:
                        s.wait_ge(S["t6"].sem, base["t6"] + t)
                    S["t5"].inc(s.activation(lny[:], ytile[pr][:], AF.Identity,
                                             bias=st[pr][:, 6:7], scale=st[pr][:, 5:6]))
                    s.wait_ge(S["t6"].sem, base["t6"] + t + 1)
                    S["t7"].inc(s.activation(junk[:], x1[t % 4][:], AF.Square,
                                             accum_out=rr1[pr][:, 1:2]))
                    if t + 2 < NT4:
                        loads(t + 2)
                    s.wait_ge(S["t8"].sem, base["t8"] + t + 1)
                    S["rbs"].inc(s.activation(rr1[pr][:, 7:8], rr1[pr][:, 4:5], AF.Ln,
                                              bias=consts[:, 0:1]))
                    s.wait_ge(S["rbs"].sem, S["rbs"].n)
                    S["t9"].inc(s.activation(rr1[pr][:, 5:6], rr1[pr][:, 7:8], AF.Exp,
                                             scale=-0.5))
                    s.wait_ge(S["t10"].sem, base["t10"] + t + 1)
                    if t >= 2:
                        s.wait_ge(S["t13"].sem, tgt_tpB(t - 2))
                    S["t11"].inc(s.activation(xf[pr][:], x1[t % 4][:], AF.Identity,
                                              bias=rr1[pr][:, 6:7], scale=rr1[pr][:, 5:6]))
                if 0 <= t - 1 < NT4:
                    u = t - 1
                    s.wait_ge(S["t17"].sem, base["t17"] + u + 1)
                    S["t18"].inc(s.activation(junk[:, 0:8], rr[:, 8:16], AF.Exp,
                                              bias=rr[:, 17:18], accum_out=rr[:, 18:19]))
                if 0 <= t - 3 < NT4:
                    w = t - 3
                    s.wait_ge(S["u1"].sem, base["u1"] + w + 1)
                    S["rbs"].inc(s.activation(rr[:, 7:8], rr[:, 4:5], AF.Ln,
                                              bias=consts[:, 0:1]))
                    s.wait_ge(S["rbs"].sem, S["rbs"].n)
                    S["u2"].inc(s.activation(rr[:, 5:6], rr[:, 7:8], AF.Exp, scale=-0.5))
                if 0 <= t - 2 < NT4:
                    q = t - 2
                    s.wait_ge(S["t20"].sem, base["t20"] + q + 1)
                    if q >= 1:
                        s.wait_ge(S["u1"].sem, base["u1"] + q)
                    S["t21"].inc(s.activation(junk[:], p_out[:], AF.Square,
                                              accum_out=rr[:, 1:2]))

        @blk.gpsimd
        def _(g):
            for t in range(3, NIT):
                w = t - 3
                if w >= NT4:
                    continue
                g.wait_ge(S["t15"].sem, base["t15"] + w + 1)
                g.wait_ge(S["t6"].sem, base["t6"] + w + 1)
                if w >= 2:
                    ct = "dmg" if (w - 2) % 2 == 0 else "dmi"
                    g.wait_ge(S[ct].sem, base[ct] + 16 * ((w - 2) // 2 + 1))
                S["u3"].inc(g.tensor_add(fo[w % 2][:], lnf[:], x1[w % 4][:]))

        @blk.tensor
        def _(t_):
            for t in range(NIT):
                if 0 <= t - 1 < NT4:
                    w1_ = t - 1
                    t_.wait_ge(S["t11"].sem, base["t11"] + w1_ + 1)
                    if w1_ >= 1:
                        t_.wait_ge(S["t22"].sem, tgt_evB(w1_ - 1))
                    for i in range(8):
                        ins = t_.transpose(p_tpA[:, i * 128:(i + 1) * 128],
                                           xf[w1_ % 2][:, i * 128:(i + 1) * 128], ident[:])
                        if i == 7:
                            S["t13"].inc(ins)
                    for i in range(8):
                        ins = t_.transpose(p_tpB[:, i * 128:(i + 1) * 128],
                                           xf[w1_ % 2][:, (8 + i) * 128:(9 + i) * 128], ident[:])
                        if i == 7:
                            S["t13"].inc(ins)
                    t_.wait_ge(S["t22"].sem, tgt_evB(w1_))
                    t_.wait_ge(S["t19"].sem, base["t19"] + w1_)
                    for c in range(16):
                        t_.matmul(p_h[:], xfT[:, c * 128:(c + 1) * 128],
                                  wa_sb[:, c * 520:c * 520 + 512],
                                  start=(c == 0), stop=(c == 15), skip_group_check=True)
                    for c in range(16):
                        ins = t_.matmul(p_lg[:], xfT[:, c * 128:(c + 1) * 128],
                                        wa_sb[:, c * 520 + 512:(c + 1) * 520],
                                        start=(c == 0), stop=(c == 15), skip_group_check=True)
                        if c == 15:
                            S["t16"].inc(ins)
                if 0 <= t - 2 < NT4:
                    u = t - 2
                    t_.wait_ge(S["dmc"].sem, base["dmc"] + 64 + 64 * (u + 1))
                    if u >= 1:
                        t_.wait_ge(S["t21"].sem, base["t21"] + u)
                    if u >= 1:
                        t_.wait_ge(S["u5"].sem, base["u5"] + u)
                    for rc in range(4):
                        hc = hmT[:, rc * 128:(rc + 1) * 128]
                        for sx in range(4):
                            t_.matmul(
                                p_out[:, sx * 512:(sx + 1) * 512], hc,
                                bc_sb[:, rc * 2048 + sx * 512:rc * 2048 + (sx + 1) * 512],
                                start=(rc == 0), stop=False, skip_group_check=True)
                    for sx in range(4):
                        for hh in range(2):
                            ins = t_.matmul(
                                p_out[:, sx * 512 + hh * 256:sx * 512 + (hh + 1) * 256],
                                maskA[0:1, 0:128], maskA[0:1, 0:256],
                                start=False, stop=True, skip_group_check=True)
                            if sx == 3 and hh == 1:
                                S["t20"].inc(ins)



def make_in_maps(L0_attn_hidden_states, L0_attn_W_qkv, L0_attn_lambda_q1,
                 L0_attn_lambda_k1, L0_attn_lambda_q2, L0_attn_lambda_k2,
                 L0_attn_W_g, L0_ffn_A, L0_ffn_B, L0_ffn_Wr, L0_ffn_br):
    x = np.asarray(L0_attn_hidden_states)
    wqkv = np.asarray(L0_attn_W_qkv)
    wg = np.asarray(L0_attn_W_g)
    A = np.asarray(L0_ffn_A)
    B = np.asarray(L0_ffn_B)
    Wr = np.asarray(L0_ffn_Wr)
    br = np.asarray(L0_ffn_br)
    lq1, lk1 = np.asarray(L0_attn_lambda_q1), np.asarray(L0_attn_lambda_k1)
    lq2, lk2 = np.asarray(L0_attn_lambda_q2), np.asarray(L0_attn_lambda_k2)

    wa = np.concatenate([A.transpose(2, 0, 1).reshape(D, 512),
                         Wr.T], axis=1).astype(np.float32)
    bcat = np.ascontiguousarray(B.transpose(0, 2, 1).reshape(512, D), dtype=np.float32)

    in_maps = []
    for c in range(8):
        b, pg = c // 4, c % 4
        blocks = []
        for i in range(4):
            h = 4 * pg + i
            blocks.append(wqkv[:, h * 384:h * 384 + 128])
            blocks.append(wqkv[:, h * 384 + 128:h * 384 + 256])
        for i in range(4):
            h = 4 * pg + i
            blocks.append(wqkv[:, h * 384 + 256:h * 384 + 384])
        blocks.append(wg[:, pg * 512:(pg + 1) * 512])
        wproj = np.ascontiguousarray(np.concatenate(blocks, axis=1), dtype=np.float32)
        prs = [2 * pg, 2 * pg + 1]
        lam4 = np.ascontiguousarray(
            np.stack([lq1[prs], lk1[prs], lq2[prs], lk2[prs]]), dtype=np.float32)
        in_maps.append({
            "x": np.ascontiguousarray(x[b], dtype=np.float32),
            "wproj": wproj,
            "wa": np.ascontiguousarray(wa),
            "bcat": bcat,
            "br": np.ascontiguousarray(br.reshape(1, 8), dtype=np.float32),
            "lam4": lam4,
        })
    return in_maps


def kernel(L0_attn_hidden_states, L0_attn_W_qkv, L0_attn_lambda_q1,
           L0_attn_lambda_k1, L0_attn_lambda_q2, L0_attn_lambda_k2,
           L0_attn_W_g, L0_ffn_A, L0_ffn_B, L0_ffn_Wr, L0_ffn_br):
    nc, _ = build_program()
    in_maps = make_in_maps(L0_attn_hidden_states, L0_attn_W_qkv, L0_attn_lambda_q1,
                           L0_attn_lambda_k1, L0_attn_lambda_q2, L0_attn_lambda_k2,
                           L0_attn_W_g, L0_ffn_A, L0_ffn_B, L0_ffn_Wr, L0_ffn_br)
    res = run_bass_kernel_spmd(nc, in_maps, core_ids=list(range(8)))
    out = np.empty((2, T, D), np.float32)
    for c in range(8):
        b, q = c // 4, c % 4
        out[b, q * 2048:(q + 1) * 2048] = res.results[c]["y_out"]
    return out


# version 18
# speedup vs baseline: 1.1267x; 1.1267x over previous
import math
import numpy as np
from contextlib import ExitStack

import concourse.bass as bass
import concourse.mybir as mybir
from concourse.bass_utils import run_bass_kernel_spmd

F32 = mybir.dt.float32
BF16 = mybir.dt.bfloat16
AF = mybir.ActivationFunctionType
ALU = mybir.AluOpType
AX = mybir.AxisListType
ET = mybir.EngineType

T = 8192
D = 2048
NT = T // 128
VD = 256
EPS_RMS = 1e-6
EPS_LN = 1e-5
LAMBDA_INIT = 0.8 - 0.6 * math.exp(-0.3 * 6)
OUT_GAIN = 1.0 - LAMBDA_INIT
SCALE = 128.0 ** -0.5
LN104_64 = math.log(10000.0) / 64.0
QBLK = 256
NQB = T // QBLK


def set_dims(t):
    global T, NT, NQB
    T = t
    NT = t // 128
    NQB = t // QBLK


class Ctr:
    def __init__(self, sem):
        self.sem = sem
        self.n = 0

    def inc(self, instr, k=1):
        instr.then_inc(self.sem, k)
        self.n += k
        return instr


class SerialEng:

    def __init__(self, eng, ctr):
        self._e = eng
        self._c = ctr

    def __getattr__(self, name):
        f = getattr(self._e, name)
        if name in ("wait_ge", "nop", "drain", "sem", "then_inc"):
            return f

        def wrapped(*a, **k):
            ins = f(*a, **k)
            self._c.inc(ins)
            self._e.wait_ge(self._c.sem, self._c.n)
            return ins
        return wrapped


def build_program(debug=False, phases=4, skip=(), mock_cc=False):
    nc = bass.Bass("TRN2", target_bir_lowering=False)

    x_d = nc.dram_tensor("x", [T, D], F32, kind="ExternalInput").ap()
    wproj_d = nc.dram_tensor("wproj", [D, 2048], F32, kind="ExternalInput").ap()
    wa_d = nc.dram_tensor("wa", [D, 520], F32, kind="ExternalInput").ap()
    bcat_d = nc.dram_tensor("bcat", [512, D], F32, kind="ExternalInput").ap()
    br_d = nc.dram_tensor("br", [1, 8], F32, kind="ExternalInput").ap()
    lam_d = nc.dram_tensor("lam4", [4, 2, 64], F32, kind="ExternalInput").ap()
    y_out = nc.dram_tensor("y_out", [T // 4, D], F32, kind="ExternalOutput").ap()

    qk_t = nc.dram_tensor("qk_sc", [2, 4, 128, T], BF16)
    v_t = nc.dram_tensor("v_sc", [2, NT, 128, VD], BF16)
    g_t = nc.dram_tensor("g_sc", [2, NT, 128, VD], BF16)
    y_t = nc.dram_tensor("y_sc", [T, 512], BF16)
    cc_t = nc.dram_tensor("cc_sc", [4 * T, 512], BF16)
    qk_d, v_d, g_d, y_d = qk_t.ap(), v_t.ap(), g_t.ap(), y_t.ap()

    dbg = {}
    if debug:
        for nm, shp, dt in [("d_xn", [128, D], BF16), ("d_xnT", [128, D], BF16),
                            ("d_qn", [128, 1024], BF16), ("d_qr", [128, 1024], BF16),
                            ("d_st", [128, 8], F32), ("d_s8", [128, 24], F32),
                            ("d_vgs", [128, 1024], BF16)]:
            dbg[nm] = nc.dram_tensor(nm, shp, dt, kind="ExternalOutput").ap()
        dbg["d_qk"] = nc.dram_tensor("d_qk", [2, 4, 128, T], BF16, kind="ExternalOutput").ap()
        dbg["d_v"] = nc.dram_tensor("d_v", [2, NT, 128, VD], BF16, kind="ExternalOutput").ap()
        dbg["d_g"] = nc.dram_tensor("d_g", [2, NT, 128, VD], BF16, kind="ExternalOutput").ap()
        dbg["d_y"] = nc.dram_tensor("d_y", [T, 512], BF16, kind="ExternalOutput").ap()
        dbg["d_cc"] = nc.dram_tensor("d_cc", [4 * T, 512], BF16, kind="ExternalOutput").ap()
        dbg["d_p4x1"] = nc.dram_tensor("d_p4x1", [128, D], F32, kind="ExternalOutput").ap()
        dbg["d_p4xf"] = nc.dram_tensor("d_p4xf", [128, D], BF16, kind="ExternalOutput").ap()
        dbg["d_p4hm"] = nc.dram_tensor("d_p4hm", [128, 512], BF16, kind="ExternalOutput").ap()
        dbg["d_p4rr"] = nc.dram_tensor("d_p4rr", [128, 32], F32, kind="ExternalOutput").ap()
        dbg["d_p4br"] = nc.dram_tensor("d_p4br", [128, 8], F32, kind="ExternalOutput").ap()
        dbg["d_p4xfT"] = nc.dram_tensor("d_p4xfT", [128, D], BF16, kind="ExternalOutput").ap()
        dbg["d_p4wa"] = nc.dram_tensor("d_p4wa", [128, 1040], BF16, kind="ExternalOutput").ap()
        dbg["d_p4yt"] = nc.dram_tensor("d_p4yt", [128, D], BF16, kind="ExternalOutput").ap()

    est = ExitStack()
    with est:
        ident = est.enter_context(nc.sbuf_tensor("ident", [128, 128], BF16))
        consts = est.enter_context(nc.sbuf_tensor("consts", [128, 8], F32))
        maskA = est.enter_context(nc.sbuf_tensor("maskA", [128, 256], BF16))
        ones_col = est.enter_context(nc.sbuf_tensor("ones_col", [128, 1], BF16))
        lam_b = est.enter_context(nc.sbuf_tensor("lam_b", [128, 2], F32))
        br_b = est.enter_context(nc.sbuf_tensor("br_b", [128, 8], F32))
        wa_sb = est.enter_context(nc.sbuf_tensor("wa_sb", [128, 16 * 520], BF16))
        bc_sb = est.enter_context(nc.sbuf_tensor("bc_sb", [128, 4 * 2048], BF16))

        S = {}
        for nm in ["dma", "dmg", "dmh", "dmi", "dmx", "dmc", "dmq", "u4", "u5", "rb", "rbp", "rbs", "u1", "u2", "u3", "t1", "t2", "t3", "t4", "t5", "t6", "t7", "t8",
                   "t9", "t10", "t11", "t12", "t13", "t14", "t15", "t16", "t17",
                   "t18", "t19", "t20", "t21", "t22"]:
            S[nm] = Ctr(est.enter_context(nc.semaphore(nm)))

        if "p0a" not in skip:
            _p0a(nc, S, ident, maskA, ones_col, consts)
        cs_stack = ExitStack()
        cos_tm = cs_stack.enter_context(nc.sbuf_tensor("cos_tm", [128, NT * 64], BF16))
        sin_tm = cs_stack.enter_context(nc.sbuf_tensor("sin_tm", [128, NT * 64], BF16))
        if "p0b" not in skip:
            _p0b(nc, S, ident, cos_tm, sin_tm, consts, skip)
        wproj = cs_stack.enter_context(nc.sbuf_tensor("wproj_sb", [128, 16 * 2048], BF16))
        if "p0c" not in skip:
            _p0c(nc, S, dict(wproj_d=wproj_d, wa_d=wa_d, bcat_d=bcat_d, br_d=br_d,
                         lam_d=lam_d, wproj=wproj, wa_sb=wa_sb, bc_sb=bc_sb,
                         lam_b=lam_b, br_b=br_b))
        if phases >= 1:
            _p1(nc, S, dict(x_d=x_d, qk_d=qk_d, v_d=v_d, g_d=g_d, wproj=wproj,
                            cos_tm=cos_tm, sin_tm=sin_tm, ident=ident, consts=consts,
                            dbg=dbg if debug else None))
        cs_stack.close()
        xq_done = None
        if phases >= 4:
            ccq_t = nc.dram_tensor("ccq_sc", [4, T // 4, 512], BF16)
            xq_t = nc.dram_tensor("xq_sc", [T // 4, D], F32)
            with nc.Block() as blk:
                @blk.sync
                def _(sync0):
                    pid0 = nc.partition_id([ET.SP])
                    qoff0 = nc.snap((pid0 % 4) * (T // 4), min_val=0,
                                    max_val=3 * (T // 4))
                    S["dmq"].inc(sync0.dma_start(
                        xq_t.ap(), x_d[bass.ds(qoff0, T // 4), :]), 16)
            xq_done = S["dmq"].n
        if phases >= 2:
            _p2(nc, S, dict(qk_d=qk_d, v_d=v_d, g_d=g_d, y_d=y_d, maskA=maskA,
                            ones_col=ones_col, lam_b=lam_b, consts=consts))
        if phases >= 3:
            with nc.Block() as blk:
                @blk.gpsimd
                def _(g):
                    if mock_cc:
                        for rk in range(4):
                            S["t12"].inc(g.dma_start(
                                cc_t.ap()[rk * T:(rk + 1) * T, :], y_d), 16)
                        g.wait_ge(S["t12"].sem, S["t12"].n)
                    else:
                        S["t12"].inc(g.collective_compute(
                            "AllGather", ALU.bypass,
                            replica_groups=[[0, 1, 2, 3], [4, 5, 6, 7]],
                            ins=[y_t.ap().opt()], outs=[cc_t.ap().opt()]))
                        g.wait_ge(S["t12"].sem, S["t12"].n)
        if phases >= 4:
            _p4(nc, S, dict(x_d=x_d, cc_d=cc_t.ap(), y_out=y_out, wa_sb=wa_sb,
                            bc_sb=bc_sb, br_b=br_b, ident=ident, consts=consts,
                            maskA=maskA, ccq=ccq_t.ap(), xq=xq_t.ap(),
                            xq_done=xq_done, dbg=dbg if debug else None))

        if debug:
            with nc.Block() as blk:
                @blk.sync
                def _(sync):
                    S["dma"].inc(sync.dma_start(dbg["d_qk"], qk_d), 16)

                    S["dma"].inc(sync.dma_start(dbg["d_v"], v_d), 16)
                    S["dma"].inc(sync.dma_start(dbg["d_g"], g_d), 16)
                    if phases >= 2:
                        S["dma"].inc(sync.dma_start(dbg["d_y"], y_d), 16)
                    if phases >= 3:
                        S["dma"].inc(sync.dma_start(dbg["d_cc"], cc_t.ap()), 16)
                    sync.wait_ge(S["dma"].sem, S["dma"].n)

        if phases < 4:
            with nc.Block() as blk, ExitStack() as tst:
                zz = tst.enter_context(nc.sbuf_tensor("zz", [128, D], F32))

                @blk.gpsimd
                def _(g):
                    S["t13"].inc(g.memset(zz[:], 0.0))

                @blk.sync
                def _(sync):
                    sync.wait_ge(S["t13"].sem, S["t13"].n)
                    for i in range(T // 512):
                        S["dma"].inc(sync.dma_start(y_out[i * 128:(i + 1) * 128, :], zz[:]), 16)
                    sync.wait_ge(S["dma"].sem, S["dma"].n)

    return nc, dbg


def _p0a(nc, S, ident, maskA, ones_col, consts):
    with nc.Block() as blk:
        @blk.gpsimd
        def _(g):
            g = SerialEng(g, S["rbp"])
            g.memset(ident[:], 0.0)
            g.affine_select(out=ident[:], in_=ident[:], compare_op=ALU.not_equal,
                            fill=1.0, base=0, pattern=[[-1, 128]], channel_multiplier=1)
            g.memset(maskA[:], 1.0)
            g.affine_select(out=maskA[:], in_=maskA[:], compare_op=ALU.is_ge,
                            fill=0.0, base=0, pattern=[[1, 256]], channel_multiplier=-1)
            g.memset(ones_col[:], 1.0)
            g.memset(consts[:, 0:1], EPS_LN)
            g.memset(consts[:, 1:2], EPS_RMS)
            g.memset(consts[:, 2:3], math.pi / 2)
            g.memset(consts[:, 3:4], math.log(OUT_GAIN))
            g.memset(consts[:, 4:5], 1.0)


def _p0b(nc, S, ident, cos_tm, sin_tm, consts, skip=()):
    with nc.Block() as blk, ExitStack() as tst:
        p_tp2 = [tst.enter_context(nc.psum_tensor(f"p_tp0{i}", [128, 1024], BF16))
                 for i in range(2)]
        iot64 = tst.enter_context(nc.sbuf_tensor("iot64", [64, 1], F32))
        invf = tst.enter_context(nc.sbuf_tensor("invf", [64, 1], F32))
        cn = tst.enter_context(nc.sbuf_tensor("cn", [64, 1], F32))
        sn = tst.enter_context(nc.sbuf_tensor("sn", [64, 1], F32))
        nsn = tst.enter_context(nc.sbuf_tensor("nsn", [64, 1], F32))
        tsc = tst.enter_context(nc.sbuf_tensor("tsc", [64, 2], F32))
        kvec = tst.enter_context(nc.sbuf_tensor("kvec", [64, 1], F32))
        stp = tst.enter_context(nc.sbuf_tensor("stp", [64, 1], F32))
        onec = tst.enter_context(nc.sbuf_tensor("onec", [64, 1], F32))
        car = tst.enter_context(nc.sbuf_tensor("car", [64, 64], F32))
        sar = tst.enter_context(nc.sbuf_tensor("sar", [64, 64], F32))
        caa = tst.enter_context(nc.sbuf_tensor("caa", [64, T // 64], F32))
        saa = tst.enter_context(nc.sbuf_tensor("saa", [64, T // 64], F32))
        tmp8 = tst.enter_context(nc.sbuf_tensor("tmp8", [64, T], F32))
        cbf = tst.enter_context(nc.sbuf_tensor("cbf", [128, T], BF16))
        sbf = tst.enter_context(nc.sbuf_tensor("sbf", [128, T], BF16))
        cf32 = tst.enter_context(nc.sbuf_tensor("cf32", [64, T], F32))
        sf32 = tst.enter_context(nc.sbuf_tensor("sf32", [64, T], F32))

        @blk.gpsimd
        def _(g):
            g = SerialEng(g, S["rbp"])
            g.iota(iot64[:], [[0, 1]], channel_multiplier=1,
                   allow_small_or_imprecise_dtypes=True)
            g.memset(cbf[:], 0.0)
            g.memset(sbf[:], 0.0)
            g.memset(kvec[:], 0.0)
            g.memset(onec[:], 1.0)
            kjs = [int(64.0 * 10000.0 ** (-j / 64.0) / (2 * math.pi)) for j in range(64)]
            m = 1
            while any(k >= m for k in kjs):
                jmax = max(j for j in range(64) if kjs[j] >= m)
                g.affine_select(out=stp[:], in_=onec[:], compare_op=ALU.is_ge,
                                fill=0.0, base=jmax, pattern=[[0, 1]],
                                channel_multiplier=-1)
                g.tensor_add(kvec[:], kvec[:], stp[:])
                m += 1
            gi = g.memset(nsn[:], 0.0)
            S["t1"].inc(g._e.memset(tsc[:, 0:1], 0.0))

        @blk.scalar
        def _(s):
            s.wait_ge(S["t1"].sem, S["t1"].n)
            S["t2"].inc(s.activation(invf[:], iot64[:], AF.Exp, scale=-LN104_64))

        def taylor_sincos(v, phi, out_c, out_s, u, hi):
            v.tensor_mul(u, phi, phi)
            v.memset(out_s, -1.0 / 121645100408832000.0)
            for ck in [1.0 / 355687428096000.0, -1.0 / 1307674368000.0,
                       1.0 / 6227020800.0, -1.0 / 39916800.0, 1.0 / 362880.0,
                       -1.0 / 5040.0, 1.0 / 120.0, -1.0 / 6.0, 1.0]:
                v.tensor_mul(out_s, out_s, u)
                v.tensor_scalar_add(out_s, out_s, ck)
            v.tensor_mul(out_s, out_s, phi)
            v.memset(out_c, 1.0 / 6402373705728000.0)
            for ck in [-1.0 / 87178291200.0, 1.0 / 479001600.0, -1.0 / 3628800.0,
                       1.0 / 40320.0, -1.0 / 720.0, 1.0 / 24.0, -1.0 / 2.0, 1.0]:
                v.tensor_mul(out_c, out_c, u)
                v.tensor_scalar_add(out_c, out_c, ck)
            _ = hi

        @blk.vector
        def _(v):
            vraw = v
            v = SerialEng(v, S["rb"])
            vraw.wait_ge(S["t2"].sem, S["t2"].n)
            NA = T // 64
            u = tsc[:, 0:1]
            taylor_sincos(v, invf[:], cn[:], sn[:], u, None)
            v.tensor_scalar_mul(nsn[:], sn[:], -1.0)
            v.memset(car[:, 0:1], 1.0)
            v.memset(sar[:, 0:1], 0.0)
            for r in range(63):
                v.tensor_scalar_mul(car[:, r + 1:r + 2], car[:, r:r + 1], cn[:])
                v.scalar_tensor_tensor(car[:, r + 1:r + 2], sar[:, r:r + 1], nsn[:],
                                       car[:, r + 1:r + 2], ALU.mult, ALU.add)
                v.tensor_scalar_mul(sar[:, r + 1:r + 2], sar[:, r:r + 1], cn[:])
                v.scalar_tensor_tensor(sar[:, r + 1:r + 2], car[:, r:r + 1], sn[:],
                                       sar[:, r + 1:r + 2], ALU.mult, ALU.add)
            v.tensor_scalar(tsc[:, 1:2], kvec[:], -2.0 * math.pi, -math.pi,
                            ALU.mult, ALU.add)
            v.scalar_tensor_tensor(tsc[:, 1:2], invf[:], 64.0, tsc[:, 1:2],
                                   ALU.mult, ALU.add)
            taylor_sincos(v, tsc[:, 1:2], cn[:], sn[:], u, None)
            v.tensor_scalar_mul(cn[:], cn[:], -1.0)
            v.tensor_scalar_mul(nsn[:], sn[:], 1.0)
            v.tensor_scalar_mul(sn[:], sn[:], -1.0)
            v.memset(caa[:, 0:1], 1.0)
            v.memset(saa[:, 0:1], 0.0)
            for a in range(NA - 1):
                v.tensor_scalar_mul(caa[:, a + 1:a + 2], caa[:, a:a + 1], cn[:])
                v.scalar_tensor_tensor(caa[:, a + 1:a + 2], saa[:, a:a + 1], nsn[:],
                                       caa[:, a + 1:a + 2], ALU.mult, ALU.add)
                v.tensor_scalar_mul(saa[:, a + 1:a + 2], saa[:, a:a + 1], cn[:])
                v.scalar_tensor_tensor(saa[:, a + 1:a + 2], caa[:, a:a + 1], sn[:],
                                       saa[:, a + 1:a + 2], ALU.mult, ALU.add)
            cavw = caa[:].rearrange("p (a o) -> p a o", o=1).broadcast_to([64, NA, 64])
            savw = saa[:].rearrange("p (a o) -> p a o", o=1).broadcast_to([64, NA, 64])
            crvw = car[:].rearrange("p (o r) -> p o r", o=1).broadcast_to([64, NA, 64])
            srvw = sar[:].rearrange("p (o r) -> p o r", o=1).broadcast_to([64, NA, 64])
            cfv = cf32[:].rearrange("p (a r) -> p a r", r=64)
            sfv = sf32[:].rearrange("p (a r) -> p a r", r=64)
            v.tensor_mul(cfv, cavw, crvw)
            v.tensor_mul(tmp8[:].rearrange("p (a r) -> p a r", r=64), savw, srvw)
            v.tensor_sub(cf32[:], cf32[:], tmp8[:])
            v.tensor_mul(sfv, savw, crvw)
            v.tensor_mul(tmp8[:].rearrange("p (a r) -> p a r", r=64), cavw, srvw)
            v.tensor_add(sf32[:], sf32[:], tmp8[:])
            v.tensor_copy(cbf[0:64, :], cf32[:])
            v.tensor_copy(sbf[0:64, :], sf32[:])
            S["t3"].inc(vraw.tensor_copy(tsc[:, 0:1], tsc[:, 0:1]))

        if "p0b_tp" not in skip:
            @blk.tensor
            def _(t):
                t.wait_ge(S["t3"].sem, S["t3"].n)
                for tt in range(NT):
                    pt = p_tp2[tt % 2]
                    if tt >= 2:
                        t.wait_ge(S["t4"].sem, tt - 1)
                    t.transpose(pt[:, 0:128], cbf[:, tt * 128:(tt + 1) * 128], ident[:])
                    S["t5"].inc(t.transpose(pt[:, 128:256],
                                            sbf[:, tt * 128:(tt + 1) * 128], ident[:]))

            @blk.vector
            def _(v):
                for tt in range(NT):
                    pt = p_tp2[tt % 2]
                    v.wait_ge(S["t5"].sem, tt + 1)
                    v.tensor_copy(cos_tm[:, tt * 64:(tt + 1) * 64], pt[:, 0:64])
                    S["t4"].inc(v.tensor_copy(sin_tm[:, tt * 64:(tt + 1) * 64],
                                              pt[:, 128:192]))


def _p0c(nc, S, env):
    wproj_d, wa_d, bcat_d = env["wproj_d"], env["wa_d"], env["bcat_d"]
    br_d, lam_d = env["br_d"], env["lam_d"]
    wproj, wa_sb, bc_sb = env["wproj"], env["wa_sb"], env["bc_sb"]
    lam_b, br_b = env["lam_b"], env["br_b"]
    d0, w6 = S["dma"].n, S["t6"].n

    with nc.Block() as blk, ExitStack() as tst:
        p_bc = tst.enter_context(nc.psum_tensor("p_bc", [128, 512], F32))
        wstg = [tst.enter_context(nc.sbuf_tensor(f"wstg{i}", [128, 16 * 520], F32))
                for i in range(2)]
        lamq = tst.enter_context(nc.sbuf_tensor("lamq", [2, 4 * 64], F32))
        lamc = tst.enter_context(nc.sbuf_tensor("lamc", [2, 8], F32))
        brs = tst.enter_context(nc.sbuf_tensor("brs", [1, 8], F32))
        onerow = tst.enter_context(nc.sbuf_tensor("onerow", [2, 128], F32))

        d1 = S["dmh"].n

        @blk.sync
        def _(sync):
            for gg in range(4):
                pn = "dma" if gg % 2 == 0 else "dmh"
                pb = d0 if gg % 2 == 0 else d1
                if gg >= 2:
                    sync.wait_ge(S["t6"].sem, w6 + gg - 1)
                    sync.wait_ge(S[pn].sem, pb + 16 * (gg // 2))
                S[pn].inc(sync.dma_start(
                    wstg[gg % 2][:, 0:8192].rearrange("p (c f) -> p c f", c=4),
                    wproj_d.rearrange("(c p) f -> p c f", p=128)[:, gg * 4:(gg + 1) * 4, :]), 16)
            sync.wait_ge(S["t6"].sem, w6 + 4)
            sync.wait_ge(S["dma"].sem, d0 + 32)
            S["dma"].inc(sync.dma_start(
                wstg[0][:, 0:16 * 520].rearrange("p (c f) -> p c f", c=16),
                wa_d.rearrange("(c p) f -> p c f", p=128)), 16)
            sync.wait_ge(S["dmh"].sem, d1 + 32)
            S["dmh"].inc(sync.dma_start(
                wstg[1][:, 0:8192].rearrange("p (c f) -> p c f", c=4),
                bcat_d.rearrange("(c p) f -> p c f", p=128)), 16)
            sync.wait_ge(S["dma"].sem, d0 + 48)
            S["dma"].inc(sync.dma_start(brs[:], br_d), 16)
            sync.wait_ge(S["dmh"].sem, d1 + 48)
            S["dmh"].inc(sync.dma_start(
                lamq[:].rearrange("p (c f) -> p c f", c=4),
                lam_d.rearrange("c p f -> p c f")), 16)

        @blk.vector
        def _(v):
            for gg in range(4):
                pn = "dma" if gg % 2 == 0 else "dmh"
                pb = d0 if gg % 2 == 0 else d1
                v.wait_ge(S[pn].sem, pb + 16 * (gg // 2 + 1))
                S["t6"].inc(v.tensor_copy(
                    wproj[:, gg * 4 * 2048:(gg + 1) * 4 * 2048], wstg[gg % 2][:, 0:8192]))
            v.wait_ge(S["dma"].sem, d0 + 48)
            v.wait_ge(S["dmh"].sem, d1 + 48)
            v.tensor_copy(wa_sb[:], wstg[0][:, 0:16 * 520])
            v.tensor_copy(bc_sb[:], wstg[1][:, 0:8192])
            v.wait_ge(S["dma"].sem, d0 + 64)
            v.wait_ge(S["dmh"].sem, d1 + 64)
            sv = SerialEng(v, S["rb"])
            sv.tensor_mul(lamq[:, 0:64], lamq[:, 0:64], lamq[:, 64:128])
            sv.tensor_mul(lamq[:, 128:192], lamq[:, 128:192], lamq[:, 192:256])
            sv.reduce_sum(lamc[:, 0:1], lamq[:, 0:64], axis=AX.X)
            sv.reduce_sum(lamc[:, 1:2], lamq[:, 128:192], axis=AX.X)
            S["t7"].inc(v.tensor_copy(lamc[:, 0:2], lamc[:, 0:2]))

        @blk.scalar
        def _(s):
            s.wait_ge(S["t7"].sem, S["t7"].n)
            ss = SerialEng(s, S["rbs"])
            ss.activation(lamc[:, 2:3], lamc[:, 0:1], AF.Exp)
            ss.activation(lamc[:, 3:4], lamc[:, 1:2], AF.Exp)
            S["t8"].inc(s.activation(lamc[:, 2:4], lamc[:, 2:4], AF.Identity))

        @blk.gpsimd
        def _(g):
            graw = g
            g = SerialEng(g, S["rbp"])
            g.memset(onerow[:], 1.0)
            graw.wait_ge(S["t8"].sem, S["t8"].n)
            g.tensor_sub(lamc[:, 4:5], lamc[:, 2:3], lamc[:, 3:4])
            g.tensor_scalar_add(lamc[:, 4:5], lamc[:, 4:5], LAMBDA_INIT)
            g.tensor_copy(lamc[:, 6:8], lamc[:, 4:5].broadcast_to([2, 2]))
            g.affine_select(
                out=lamc[:, 6:8], in_=lamc[:, 6:8], compare_op=ALU.is_equal,
                fill=0.0, base=0, pattern=[[-1, 2]], channel_multiplier=1)
            S["t9"].inc(graw.tensor_copy(lamc[:, 6:8], lamc[:, 6:8]))

        @blk.tensor
        def _(t):
            t.wait_ge(S["t9"].sem, S["t9"].n)
            t.matmul(p_bc[:, 0:2], onerow[:], lamc[:, 6:8], start=True, stop=True)
            S["t10"].inc(t.matmul(p_bc[:, 8:16], onerow[0:1, :], brs[:],
                                  start=True, stop=True))

        @blk.vector
        def _(v):
            v.wait_ge(S["t10"].sem, S["t10"].n)
            v.tensor_copy(lam_b[:], p_bc[:, 0:2])
            S["t11"].inc(v.tensor_copy(br_b[:], p_bc[:, 8:16]))


def _p1(nc, S, env):
    x_d, qk_d, v_d, g_d = env["x_d"], env["qk_d"], env["v_d"], env["g_d"]
    wproj, cos_tm, sin_tm, ident = env["wproj"], env["cos_tm"], env["sin_tm"], env["ident"]
    consts = env["consts"]
    dbg_aps = env.get("dbg")

    with nc.Block() as blk, ExitStack() as tst:
        p_tpA = tst.enter_context(nc.psum_tensor("p1_tpA", [128, 1024], BF16))
        p_tpB = tst.enter_context(nc.psum_tensor("p1_tpB", [128, 1024], BF16))
        p_qk = tst.enter_context(nc.psum_tensor("p1_qk", [128, 1024], BF16))
        p_prj = [tst.enter_context(nc.psum_tensor(f"p1_prj{i}", [128, 512], F32))
                 for i in range(4)]
        xf32 = [tst.enter_context(nc.sbuf_tensor(f"p1_x{i}", [128, D], F32)) for i in range(3)]
        xn = [tst.enter_context(nc.sbuf_tensor(f"p1_xn{i}", [128, D], BF16)) for i in range(2)]
        xnT = [tst.enter_context(nc.sbuf_tensor(f"p1_xnT{i}", [128, D], BF16)) for i in range(2)]
        junk = tst.enter_context(nc.sbuf_tensor("p1_junk", [128, D], BF16))
        stats = [tst.enter_context(nc.sbuf_tensor(f"p1_st{i}", [128, 8], F32)) for i in range(2)]
        q2 = [tst.enter_context(nc.sbuf_tensor(f"p1_q2{i}", [128, 1024], F32)) for i in range(2)]
        s8 = [tst.enter_context(nc.sbuf_tensor(f"p1_s8{i}", [128, 24], F32)) for i in range(2)]
        qn = [tst.enter_context(nc.sbuf_tensor(f"p1_qn{i}", [128, 1024], BF16)) for i in range(2)]
        qrot = [tst.enter_context(nc.sbuf_tensor(f"p1_qr{i}", [128, 1024], BF16)) for i in range(2)]
        rsc = tst.enter_context(nc.sbuf_tensor("p1_rsc", [128, 1024], BF16))
        qkTs = [tst.enter_context(nc.sbuf_tensor(f"p1_qkT{i}", [128, 1024], BF16)) for i in range(2)]
        vgs = [tst.enter_context(nc.sbuf_tensor(f"p1_vg{i}", [128, 1024], BF16)) for i in range(2)]
        qc = tst.enter_context(nc.sbuf_tensor("p1_qc", [128, 1024], F32))
        gtmp = tst.enter_context(nc.sbuf_tensor("p1_gt", [128, 512], F32))
        esb = tst.enter_context(nc.sbuf_tensor("p1_es", [128, 512], F32))

        base = {k: S[k].n for k in S}

        def spill(sync, t):
            pr = t % 2
            snm = "dmg" if t % 2 == 0 else "dmi"
            sync.wait_ge(S["t8"].sem, base["t8"] + t + 1)
            for i in range(8):
                hi, isq = i // 2, (i % 2 == 0)
                pair, slot = hi // 2, (hi % 2) if isq else 2 + (hi % 2)
                S[snm].inc(sync.dma_start(
                    qk_d[pair, slot, :, t * 128:(t + 1) * 128],
                    qkTs[pr][:, i * 128:(i + 1) * 128]), 16)
            sync.wait_ge(S["t5"].sem, base["t5"] + t + 1)
            sync.wait_ge(S["u4"].sem, base["u4"] + t + 1)
            for p in range(2):
                S[snm].inc(sync.dma_start(
                    v_d[p, t, :, :], vgs[pr][:, p * 256:(p + 1) * 256]), 16)
                S[snm].inc(sync.dma_start(
                    g_d[p, t, :, :], vgs[pr][:, 512 + p * 256:512 + (p + 1) * 256]), 16)

        @blk.sync
        def _(sync):
            for t in range(NT + 3):
                if t < NT:
                    if t >= 3:
                        sync.wait_ge(S["t15"].sem, base["t15"] + t - 2)
                        sync.wait_ge(S["t14"].sem, base["t14"] + t - 2)
                    S[["dma", "dmh", "dmx"][t % 3]].inc(
                        sync.dma_start(xf32[t % 3][:], x_d[t * 128:(t + 1) * 128, :]), 16)
                if t >= 3:
                    spill(sync, t - 3)
            sync.wait_ge(S["dmg"].sem, S["dmg"].n)
            sync.wait_ge(S["dmi"].sem, S["dmi"].n)
            if dbg_aps:
                pr_l = (NT - 1) % 2
                for nm, tens in [("d_xn", xn[pr_l]), ("d_xnT", xnT[pr_l]),
                                 ("d_qn", qn[pr_l]), ("d_qr", qrot[pr_l]),
                                 ("d_st", stats[pr_l]), ("d_s8", s8[pr_l]),
                                 ("d_vgs", vgs[pr_l])]:
                    S["dmg"].inc(sync.dma_start(dbg_aps[nm], tens[:]), 16)
            sync.wait_ge(S["dmg"].sem, S["dmg"].n)

        @blk.vector
        def _(v):
            for t in range(NT + 2):
                if t < NT:
                    pr3 = t % 3
                    st = stats[t % 2]
                    lnm = ["dma", "dmh", "dmx"][t % 3]
                    v.wait_ge(S[lnm].sem, base[lnm] + 16 * (t // 3 + 1))
                    S["rb"].inc(v.reduce_sum(st[:, 0:1], xf32[pr3][:], axis=AX.X))
                    v.wait_ge(S["rb"].sem, S["rb"].n)
                    S["t14"].inc(v.tensor_scalar_mul(st[:, 2:3], st[:, 0:1], 1.0 / D))
                    v.wait_ge(S["t14"].sem, base["t14"] + t + 1)
                    v.wait_ge(S["t16"].sem, base["t16"] + t + 1)
                    S["rb"].inc(v.tensor_mul(st[:, 3:4], st[:, 2:3], st[:, 2:3]))
                    v.wait_ge(S["rb"].sem, S["rb"].n)
                    S["t17"].inc(v.scalar_tensor_tensor(st[:, 4:5], st[:, 1:2], 1.0 / D,
                                                        st[:, 3:4], ALU.mult, ALU.subtract))
                    v.wait_ge(S["t18"].sem, base["t18"] + t + 1)
                    S["t19"].inc(v.scalar_tensor_tensor(st[:, 6:7], st[:, 2:3], -1.0,
                                                        st[:, 5:6], ALU.mult, ALU.mult))
                    v.wait_ge(S["t20"].sem, base["t20"] + 2 * t + 1)
                    S["t21"].inc(v.tensor_copy(xnT[t % 2][:, 0:1024], p_tpA[:]))
                    v.wait_ge(S["t20"].sem, base["t20"] + 2 * t + 2)
                    S["t21"].inc(v.tensor_copy(xnT[t % 2][:, 1024:2048], p_tpB[:]))
                if 0 <= t - 1 < NT:
                    u = t - 1
                    pu = u % 2
                    v.wait_ge(S["t1"].sem, base["t1"] + u + 1)
                    S["rb"].inc(v.reduce_sum(
                        s8[pu][:, 0:8], q2[pu][:].rearrange("p (s x) -> p s x", s=8),
                        axis=AX.X))
                    v.wait_ge(S["rb"].sem, S["rb"].n)
                    S["t2"].inc(v.tensor_scalar(s8[pu][:, 8:16], s8[pu][:, 0:8],
                                                1.0 / 128.0, EPS_RMS, ALU.mult, ALU.add))
                    v.wait_ge(S["t3"].sem, base["t3"] + u + 1)
                    v.tensor_tensor(
                        qn[pu][:, 0:512].rearrange("p (s x) -> p s x", s=4),
                        qc[:, 0:512].rearrange("p (s x) -> p s x", s=4),
                        s8[pu][:, 16:20].rearrange("p (s o) -> p s o", o=1
                                                   ).broadcast_to([128, 4, 128]),
                        ALU.mult)
                    S["t4"].inc(v.tensor_tensor(
                        qn[pu][:, 512:1024].rearrange("p (s x) -> p s x", s=4),
                        qc[:, 512:1024].rearrange("p (s x) -> p s x", s=4),
                        s8[pu][:, 20:24].rearrange("p (s o) -> p s o", o=1
                                                   ).broadcast_to([128, 4, 128]),
                        ALU.mult))
                    v.wait_ge(S["t5"].sem, base["t5"] + u + 1)
                    S["rb"].inc(v.tensor_scalar_add(esb[:], esb[:], 1.0))
                    v.wait_ge(S["rb"].sem, S["rb"].n)
                    S["rb"].inc(v.reciprocal(esb[:], esb[:]))
                    v.wait_ge(S["rb"].sem, S["rb"].n)
                    S["u4"].inc(v.tensor_mul(vgs[pu][:, 512:1024], gtmp[:], esb[:]))
                if 0 <= t - 2 < NT:
                    w = t - 2
                    v.wait_ge(S["t7"].sem, base["t7"] + w + 1)
                    if w >= 2:
                        c3 = "dmg" if (w - 2) % 2 == 0 else "dmi"
                        v.wait_ge(S[c3].sem, base[c3] + 192 * ((w - 2) // 2 + 1))
                    S["t8"].inc(v.tensor_copy(qkTs[w % 2][:], p_qk[:]))

        @blk.scalar
        def _(s):
            for t in range(NT + 1):
                if t < NT:
                    pr3 = t % 3
                    st = stats[t % 2]
                    lnm = ["dma", "dmh", "dmx"][t % 3]
                    s.wait_ge(S[lnm].sem, base[lnm] + 16 * (t // 3 + 1))
                    S["t16"].inc(s.activation(junk[:], xf32[pr3][:], AF.Square,
                                              accum_out=st[:, 1:2]))
                    s.wait_ge(S["t17"].sem, base["t17"] + t + 1)
                    S["rbs"].inc(s.activation(st[:, 7:8], st[:, 4:5], AF.Ln,
                                              bias=consts[:, 0:1]))
                    s.wait_ge(S["rbs"].sem, S["rbs"].n)
                    S["t18"].inc(s.activation(st[:, 5:6], st[:, 7:8], AF.Exp, scale=-0.5))
                    s.wait_ge(S["t19"].sem, base["t19"] + t + 1)
                    if t >= 2:
                        s.wait_ge(S["t20"].sem, base["t20"] + 2 * (t - 2) + 2)
                    S["t15"].inc(s.activation(xn[t % 2][:], xf32[pr3][:], AF.Identity,
                                              bias=st[:, 6:7], scale=st[:, 5:6]))
                if 0 <= t - 1 < NT:
                    u = t - 1
                    pu = u % 2
                    s.wait_ge(S["t22"].sem, base["t22"] + u + 1)
                    if u >= 1:
                        s.wait_ge(S["t4"].sem, base["t4"] + u)
                        s.wait_ge(S["u4"].sem, base["u4"] + u)
                    if u >= 2:
                        c2 = "dmg" if u % 2 == 0 else "dmi"
                        s.wait_ge(S[c2].sem, base[c2] + 192 * ((u - 2) // 2 + 1))
                    S["rbs"].inc(s.activation(qc[:, 0:512], p_prj[0][:], AF.Copy))
                    S["rbs"].inc(s.activation(qc[:, 512:1024], p_prj[1][:], AF.Copy))
                    s.activation(vgs[pu][:, 0:512], p_prj[2][:], AF.Copy)
                    s.activation(gtmp[:], p_prj[3][:], AF.Copy)
                    S["t5"].inc(s.activation(esb[:], p_prj[3][:], AF.Exp, scale=-1.0))
                    s.wait_ge(S["rbs"].sem, S["rbs"].n)
                    s.activation(q2[pu][:, 0:512], qc[:, 0:512], AF.Square)
                    S["t1"].inc(s.activation(q2[pu][:, 512:1024], qc[:, 512:1024],
                                             AF.Square))
                    s.wait_ge(S["t2"].sem, base["t2"] + u + 1)
                    S["rbs"].inc(s.activation(s8[pu][:, 16:24], s8[pu][:, 8:16], AF.Ln))
                    s.wait_ge(S["rbs"].sem, S["rbs"].n)
                    S["t3"].inc(s.activation(s8[pu][:, 16:24], s8[pu][:, 16:24], AF.Exp,
                                             scale=-0.5))

        @blk.tensor
        def _(t_):
            for t in range(NT + 2):
                if 0 <= t - 2 < NT:
                    w = t - 2
                    t_.wait_ge(S["t6"].sem, base["t6"] + w + 1)
                    if w >= 1:
                        t_.wait_ge(S["t8"].sem, base["t8"] + w)
                    for i in range(8):
                        ins = t_.transpose(p_qk[:, i * 128:(i + 1) * 128],
                                           qrot[w % 2][:, i * 128:(i + 1) * 128], ident[:])
                        if i == 7:
                            S["t7"].inc(ins)
                if t < NT:
                    t_.wait_ge(S["t15"].sem, base["t15"] + t + 1)
                    if t >= 1:
                        t_.wait_ge(S["t21"].sem, base["t21"] + 2 * (t - 1) + 1)
                    for i in range(8):
                        ins = t_.transpose(p_tpA[:, i * 128:(i + 1) * 128],
                                           xn[t % 2][:, i * 128:(i + 1) * 128], ident[:])
                        if i == 7:
                            S["t20"].inc(ins)
                    if t >= 1:
                        t_.wait_ge(S["t21"].sem, base["t21"] + 2 * (t - 1) + 2)
                    for i in range(8):
                        ins = t_.transpose(p_tpB[:, i * 128:(i + 1) * 128],
                                           xn[t % 2][:, (8 + i) * 128:(9 + i) * 128], ident[:])
                        if i == 7:
                            S["t20"].inc(ins)
                    t_.wait_ge(S["t21"].sem, base["t21"] + 2 * t + 2)
                    if t >= 1:
                        t_.wait_ge(S["t5"].sem, base["t5"] + t)
                    for c in range(16):
                        for sx in range(4):
                            ins = t_.matmul(
                                p_prj[sx][:], xnT[t % 2][:, c * 128:(c + 1) * 128],
                                wproj[:, c * 2048 + sx * 512:c * 2048 + (sx + 1) * 512],
                                start=(c == 0), stop=(c == 15))
                            if c == 15 and sx == 3:
                                S["t22"].inc(ins)

        @blk.gpsimd
        def _(g):
            for t in range(1, NT + 1):
                u = t - 1
                pu = u % 2
                g.wait_ge(S["t4"].sem, base["t4"] + u + 1)
                if u >= 1:
                    g.wait_ge(S["t6"].sem, base["t6"] + u)
                if u >= 2:
                    g.wait_ge(S["t7"].sem, base["t7"] + u - 1)
                qv = qn[pu][:].rearrange("p (s h x) -> p s h x", s=8, h=2)
                rv = qrot[pu][:].rearrange("p (s h x) -> p s h x", s=8, h=2)
                sc = rsc[:, 0:512].rearrange("p (s o x) -> p s o x", s=8, o=1)
                sc2 = rsc[:, 512:1024].rearrange("p (s o x) -> p s o x", s=8, o=1)
                c_t = cos_tm[:, u * 64:(u + 1) * 64].rearrange(
                    "p (a o x) -> p a o x", a=1, o=1).broadcast_to([128, 8, 1, 64])
                s_t = sin_tm[:, u * 64:(u + 1) * 64].rearrange(
                    "p (a o x) -> p a o x", a=1, o=1).broadcast_to([128, 8, 1, 64])
                g.tensor_mul(rv[:, :, 0:1, :], qv[:, :, 0:1, :], c_t)
                g.tensor_mul(sc, qv[:, :, 1:2, :], s_t)
                S["rbp"].inc(g.tensor_mul(sc2, qv[:, :, 0:1, :], s_t))
                g.wait_ge(S["rbp"].sem, S["rbp"].n)
                g.tensor_add(rv[:, :, 0:1, :], rv[:, :, 0:1, :], sc)
                S["rbp"].inc(g.tensor_mul(rv[:, :, 1:2, :], qv[:, :, 1:2, :], c_t))
                g.wait_ge(S["rbp"].sem, S["rbp"].n)
                S["t6"].inc(g.tensor_sub(rv[:, :, 1:2, :], rv[:, :, 1:2, :], sc2))


def _p2(nc, S, env):
    qk_d, v_d, g_d, y_d = env["qk_d"], env["v_d"], env["g_d"], env["y_d"]
    maskA, ones_col, lam_b = env["maskA"], env["ones_col"], env["lam_b"]
    consts = env["consts"]

    with ExitStack() as tst:
        qs_ = [tst.enter_context(nc.sbuf_tensor(f"p2_q{i}", [128, T], BF16)) for i in range(2)]
        ks_ = [tst.enter_context(nc.sbuf_tensor(f"p2_k{i}", [128, T], BF16)) for i in range(2)]
        v_sb = tst.enter_context(nc.sbuf_tensor("p2_v", [128, (T // 128) * VD], BF16))
        g_sb = tst.enter_context(nc.sbuf_tensor("p2_g", [128, (T // 128) * VD], BF16))
        P2b = [tst.enter_context(nc.sbuf_tensor(f"p2_P{i}", [128, 512], BF16)) for i in range(2)]
        eR = tst.enter_context(nc.sbuf_tensor("p2_eR", [128, 8], F32))
        ey = tst.enter_context(nc.sbuf_tensor("p2_ey", [128, 256], F32))
        ey2 = tst.enter_context(nc.sbuf_tensor("p2_ey2", [128, 256], F32))
        eyg = tst.enter_context(nc.sbuf_tensor("p2_eyg", [128, 512], F32))
        esq = tst.enter_context(nc.sbuf_tensor("p2_esq", [128, 256], BF16))
        es = tst.enter_context(nc.sbuf_tensor("p2_es", [128, 8], F32))
        ybf = [tst.enter_context(nc.sbuf_tensor(f"p2_ybf{i}", [128, 256], BF16)) for i in range(2)]

        for pr in range(2):
            with nc.Block() as blk, ExitStack() as pst:
                p_s = [pst.enter_context(nc.psum_tensor(f"p2_s{pr}{i}", [128, 512], F32))
                       for i in range(2)]
                p_o = [pst.enter_context(nc.psum_tensor(f"p2_o{pr}{i}", [128, 512], F32))
                       for i in range(4)]
                p_z = [pst.enter_context(nc.psum_tensor(f"p2_z{pr}{i}", [128, 4], F32))
                       for i in range(2)]
                base = {k: S[k].n for k in S}

                iters = []
                for j in range(NQB):
                    for kt in range(2 * j + 2):
                        kind = "full" if kt < 2 * j else ("diag0" if kt == 2 * j else "half")
                        iters.append((j, kt, kind))
                n_it = len(iters)
                cum_it = {}
                c = 0
                nf = []
                nm = []
                cf = cm = 0
                for (j, kt, kind) in iters:
                    c += 1
                    if kind == "full":
                        cf += 1
                    else:
                        cm += 1
                    nf.append(cf)
                    nm.append(cm)
                    if kt == 2 * j + 1:
                        cum_it[j] = c

                @blk.sync
                def _(sync):
                    S["dma"].inc(sync.dma_start(ks_[0][:], qk_d[pr, 2]), 16)
                    S["dma"].inc(sync.dma_start(ks_[1][:], qk_d[pr, 3]), 16)
                    S["dma"].inc(sync.dma_start(qs_[0][:], qk_d[pr, 0]), 16)
                    S["dma"].inc(sync.dma_start(qs_[1][:], qk_d[pr, 1]), 16)
                    S["dmx"].inc(sync.dma_start(
                        v_sb[:].rearrange("p (t f) -> p t f", f=VD),
                        v_d[pr].rearrange("t p f -> p t f")), 16)
                    S["dmc"].inc(sync.dma_start(
                        g_sb[:].rearrange("p (t f) -> p t f", f=VD),
                        g_d[pr].rearrange("t p f -> p t f")), 16)
                    for j in range(NQB):
                        for qs in range(2):
                            u = 2 * j + qs
                            sync.wait_ge(S["t20"].sem, base["t20"] + u + 1)
                            S["dmg" if u % 2 == 0 else "dmi"].inc(sync.dma_start(
                                y_d[j * 256 + qs * 128: j * 256 + (qs + 1) * 128,
                                    pr * 256:(pr + 1) * 256], ybf[u % 2][:]), 16)
                    sync.wait_ge(S["dmg"].sem, S["dmg"].n)
                    sync.wait_ge(S["dmi"].sem, S["dmi"].n)

                @blk.tensor
                def _(t_):
                    t_.wait_ge(S["dma"].sem, base["dma"] + 64)
                    for i, (j, kt, kind) in enumerate(iters):
                        sb = p_s[i % 2]
                        if i >= 2:
                            t_.wait_ge(S["t15"].sem, base["t15"] + nf[i - 2])
                            t_.wait_ge(S["t13"].sem, base["t13"] + nm[i - 2])
                        if kind == "half":
                            qsl = slice(j * 256 + 128, j * 256 + 256)
                            t_.matmul(sb[:, 0:128], ks_[0][:, kt * 128:(kt + 1) * 128],
                                      qs_[0][:, qsl], start=True, stop=True,
                                      skip_group_check=True)
                            ins = t_.matmul(sb[:, 128:256], ks_[1][:, kt * 128:(kt + 1) * 128],
                                            qs_[1][:, qsl], start=True, stop=True,
                                            skip_group_check=True)
                        else:
                            qsl = slice(j * 256, j * 256 + 256)
                            t_.matmul(sb[:, 0:256], ks_[0][:, kt * 128:(kt + 1) * 128],
                                      qs_[0][:, qsl], start=True, stop=True,
                                      skip_group_check=True)
                            ins = t_.matmul(sb[:, 256:512], ks_[1][:, kt * 128:(kt + 1) * 128],
                                            qs_[1][:, qsl], start=True, stop=True,
                                            skip_group_check=True)
                        S["t14"].inc(ins)
                        if i >= 1:
                            _p2_pv(nc, t_, S, base, iters[i - 1], i - 1, P2b, p_o, p_z,
                                   v_sb, ones_col, nf[i - 1], nm[i - 1])
                    _p2_pv(nc, t_, S, base, iters[n_it - 1], n_it - 1, P2b, p_o, p_z,
                           v_sb, ones_col, nf[n_it - 1], nm[n_it - 1])

                @blk.scalar
                def _(s):
                    da = 0
                    for i, (j, kt, kind) in enumerate(iters):
                        s.wait_ge(S["t14"].sem, base["t14"] + i + 1)
                        if i >= 2:
                            s.wait_ge(S["t16"].sem, base["t16"] + i - 1)
                        w = 256 if kind == "half" else 512
                        ins = s.activation(P2b[i % 2][:, 0:w], p_s[i % 2][:, 0:w],
                                           AF.Exp, scale=SCALE)
                        if kind == "full":
                            S["t15"].inc(ins)
                        else:
                            S["t17"].inc(ins)
                            da += 1
                        if kt == min(5, 2 * j + 1) and j >= 1:
                            for qs in range(2):
                                u = 2 * (j - 1) + qs
                                s.wait_ge(S["t18"].sem, base["t18"] + u + 1)
                                S["rbs"].inc(s.activation(
                                    esq[:], eyg[:, qs * 256:(qs + 1) * 256],
                                    AF.Square, accum_out=es[:, qs:qs + 1]))
                                s.wait_ge(S["rbs"].sem, S["rbs"].n)
                                S["rbs"].inc(s.activation(
                                    es[:, 2 + qs:3 + qs], es[:, qs:qs + 1], AF.Ln,
                                    scale=1.0 / 256.0, bias=consts[:, 1:2]))
                                s.wait_ge(S["rbs"].sem, S["rbs"].n)
                                S["t19"].inc(s.activation(
                                    es[:, 4 + qs:5 + qs], es[:, 2 + qs:3 + qs], AF.Exp,
                                    scale=-0.5, bias=consts[:, 3:4]))
                    for qs in range(2):
                        u = 2 * (NQB - 1) + qs
                        s.wait_ge(S["t18"].sem, base["t18"] + u + 1)
                        S["rbs"].inc(s.activation(
                            esq[:], eyg[:, qs * 256:(qs + 1) * 256],
                            AF.Square, accum_out=es[:, qs:qs + 1]))
                        s.wait_ge(S["rbs"].sem, S["rbs"].n)
                        S["rbs"].inc(s.activation(
                            es[:, 2 + qs:3 + qs], es[:, qs:qs + 1], AF.Ln,
                            scale=1.0 / 256.0, bias=consts[:, 1:2]))
                        s.wait_ge(S["rbs"].sem, S["rbs"].n)
                        S["t19"].inc(s.activation(
                            es[:, 4 + qs:5 + qs], es[:, 2 + qs:3 + qs], AF.Exp,
                            scale=-0.5, bias=consts[:, 3:4]))

                @blk.vector
                def _(v):
                    da = 0
                    for i, (j, kt, kind) in enumerate(iters):
                        if kind == "diag0":
                            da += 1
                            v.wait_ge(S["t17"].sem, base["t17"] + da)
                            S["t13"].inc(v.tensor_mul(
                                P2b[i % 2][:].rearrange("p (s x) -> p s x", s=2),
                                P2b[i % 2][:].rearrange("p (s x) -> p s x", s=2),
                                maskA[:].rearrange("p (o x) -> p o x", o=1
                                                   ).broadcast_to([128, 2, 256])))
                        elif kind == "half":
                            da += 1
                            v.wait_ge(S["t17"].sem, base["t17"] + da)
                            S["t13"].inc(v.tensor_mul(
                                P2b[i % 2][:, 0:256].rearrange("p (s x) -> p s x", s=2),
                                P2b[i % 2][:, 0:256].rearrange("p (s x) -> p s x", s=2),
                                maskA[:, 0:128].rearrange("p (o x) -> p o x", o=1
                                                          ).broadcast_to([128, 2, 128])))
                            _p2_epi(nc, v, S, base, j, pr, cum_it[j], p_o, p_z,
                                    eR, ey, ey2, eyg, es, ybf, g_sb, lam_b)

                @blk.gpsimd
                def _(g):
                    pass

    return


def _p2_pv(nc, t_, S, base, it, i, P2b, p_o, p_z, v_sb, ones_col, nfi, nmi):
    j, kt, kind = it
    og = 2 * (j % 2)
    if i == 0:
        t_.wait_ge(S["dmx"].sem, base["dmx"] + 16)
    if kt == 0 and j >= 2:
        t_.wait_ge(S["t21"].sem, base["t21"] + j - 1)
    if kind == "full":
        t_.wait_ge(S["t15"].sem, base["t15"] + nfi)
    else:
        t_.wait_ge(S["t13"].sem, base["t13"] + nmi)
    combos = [(0, 1), (1, 1)] if kind == "half" else [(0, 0), (0, 1), (1, 0), (1, 1)]
    last = None
    for (st, qs) in combos:
        off = st * 128 if kind == "half" else st * 256 + qs * 128
        psl = P2b[i % 2][:, off:off + 128]
        t_.matmul(p_o[og + st][:, qs * 256:(qs + 1) * 256], psl,
                  v_sb[:, kt * VD:(kt + 1) * VD],
                  start=(kt == 0 and qs == 0), stop=False, skip_group_check=True)
        last = t_.matmul(p_z[j % 2][:, st * 2 + qs:st * 2 + qs + 1], psl, ones_col[:],
                         start=(kt == 0 and st == 0 and qs == 0), stop=False,
                         skip_group_check=True)
    S["t16"].inc(last)


def _p2_epi(nc, v, S, base, j, pr, cum, p_o, p_z, eR, ey, ey2, eyg, es, ybf,
            g_sb, lam_b):
    og = 2 * (j % 2)
    if j == 0:
        v.wait_ge(S["dmc"].sem, base["dmc"] + 16)
    v.wait_ge(S["t16"].sem, base["t16"] + cum)
    S["rb"].inc(v.reciprocal(eR[:, 0:4], p_z[j % 2][:]))
    v.wait_ge(S["rb"].sem, S["rb"].n)
    S["rb"].inc(v.tensor_scalar_mul(eR[:, 4:6], eR[:, 2:4], lam_b[:, pr:pr + 1]))
    v.wait_ge(S["rb"].sem, S["rb"].n)
    for qs in range(2):
        S["rb"].inc(v.tensor_scalar_mul(
            ey[:], p_o[og + 1][:, qs * 256:(qs + 1) * 256], eR[:, 4 + qs:5 + qs]))
        v.wait_ge(S["rb"].sem, S["rb"].n)
        S["rb"].inc(v.scalar_tensor_tensor(
            ey2[:], p_o[og][:, qs * 256:(qs + 1) * 256], eR[:, qs:qs + 1], ey[:],
            ALU.mult, ALU.subtract))
        v.wait_ge(S["rb"].sem, S["rb"].n)
        S["t18"].inc(v.tensor_mul(
            eyg[:, qs * 256:(qs + 1) * 256], ey2[:],
            g_sb[:, (2 * j + qs) * VD:(2 * j + qs + 1) * VD]))
        v.wait_ge(S["rb"].sem, S["rb"].n)
    for qs in range(2):
        u = 2 * j + qs
        v.wait_ge(S["t19"].sem, base["t19"] + u + 1)
        if u >= 2:
            cu = "dmg" if u % 2 == 0 else "dmi"
            v.wait_ge(S[cu].sem, base[cu] + 16 * ((u - 2) // 2 + 1))
        S["t20"].inc(v.tensor_scalar_mul(ybf[u % 2][:], eyg[:, qs * 256:(qs + 1) * 256],
                                         es[:, 4 + qs:5 + qs]))
    S["t21"].inc(v.tensor_copy(eR[:, 6:7], eR[:, 0:1]))


def _p4(nc, S, env):
    x_d, cc_d, y_out = env["x_d"], env["cc_d"], env["y_out"]
    wa_sb, bc_sb, br_b, ident = env["wa_sb"], env["bc_sb"], env["br_b"], env["ident"]
    consts, maskA = env["consts"], env["maskA"]
    ccq, xq, xq_done = env["ccq"], env["xq"], env["xq_done"]
    dbg_aps = env.get("dbg")
    NT4 = T // 512
    QT = T // 4

    with nc.Block() as blk, ExitStack() as tst:
        p_tpA = tst.enter_context(nc.psum_tensor("p4_tpA", [128, 1024], BF16))
        p_tpB = tst.enter_context(nc.psum_tensor("p4_tpB", [128, 1024], BF16))
        p_h = tst.enter_context(nc.psum_tensor("p4_h", [128, 512], F32))
        p_lg = tst.enter_context(nc.psum_tensor("p4_lg", [128, 8], F32))
        p_out = tst.enter_context(nc.psum_tensor("p4_out", [128, 2048], F32))
        ytile = [tst.enter_context(nc.sbuf_tensor(f"p4_y{i}", [128, D], BF16)) for i in range(2)]
        xtile = [tst.enter_context(nc.sbuf_tensor(f"p4_x{i}", [128, D], F32)) for i in range(2)]
        lny = tst.enter_context(nc.sbuf_tensor("p4_lny", [128, D], F32))
        x1 = [tst.enter_context(nc.sbuf_tensor(f"p4_x1{i}", [128, D], F32)) for i in range(4)]
        xf = [tst.enter_context(nc.sbuf_tensor(f"p4_xf{i}", [128, D], BF16)) for i in range(2)]
        xfT = tst.enter_context(nc.sbuf_tensor("p4_xfT", [128, D], BF16))
        hmT = tst.enter_context(nc.sbuf_tensor("p4_hmT", [128, 512], BF16))
        hm = tst.enter_context(nc.sbuf_tensor("p4_hm", [128, 512], BF16))
        junk = tst.enter_context(nc.sbuf_tensor("p4_junk", [128, D], BF16))
        st = [tst.enter_context(nc.sbuf_tensor(f"p4_st{i}", [128, 8], F32)) for i in range(2)]
        rr1 = [tst.enter_context(nc.sbuf_tensor(f"p4_rr1{i}", [128, 8], F32)) for i in range(2)]
        rr = tst.enter_context(nc.sbuf_tensor("p4_rr", [128, 32], F32))
        lnf = tst.enter_context(nc.sbuf_tensor("p4_lnf", [128, D], F32))
        sqa = tst.enter_context(nc.sbuf_tensor("p4_sqa", [128, D], F32))
        sqb = tst.enter_context(nc.sbuf_tensor("p4_sqb", [128, D], F32))
        osb = tst.enter_context(nc.sbuf_tensor("p4_osb", [128, D], F32))
        fo = [tst.enter_context(nc.sbuf_tensor(f"p4_fo{i}", [128, D], F32)) for i in range(2)]
        base = {k: S[k].n for k in S}

        def tgt_tpA(t):
            return base["t13"] + 2 * t + 1

        def tgt_tpB(t):
            return base["t13"] + 2 * t + 2

        def tgt_evB(t):
            return base["t22"] + 2 * t + 2

        NIT = NT4 + 4

        @blk.sync
        def _(sync):
            pid = nc.partition_id([ET.SP])
            qoff = nc.snap((pid % 4) * QT, min_val=0, max_val=3 * QT)
            for rk in range(4):
                S["dmc"].inc(sync.dma_start(
                    ccq[rk], cc_d[rk * T:(rk + 1) * T, :][bass.ds(qoff, QT), :]), 16)
            sync.wait_ge(S["dmc"].sem, S["dmc"].n)
            for t in range(NIT):
                if 0 <= t - 2 < NT4:
                    u = t - 2
                    sync.wait_ge(S["t19"].sem, base["t19"] + u + 1)
                    if u >= 1:
                        sync.wait_ge(S["t20"].sem, base["t20"] + u)
                    for i in range(4):
                        S["dmc"].inc(sync.dma_start_transpose(
                            hmT[:, i * 128:(i + 1) * 128],
                            hm[:, i * 128:(i + 1) * 128]), 16)
                if 0 <= t - 4 < NT4:
                    w = t - 4
                    sync.wait_ge(S["u3"].sem, base["u3"] + w + 1)
                    S["dmg" if w % 2 == 0 else "dmi"].inc(sync.dma_start(
                        y_out[w * 128:(w + 1) * 128, :], fo[w % 2][:]), 16)
            if dbg_aps:
                tl = NT4 - 1
                for nm, tens in [("d_p4x1", x1[tl % 4]), ("d_p4xf", xf[tl % 2]),
                                 ("d_p4hm", hm), ("d_p4rr", rr), ("d_p4yt", ytile[tl % 2]),
                                 ("d_p4br", br_b), ("d_p4xfT", xfT),
                                 ("d_p4wa", wa_sb[:, 0:1040])]:
                    S["dmg"].inc(sync.dma_start(dbg_aps[nm], tens[:]), 16)
            sync.wait_ge(S["dmg"].sem, S["dmg"].n)
            sync.wait_ge(S["dmi"].sem, S["dmi"].n)

        @blk.vector
        def _(v):
            for t in range(NIT):
                if 0 <= t - 1 < NT4:
                    w1_ = t - 1
                    v.wait_ge(S["t13"].sem, tgt_tpA(w1_))
                    S["t22"].inc(v.tensor_copy(xfT[:, 0:1024], p_tpA[:]))
                    v.wait_ge(S["t13"].sem, tgt_tpB(w1_))
                    S["t22"].inc(v.tensor_copy(xfT[:, 1024:2048], p_tpB[:]))
                if t < NT4:
                    pr = t % 2
                    lnm = "dma" if t % 2 == 0 else "dmh"
                    v.wait_ge(S[lnm].sem, base[lnm] + 80 * ((t // 2) + 1))
                    S["rb"].inc(v.reduce_sum(st[pr][:, 0:1], ytile[pr][:], axis=AX.X))
                    v.wait_ge(S["rb"].sem, S["rb"].n)
                    S["rb"].inc(v.tensor_scalar_mul(st[pr][:, 2:3], st[pr][:, 0:1], 1.0 / D))
                    v.wait_ge(S["rb"].sem, S["rb"].n)
                    v.wait_ge(S["t1"].sem, base["t1"] + t + 1)
                    S["rb"].inc(v.tensor_mul(st[pr][:, 3:4], st[pr][:, 2:3], st[pr][:, 2:3]))
                    v.wait_ge(S["rb"].sem, S["rb"].n)
                    S["t2"].inc(v.scalar_tensor_tensor(st[pr][:, 4:5], st[pr][:, 1:2],
                                                       1.0 / D, st[pr][:, 3:4],
                                                       ALU.mult, ALU.subtract))
                    v.wait_ge(S["t3"].sem, base["t3"] + t + 1)
                    S["t4"].inc(v.scalar_tensor_tensor(st[pr][:, 6:7], st[pr][:, 2:3],
                                                       -1.0, st[pr][:, 5:6],
                                                       ALU.mult, ALU.mult))
                    v.wait_ge(S["t5"].sem, base["t5"] + t + 1)
                    if t >= 3:
                        v.wait_ge(S["u3"].sem, base["u3"] + t - 3)
                    S["t6"].inc(v.tensor_add(x1[t % 4][:], xtile[pr][:], lny[:]))
                    S["rb"].inc(v.reduce_sum(rr1[pr][:, 0:1], x1[t % 4][:], axis=AX.X))
                    v.wait_ge(S["rb"].sem, S["rb"].n)
                    S["rb"].inc(v.tensor_scalar_mul(rr1[pr][:, 2:3], rr1[pr][:, 0:1], 1.0 / D))
                    v.wait_ge(S["rb"].sem, S["rb"].n)
                    v.wait_ge(S["t7"].sem, base["t7"] + t + 1)
                    S["rb"].inc(v.tensor_mul(rr1[pr][:, 3:4], rr1[pr][:, 2:3], rr1[pr][:, 2:3]))
                    v.wait_ge(S["rb"].sem, S["rb"].n)
                    S["t8"].inc(v.scalar_tensor_tensor(rr1[pr][:, 4:5], rr1[pr][:, 1:2],
                                                       1.0 / D, rr1[pr][:, 3:4],
                                                       ALU.mult, ALU.subtract))
                    v.wait_ge(S["t9"].sem, base["t9"] + t + 1)
                    S["t10"].inc(v.scalar_tensor_tensor(rr1[pr][:, 6:7], rr1[pr][:, 2:3],
                                                        -1.0, rr1[pr][:, 5:6],
                                                        ALU.mult, ALU.mult))
                if 0 <= t - 1 < NT4:
                    u = t - 1
                    v.wait_ge(S["t16"].sem, base["t16"] + u + 1)
                    S["rb"].inc(v.tensor_add(rr[:, 8:16], p_lg[:], br_b[:]))
                    v.wait_ge(S["rb"].sem, S["rb"].n)
                    S["rb"].inc(v.reduce_max(rr[:, 16:17], rr[:, 8:16], axis=AX.X))
                    v.wait_ge(S["rb"].sem, S["rb"].n)
                    S["t17"].inc(v.tensor_scalar_mul(rr[:, 17:18], rr[:, 16:17], -1.0))
                    v.wait_ge(S["t18"].sem, base["t18"] + u + 1)
                    S["rb"].inc(v.reciprocal(rr[:, 19:20], rr[:, 18:19]))
                    v.wait_ge(S["rb"].sem, S["rb"].n)
                    S["rb"].inc(v.tensor_single_scalar(rr[:, 24:32], rr[:, 8:16],
                                                       rr[:, 16:17], ALU.is_ge))
                    v.wait_ge(S["rb"].sem, S["rb"].n)
                    S["rb"].inc(v.tensor_scalar(rr[:, 24:32], rr[:, 24:32], rr[:, 19:20],
                                                1.0 / 64.0, ALU.mult, ALU.mult))
                    v.wait_ge(S["rb"].sem, S["rb"].n)
                    if u >= 1:
                        v.wait_ge(S["dmc"].sem, base["dmc"] + 64 + 64 * u)
                    S["t19"].inc(v.tensor_mul(
                        hm[:].rearrange("p (e r) -> p e r", e=8),
                        p_h[:].rearrange("p (e r) -> p e r", e=8),
                        rr[:, 24:32].rearrange("p (e o) -> p e o", o=1
                                               ).broadcast_to([128, 8, 64])))
                if 0 <= t - 3 < NT4:
                    w = t - 3
                    v.wait_ge(S["u5"].sem, base["u5"] + w + 1)
                    v.wait_ge(S["t21"].sem, base["t21"] + w + 1)
                    S["rb"].inc(v.tensor_scalar_mul(rr[:, 3:4], rr[:, 0:1], 1.0 / D))
                    v.wait_ge(S["rb"].sem, S["rb"].n)
                    S["rb"].inc(v.tensor_mul(rr[:, 20:21], rr[:, 3:4], rr[:, 3:4]))
                    v.wait_ge(S["rb"].sem, S["rb"].n)
                    S["u1"].inc(v.scalar_tensor_tensor(rr[:, 4:5], rr[:, 1:2], 1.0 / D,
                                                       rr[:, 20:21], ALU.mult, ALU.subtract))
                    v.wait_ge(S["u2"].sem, base["u2"] + w + 1)
                    S["rb"].inc(v.scalar_tensor_tensor(rr[:, 6:7], rr[:, 3:4], -1.0,
                                                       rr[:, 5:6], ALU.mult, ALU.mult))
                    v.wait_ge(S["rb"].sem, S["rb"].n)
                    if w >= 1:
                        v.wait_ge(S["u3"].sem, base["u3"] + w)
                    S["t15"].inc(v.tensor_scalar(lnf[:], osb[:], rr[:, 5:6], rr[:, 6:7],
                                                 ALU.mult, ALU.add))

        @blk.scalar
        def _(s):
            def loads(s_):
                lnm = "dma" if s_ % 2 == 0 else "dmh"
                for rk in range(4):
                    S[lnm].inc(s.dma_start(
                        ytile[s_ % 2][:, rk * 512:(rk + 1) * 512],
                        ccq[rk, s_ * 128:(s_ + 1) * 128, :]), 16)
                S[lnm].inc(s.dma_start(
                    xtile[s_ % 2][:], xq[s_ * 128:(s_ + 1) * 128, :]), 16)

            for t in range(NIT):
                if t == 0:
                    s.wait_ge(S["dmc"].sem, base["dmc"] + 64)
                    s.wait_ge(S["dmq"].sem, xq_done)
                    for s_ in (0, 1):
                        if s_ < NT4:
                            loads(s_)
                if 0 <= t - 3 < NT4:
                    w = t - 3
                    if w >= 1:
                        s.wait_ge(S["t15"].sem, base["t15"] + w)
                    S["u5"].inc(s.activation(osb[:], p_out[:], AF.Abs,
                                             accum_out=rr[:, 0:1]))
                if t < NT4:
                    pr = t % 2
                    lnm = "dma" if t % 2 == 0 else "dmh"
                    s.wait_ge(S[lnm].sem, base[lnm] + 80 * ((t // 2) + 1))
                    S["t1"].inc(s.activation(junk[:], ytile[pr][:], AF.Square,
                                             accum_out=st[pr][:, 1:2]))
                    s.wait_ge(S["t2"].sem, base["t2"] + t + 1)
                    S["rbs"].inc(s.activation(st[pr][:, 7:8], st[pr][:, 4:5], AF.Ln,
                                              bias=consts[:, 0:1]))
                    s.wait_ge(S["rbs"].sem, S["rbs"].n)
                    S["t3"].inc(s.activation(st[pr][:, 5:6], st[pr][:, 7:8], AF.Exp,
                                             scale=-0.5))
                    s.wait_ge(S["t4"].sem, base["t4"] + t + 1)
                    if t >= 1:
                        s.wait_ge(S["t6"].sem, base["t6"] + t)
                    S["t5"].inc(s.activation(lny[:], ytile[pr][:], AF.Identity,
                                             bias=st[pr][:, 6:7], scale=st[pr][:, 5:6]))
                    s.wait_ge(S["t6"].sem, base["t6"] + t + 1)
                    S["t7"].inc(s.activation(junk[:], x1[t % 4][:], AF.Square,
                                             accum_out=rr1[pr][:, 1:2]))
                    if t + 2 < NT4:
                        loads(t + 2)
                    s.wait_ge(S["t8"].sem, base["t8"] + t + 1)
                    S["rbs"].inc(s.activation(rr1[pr][:, 7:8], rr1[pr][:, 4:5], AF.Ln,
                                              bias=consts[:, 0:1]))
                    s.wait_ge(S["rbs"].sem, S["rbs"].n)
                    S["t9"].inc(s.activation(rr1[pr][:, 5:6], rr1[pr][:, 7:8], AF.Exp,
                                             scale=-0.5))
                    s.wait_ge(S["t10"].sem, base["t10"] + t + 1)
                    if t >= 2:
                        s.wait_ge(S["t13"].sem, tgt_tpB(t - 2))
                    S["t11"].inc(s.activation(xf[pr][:], x1[t % 4][:], AF.Identity,
                                              bias=rr1[pr][:, 6:7], scale=rr1[pr][:, 5:6]))
                if 0 <= t - 1 < NT4:
                    u = t - 1
                    s.wait_ge(S["t17"].sem, base["t17"] + u + 1)
                    S["t18"].inc(s.activation(junk[:, 0:8], rr[:, 8:16], AF.Exp,
                                              bias=rr[:, 17:18], accum_out=rr[:, 18:19]))
                    s.wait_ge(S["t18"].sem, base["t18"] + u + 1)
                if 0 <= t - 3 < NT4:
                    w = t - 3
                    s.wait_ge(S["u1"].sem, base["u1"] + w + 1)
                    S["rbs"].inc(s.activation(rr[:, 7:8], rr[:, 4:5], AF.Ln,
                                              bias=consts[:, 0:1]))
                    s.wait_ge(S["rbs"].sem, S["rbs"].n)
                    S["u2"].inc(s.activation(rr[:, 5:6], rr[:, 7:8], AF.Exp, scale=-0.5))
                if 0 <= t - 2 < NT4:
                    q = t - 2
                    s.wait_ge(S["t20"].sem, base["t20"] + q + 1)
                    if q >= 1:
                        s.wait_ge(S["u1"].sem, base["u1"] + q)
                    S["t21"].inc(s.activation(sqb[:], p_out[:], AF.Square,
                                              accum_out=rr[:, 1:2]))

        @blk.gpsimd
        def _(g):
            for t in range(3, NIT):
                w = t - 3
                if w >= NT4:
                    continue
                g.wait_ge(S["t15"].sem, base["t15"] + w + 1)
                g.wait_ge(S["t6"].sem, base["t6"] + w + 1)
                if w >= 2:
                    ct = "dmg" if (w - 2) % 2 == 0 else "dmi"
                    g.wait_ge(S[ct].sem, base[ct] + 16 * ((w - 2) // 2 + 1))
                S["u3"].inc(g.tensor_add(fo[w % 2][:], lnf[:], x1[w % 4][:]))

        @blk.tensor
        def _(t_):
            for t in range(NIT):
                if 0 <= t - 1 < NT4:
                    w1_ = t - 1
                    t_.wait_ge(S["t11"].sem, base["t11"] + w1_ + 1)
                    if w1_ >= 1:
                        t_.wait_ge(S["t22"].sem, tgt_evB(w1_ - 1))
                    for i in range(8):
                        ins = t_.transpose(p_tpA[:, i * 128:(i + 1) * 128],
                                           xf[w1_ % 2][:, i * 128:(i + 1) * 128], ident[:])
                        if i == 7:
                            S["t13"].inc(ins)
                    for i in range(8):
                        ins = t_.transpose(p_tpB[:, i * 128:(i + 1) * 128],
                                           xf[w1_ % 2][:, (8 + i) * 128:(9 + i) * 128], ident[:])
                        if i == 7:
                            S["t13"].inc(ins)
                    t_.wait_ge(S["t22"].sem, tgt_evB(w1_))
                    t_.wait_ge(S["t19"].sem, base["t19"] + w1_)
                    for c in range(16):
                        t_.matmul(p_h[:], xfT[:, c * 128:(c + 1) * 128],
                                  wa_sb[:, c * 520:c * 520 + 512],
                                  start=(c == 0), stop=(c == 15), skip_group_check=True)
                    for c in range(16):
                        ins = t_.matmul(p_lg[:], xfT[:, c * 128:(c + 1) * 128],
                                        wa_sb[:, c * 520 + 512:(c + 1) * 520],
                                        start=(c == 0), stop=(c == 15), skip_group_check=True)
                        if c == 15:
                            S["t16"].inc(ins)
                if 0 <= t - 2 < NT4:
                    u = t - 2
                    t_.wait_ge(S["dmc"].sem, base["dmc"] + 64 + 64 * (u + 1))
                    if u >= 1:
                        t_.wait_ge(S["t21"].sem, base["t21"] + u)
                    if u >= 1:
                        t_.wait_ge(S["u5"].sem, base["u5"] + u)
                    for rc in range(4):
                        hc = hmT[:, rc * 128:(rc + 1) * 128]
                        for sx in range(4):
                            t_.matmul(
                                p_out[:, sx * 512:(sx + 1) * 512], hc,
                                bc_sb[:, rc * 2048 + sx * 512:rc * 2048 + (sx + 1) * 512],
                                start=(rc == 0), stop=False, skip_group_check=True)
                    for sx in range(4):
                        for hh in range(2):
                            ins = t_.matmul(
                                p_out[:, sx * 512 + hh * 256:sx * 512 + (hh + 1) * 256],
                                maskA[0:1, 0:128], maskA[0:1, 0:256],
                                start=False, stop=True, skip_group_check=True)
                            if sx == 3 and hh == 1:
                                S["t20"].inc(ins)



def make_in_maps(L0_attn_hidden_states, L0_attn_W_qkv, L0_attn_lambda_q1,
                 L0_attn_lambda_k1, L0_attn_lambda_q2, L0_attn_lambda_k2,
                 L0_attn_W_g, L0_ffn_A, L0_ffn_B, L0_ffn_Wr, L0_ffn_br):
    x = np.asarray(L0_attn_hidden_states)
    wqkv = np.asarray(L0_attn_W_qkv)
    wg = np.asarray(L0_attn_W_g)
    A = np.asarray(L0_ffn_A)
    B = np.asarray(L0_ffn_B)
    Wr = np.asarray(L0_ffn_Wr)
    br = np.asarray(L0_ffn_br)
    lq1, lk1 = np.asarray(L0_attn_lambda_q1), np.asarray(L0_attn_lambda_k1)
    lq2, lk2 = np.asarray(L0_attn_lambda_q2), np.asarray(L0_attn_lambda_k2)

    wa = np.concatenate([A.transpose(2, 0, 1).reshape(D, 512),
                         Wr.T], axis=1).astype(np.float32)
    bcat = np.ascontiguousarray(B.transpose(0, 2, 1).reshape(512, D), dtype=np.float32)

    in_maps = []
    for c in range(8):
        b, pg = c // 4, c % 4
        blocks = []
        for i in range(4):
            h = 4 * pg + i
            blocks.append(wqkv[:, h * 384:h * 384 + 128])
            blocks.append(wqkv[:, h * 384 + 128:h * 384 + 256])
        for i in range(4):
            h = 4 * pg + i
            blocks.append(wqkv[:, h * 384 + 256:h * 384 + 384])
        blocks.append(wg[:, pg * 512:(pg + 1) * 512])
        wproj = np.ascontiguousarray(np.concatenate(blocks, axis=1), dtype=np.float32)
        prs = [2 * pg, 2 * pg + 1]
        lam4 = np.ascontiguousarray(
            np.stack([lq1[prs], lk1[prs], lq2[prs], lk2[prs]]), dtype=np.float32)
        in_maps.append({
            "x": np.ascontiguousarray(x[b], dtype=np.float32),
            "wproj": wproj,
            "wa": np.ascontiguousarray(wa),
            "bcat": bcat,
            "br": np.ascontiguousarray(br.reshape(1, 8), dtype=np.float32),
            "lam4": lam4,
        })
    return in_maps


def kernel(L0_attn_hidden_states, L0_attn_W_qkv, L0_attn_lambda_q1,
           L0_attn_lambda_k1, L0_attn_lambda_q2, L0_attn_lambda_k2,
           L0_attn_W_g, L0_ffn_A, L0_ffn_B, L0_ffn_Wr, L0_ffn_br):
    nc, _ = build_program()
    in_maps = make_in_maps(L0_attn_hidden_states, L0_attn_W_qkv, L0_attn_lambda_q1,
                           L0_attn_lambda_k1, L0_attn_lambda_q2, L0_attn_lambda_k2,
                           L0_attn_W_g, L0_ffn_A, L0_ffn_B, L0_ffn_Wr, L0_ffn_br)
    res = run_bass_kernel_spmd(nc, in_maps, core_ids=list(range(8)))
    out = np.empty((2, T, D), np.float32)
    for c in range(8):
        b, q = c // 4, c % 4
        out[b, q * 2048:(q + 1) * 2048] = res.results[c]["y_out"]
    return out


# version 19
# speedup vs baseline: 1.1746x; 1.0425x over previous
import math
import numpy as np
from contextlib import ExitStack

import concourse.bass as bass
import concourse.mybir as mybir
from concourse.bass_utils import run_bass_kernel_spmd

F32 = mybir.dt.float32
BF16 = mybir.dt.bfloat16
AF = mybir.ActivationFunctionType
ALU = mybir.AluOpType
AX = mybir.AxisListType
ET = mybir.EngineType

T = 8192
D = 2048
NT = T // 128
VD = 256
EPS_RMS = 1e-6
EPS_LN = 1e-5
LAMBDA_INIT = 0.8 - 0.6 * math.exp(-0.3 * 6)
OUT_GAIN = 1.0 - LAMBDA_INIT
SCALE = 128.0 ** -0.5
LN104_64 = math.log(10000.0) / 64.0
QBLK = 256
NQB = T // QBLK


def set_dims(t):
    global T, NT, NQB
    T = t
    NT = t // 128
    NQB = t // QBLK


class Ctr:
    def __init__(self, sem):
        self.sem = sem
        self.n = 0

    def inc(self, instr, k=1):
        instr.then_inc(self.sem, k)
        self.n += k
        return instr


class SerialEng:

    def __init__(self, eng, ctr):
        self._e = eng
        self._c = ctr

    def __getattr__(self, name):
        f = getattr(self._e, name)
        if name in ("wait_ge", "nop", "drain", "sem", "then_inc"):
            return f

        def wrapped(*a, **k):
            ins = f(*a, **k)
            self._c.inc(ins)
            self._e.wait_ge(self._c.sem, self._c.n)
            return ins
        return wrapped


def build_program(debug=False, phases=4, skip=(), mock_cc=False):
    nc = bass.Bass("TRN2", target_bir_lowering=False)

    x_d = nc.dram_tensor("x", [T, D], F32, kind="ExternalInput").ap()
    wproj_d = nc.dram_tensor("wproj", [D, 2048], F32, kind="ExternalInput").ap()
    wa_d = nc.dram_tensor("wa", [D, 520], F32, kind="ExternalInput").ap()
    bcat_d = nc.dram_tensor("bcat", [512, D], F32, kind="ExternalInput").ap()
    br_d = nc.dram_tensor("br", [1, 8], F32, kind="ExternalInput").ap()
    lam_d = nc.dram_tensor("lam4", [4, 2, 64], F32, kind="ExternalInput").ap()
    y_out = nc.dram_tensor("y_out", [T // 4, D], F32, kind="ExternalOutput").ap()

    qk_t = nc.dram_tensor("qk_sc", [2, 4, 128, T], BF16)
    v_t = nc.dram_tensor("v_sc", [2, NT, 128, VD], BF16)
    g_t = nc.dram_tensor("g_sc", [2, NT, 128, VD], BF16)
    y_t = nc.dram_tensor("y_sc", [T, 512], BF16)
    cc_t = nc.dram_tensor("cc_sc", [4 * T, 512], BF16)
    qk_d, v_d, g_d, y_d = qk_t.ap(), v_t.ap(), g_t.ap(), y_t.ap()

    dbg = {}
    if debug:
        for nm, shp, dt in [("d_xn", [128, D], BF16), ("d_xnT", [128, D], BF16),
                            ("d_qn", [128, 1024], BF16), ("d_qr", [128, 1024], BF16),
                            ("d_st", [128, 8], F32), ("d_s8", [128, 24], F32),
                            ("d_vgs", [128, 1024], BF16)]:
            dbg[nm] = nc.dram_tensor(nm, shp, dt, kind="ExternalOutput").ap()
        dbg["d_qk"] = nc.dram_tensor("d_qk", [2, 4, 128, T], BF16, kind="ExternalOutput").ap()
        dbg["d_v"] = nc.dram_tensor("d_v", [2, NT, 128, VD], BF16, kind="ExternalOutput").ap()
        dbg["d_g"] = nc.dram_tensor("d_g", [2, NT, 128, VD], BF16, kind="ExternalOutput").ap()
        dbg["d_y"] = nc.dram_tensor("d_y", [T, 512], BF16, kind="ExternalOutput").ap()
        dbg["d_cc"] = nc.dram_tensor("d_cc", [4 * T, 512], BF16, kind="ExternalOutput").ap()
        dbg["d_p4x1"] = nc.dram_tensor("d_p4x1", [128, D], F32, kind="ExternalOutput").ap()
        dbg["d_p4xf"] = nc.dram_tensor("d_p4xf", [128, D], BF16, kind="ExternalOutput").ap()
        dbg["d_p4hm"] = nc.dram_tensor("d_p4hm", [128, 512], BF16, kind="ExternalOutput").ap()
        dbg["d_p4rr"] = nc.dram_tensor("d_p4rr", [128, 32], F32, kind="ExternalOutput").ap()
        dbg["d_p4br"] = nc.dram_tensor("d_p4br", [128, 8], F32, kind="ExternalOutput").ap()
        dbg["d_p4xfT"] = nc.dram_tensor("d_p4xfT", [128, D], BF16, kind="ExternalOutput").ap()
        dbg["d_p4wa"] = nc.dram_tensor("d_p4wa", [128, 1040], BF16, kind="ExternalOutput").ap()
        dbg["d_p4yt"] = nc.dram_tensor("d_p4yt", [128, D], BF16, kind="ExternalOutput").ap()

    est = ExitStack()
    with est:
        ident = est.enter_context(nc.sbuf_tensor("ident", [128, 128], BF16))
        consts = est.enter_context(nc.sbuf_tensor("consts", [128, 8], F32))
        maskA = est.enter_context(nc.sbuf_tensor("maskA", [128, 256], BF16))
        ones_col = est.enter_context(nc.sbuf_tensor("ones_col", [128, 1], BF16))
        lam_b = est.enter_context(nc.sbuf_tensor("lam_b", [128, 2], F32))
        br_b = est.enter_context(nc.sbuf_tensor("br_b", [128, 8], F32))
        wa_sb = est.enter_context(nc.sbuf_tensor("wa_sb", [128, 16 * 520], BF16))
        bc_sb = est.enter_context(nc.sbuf_tensor("bc_sb", [128, 4 * 2048], BF16))

        S = {}
        for nm in ["dma", "dmg", "dmh", "dmi", "dmx", "dmc", "dmq", "u4", "u5", "rb", "rbp", "rbs", "u1", "u2", "u3", "t1", "t2", "t3", "t4", "t5", "t6", "t7", "t8",
                   "t9", "t10", "t11", "t12", "t13", "t14", "t15", "t16", "t17",
                   "t18", "t19", "t20", "t21", "t22"]:
            S[nm] = Ctr(est.enter_context(nc.semaphore(nm)))

        if "p0a" not in skip:
            _p0a(nc, S, ident, maskA, ones_col, consts)
        cs_stack = ExitStack()
        cos_tm = cs_stack.enter_context(nc.sbuf_tensor("cos_tm", [128, NT * 64], BF16))
        sin_tm = cs_stack.enter_context(nc.sbuf_tensor("sin_tm", [128, NT * 64], BF16))
        if "p0b" not in skip:
            _p0b(nc, S, ident, cos_tm, sin_tm, consts, skip)
        wproj = cs_stack.enter_context(nc.sbuf_tensor("wproj_sb", [128, 16 * 2048], BF16))
        if "p0c" not in skip:
            _p0c(nc, S, dict(wproj_d=wproj_d, wa_d=wa_d, bcat_d=bcat_d, br_d=br_d,
                         lam_d=lam_d, wproj=wproj, wa_sb=wa_sb, bc_sb=bc_sb,
                         lam_b=lam_b, br_b=br_b))
        if phases >= 1:
            _p1(nc, S, dict(x_d=x_d, qk_d=qk_d, v_d=v_d, g_d=g_d, wproj=wproj,
                            cos_tm=cos_tm, sin_tm=sin_tm, ident=ident, consts=consts,
                            dbg=dbg if debug else None))
        cs_stack.close()
        xq_done = None
        if phases >= 4:
            ccq_t = nc.dram_tensor("ccq_sc", [4, T // 4, 512], BF16)
            xq_t = nc.dram_tensor("xq_sc", [T // 4, D], F32)
            with nc.Block() as blk:
                @blk.sync
                def _(sync0):
                    pid0 = nc.partition_id([ET.SP])
                    qoff0 = nc.snap((pid0 % 4) * (T // 4), min_val=0,
                                    max_val=3 * (T // 4))
                    S["dmq"].inc(sync0.dma_start(
                        xq_t.ap(), x_d[bass.ds(qoff0, T // 4), :]), 16)
            xq_done = S["dmq"].n
        if phases >= 2:
            _p2(nc, S, dict(qk_d=qk_d, v_d=v_d, g_d=g_d, y_d=y_d, maskA=maskA,
                            ones_col=ones_col, lam_b=lam_b, consts=consts))
        if phases >= 3:
            with nc.Block() as blk:
                @blk.gpsimd
                def _(g):
                    if mock_cc:
                        for rk in range(4):
                            S["t12"].inc(g.dma_start(
                                cc_t.ap()[rk * T:(rk + 1) * T, :], y_d), 16)
                        g.wait_ge(S["t12"].sem, S["t12"].n)
                    else:
                        S["t12"].inc(g.collective_compute(
                            "AllGather", ALU.bypass,
                            replica_groups=[[0, 1, 2, 3], [4, 5, 6, 7]],
                            ins=[y_t.ap().opt()], outs=[cc_t.ap().opt()]))
                        g.wait_ge(S["t12"].sem, S["t12"].n)
        if phases >= 4:
            _p4(nc, S, dict(x_d=x_d, cc_d=cc_t.ap(), y_out=y_out, wa_sb=wa_sb,
                            bc_sb=bc_sb, br_b=br_b, ident=ident, consts=consts,
                            maskA=maskA, ccq=ccq_t.ap(), xq=xq_t.ap(),
                            xq_done=xq_done, dbg=dbg if debug else None))

        if debug:
            with nc.Block() as blk:
                @blk.sync
                def _(sync):
                    S["dma"].inc(sync.dma_start(dbg["d_qk"], qk_d), 16)

                    S["dma"].inc(sync.dma_start(dbg["d_v"], v_d), 16)
                    S["dma"].inc(sync.dma_start(dbg["d_g"], g_d), 16)
                    if phases >= 2:
                        S["dma"].inc(sync.dma_start(dbg["d_y"], y_d), 16)
                    if phases >= 3:
                        S["dma"].inc(sync.dma_start(dbg["d_cc"], cc_t.ap()), 16)
                    sync.wait_ge(S["dma"].sem, S["dma"].n)

        if phases < 4:
            with nc.Block() as blk, ExitStack() as tst:
                zz = tst.enter_context(nc.sbuf_tensor("zz", [128, D], F32))

                @blk.gpsimd
                def _(g):
                    S["t13"].inc(g.memset(zz[:], 0.0))

                @blk.sync
                def _(sync):
                    sync.wait_ge(S["t13"].sem, S["t13"].n)
                    for i in range(T // 512):
                        S["dma"].inc(sync.dma_start(y_out[i * 128:(i + 1) * 128, :], zz[:]), 16)
                    sync.wait_ge(S["dma"].sem, S["dma"].n)

    return nc, dbg


def _p0a(nc, S, ident, maskA, ones_col, consts):
    with nc.Block() as blk:
        @blk.gpsimd
        def _(g):
            g = SerialEng(g, S["rbp"])
            g.memset(ident[:], 0.0)
            g.affine_select(out=ident[:], in_=ident[:], compare_op=ALU.not_equal,
                            fill=1.0, base=0, pattern=[[-1, 128]], channel_multiplier=1)
            g.memset(maskA[:], 1.0)
            g.affine_select(out=maskA[:], in_=maskA[:], compare_op=ALU.is_ge,
                            fill=0.0, base=0, pattern=[[1, 256]], channel_multiplier=-1)
            g.memset(ones_col[:], 1.0)
            g.memset(consts[:, 0:1], EPS_LN)
            g.memset(consts[:, 1:2], EPS_RMS)
            g.memset(consts[:, 2:3], math.pi / 2)
            g.memset(consts[:, 3:4], math.log(OUT_GAIN))
            g.memset(consts[:, 4:5], 1.0)


def _p0b(nc, S, ident, cos_tm, sin_tm, consts, skip=()):
    with nc.Block() as blk, ExitStack() as tst:
        p_tp2 = [tst.enter_context(nc.psum_tensor(f"p_tp0{i}", [128, 1024], BF16))
                 for i in range(2)]
        iot64 = tst.enter_context(nc.sbuf_tensor("iot64", [64, 1], F32))
        invf = tst.enter_context(nc.sbuf_tensor("invf", [64, 1], F32))
        cn = tst.enter_context(nc.sbuf_tensor("cn", [64, 1], F32))
        sn = tst.enter_context(nc.sbuf_tensor("sn", [64, 1], F32))
        nsn = tst.enter_context(nc.sbuf_tensor("nsn", [64, 1], F32))
        tsc = tst.enter_context(nc.sbuf_tensor("tsc", [64, 2], F32))
        kvec = tst.enter_context(nc.sbuf_tensor("kvec", [64, 1], F32))
        stp = tst.enter_context(nc.sbuf_tensor("stp", [64, 1], F32))
        onec = tst.enter_context(nc.sbuf_tensor("onec", [64, 1], F32))
        car = tst.enter_context(nc.sbuf_tensor("car", [64, 64], F32))
        sar = tst.enter_context(nc.sbuf_tensor("sar", [64, 64], F32))
        caa = tst.enter_context(nc.sbuf_tensor("caa", [64, T // 64], F32))
        saa = tst.enter_context(nc.sbuf_tensor("saa", [64, T // 64], F32))
        tmp8 = tst.enter_context(nc.sbuf_tensor("tmp8", [64, T], F32))
        cbf = tst.enter_context(nc.sbuf_tensor("cbf", [128, T], BF16))
        sbf = tst.enter_context(nc.sbuf_tensor("sbf", [128, T], BF16))
        cf32 = tst.enter_context(nc.sbuf_tensor("cf32", [64, T], F32))
        sf32 = tst.enter_context(nc.sbuf_tensor("sf32", [64, T], F32))

        @blk.gpsimd
        def _(g):
            g = SerialEng(g, S["rbp"])
            g.iota(iot64[:], [[0, 1]], channel_multiplier=1,
                   allow_small_or_imprecise_dtypes=True)
            g.memset(cbf[:], 0.0)
            g.memset(sbf[:], 0.0)
            g.memset(kvec[:], 0.0)
            g.memset(onec[:], 1.0)
            kjs = [int(64.0 * 10000.0 ** (-j / 64.0) / (2 * math.pi)) for j in range(64)]
            m = 1
            while any(k >= m for k in kjs):
                jmax = max(j for j in range(64) if kjs[j] >= m)
                g.affine_select(out=stp[:], in_=onec[:], compare_op=ALU.is_ge,
                                fill=0.0, base=jmax, pattern=[[0, 1]],
                                channel_multiplier=-1)
                g.tensor_add(kvec[:], kvec[:], stp[:])
                m += 1
            gi = g.memset(nsn[:], 0.0)
            S["t1"].inc(g._e.memset(tsc[:, 0:1], 0.0))

        @blk.scalar
        def _(s):
            s.wait_ge(S["t1"].sem, S["t1"].n)
            S["t2"].inc(s.activation(invf[:], iot64[:], AF.Exp, scale=-LN104_64))

        def taylor_sincos(v, phi, out_c, out_s, u, hi):
            v.tensor_mul(u, phi, phi)
            v.memset(out_s, -1.0 / 121645100408832000.0)
            for ck in [1.0 / 355687428096000.0, -1.0 / 1307674368000.0,
                       1.0 / 6227020800.0, -1.0 / 39916800.0, 1.0 / 362880.0,
                       -1.0 / 5040.0, 1.0 / 120.0, -1.0 / 6.0, 1.0]:
                v.tensor_mul(out_s, out_s, u)
                v.tensor_scalar_add(out_s, out_s, ck)
            v.tensor_mul(out_s, out_s, phi)
            v.memset(out_c, 1.0 / 6402373705728000.0)
            for ck in [-1.0 / 87178291200.0, 1.0 / 479001600.0, -1.0 / 3628800.0,
                       1.0 / 40320.0, -1.0 / 720.0, 1.0 / 24.0, -1.0 / 2.0, 1.0]:
                v.tensor_mul(out_c, out_c, u)
                v.tensor_scalar_add(out_c, out_c, ck)
            _ = hi

        @blk.vector
        def _(v):
            vraw = v
            v = SerialEng(v, S["rb"])
            vraw.wait_ge(S["t2"].sem, S["t2"].n)
            NA = T // 64
            u = tsc[:, 0:1]
            taylor_sincos(v, invf[:], cn[:], sn[:], u, None)
            v.tensor_scalar_mul(nsn[:], sn[:], -1.0)
            v.memset(car[:, 0:1], 1.0)
            v.memset(sar[:, 0:1], 0.0)
            for r in range(63):
                v.tensor_scalar_mul(car[:, r + 1:r + 2], car[:, r:r + 1], cn[:])
                v.scalar_tensor_tensor(car[:, r + 1:r + 2], sar[:, r:r + 1], nsn[:],
                                       car[:, r + 1:r + 2], ALU.mult, ALU.add)
                v.tensor_scalar_mul(sar[:, r + 1:r + 2], sar[:, r:r + 1], cn[:])
                v.scalar_tensor_tensor(sar[:, r + 1:r + 2], car[:, r:r + 1], sn[:],
                                       sar[:, r + 1:r + 2], ALU.mult, ALU.add)
            v.tensor_scalar(tsc[:, 1:2], kvec[:], -2.0 * math.pi, -math.pi,
                            ALU.mult, ALU.add)
            v.scalar_tensor_tensor(tsc[:, 1:2], invf[:], 64.0, tsc[:, 1:2],
                                   ALU.mult, ALU.add)
            taylor_sincos(v, tsc[:, 1:2], cn[:], sn[:], u, None)
            v.tensor_scalar_mul(cn[:], cn[:], -1.0)
            v.tensor_scalar_mul(nsn[:], sn[:], 1.0)
            v.tensor_scalar_mul(sn[:], sn[:], -1.0)
            v.memset(caa[:, 0:1], 1.0)
            v.memset(saa[:, 0:1], 0.0)
            for a in range(NA - 1):
                v.tensor_scalar_mul(caa[:, a + 1:a + 2], caa[:, a:a + 1], cn[:])
                v.scalar_tensor_tensor(caa[:, a + 1:a + 2], saa[:, a:a + 1], nsn[:],
                                       caa[:, a + 1:a + 2], ALU.mult, ALU.add)
                v.tensor_scalar_mul(saa[:, a + 1:a + 2], saa[:, a:a + 1], cn[:])
                v.scalar_tensor_tensor(saa[:, a + 1:a + 2], caa[:, a:a + 1], sn[:],
                                       saa[:, a + 1:a + 2], ALU.mult, ALU.add)
            cavw = caa[:].rearrange("p (a o) -> p a o", o=1).broadcast_to([64, NA, 64])
            savw = saa[:].rearrange("p (a o) -> p a o", o=1).broadcast_to([64, NA, 64])
            crvw = car[:].rearrange("p (o r) -> p o r", o=1).broadcast_to([64, NA, 64])
            srvw = sar[:].rearrange("p (o r) -> p o r", o=1).broadcast_to([64, NA, 64])
            cfv = cf32[:].rearrange("p (a r) -> p a r", r=64)
            sfv = sf32[:].rearrange("p (a r) -> p a r", r=64)
            v.tensor_mul(cfv, cavw, crvw)
            v.tensor_mul(tmp8[:].rearrange("p (a r) -> p a r", r=64), savw, srvw)
            v.tensor_sub(cf32[:], cf32[:], tmp8[:])
            v.tensor_mul(sfv, savw, crvw)
            v.tensor_mul(tmp8[:].rearrange("p (a r) -> p a r", r=64), cavw, srvw)
            v.tensor_add(sf32[:], sf32[:], tmp8[:])
            v.tensor_copy(cbf[0:64, :], cf32[:])
            v.tensor_copy(sbf[0:64, :], sf32[:])
            S["t3"].inc(vraw.tensor_copy(tsc[:, 0:1], tsc[:, 0:1]))

        if "p0b_tp" not in skip:
            @blk.tensor
            def _(t):
                t.wait_ge(S["t3"].sem, S["t3"].n)
                for tt in range(NT):
                    pt = p_tp2[tt % 2]
                    if tt >= 2:
                        t.wait_ge(S["t4"].sem, tt - 1)
                    t.transpose(pt[:, 0:128], cbf[:, tt * 128:(tt + 1) * 128], ident[:])
                    S["t5"].inc(t.transpose(pt[:, 128:256],
                                            sbf[:, tt * 128:(tt + 1) * 128], ident[:]))

            @blk.vector
            def _(v):
                for tt in range(NT):
                    pt = p_tp2[tt % 2]
                    v.wait_ge(S["t5"].sem, tt + 1)
                    v.tensor_copy(cos_tm[:, tt * 64:(tt + 1) * 64], pt[:, 0:64])
                    S["t4"].inc(v.tensor_copy(sin_tm[:, tt * 64:(tt + 1) * 64],
                                              pt[:, 128:192]))


def _p0c(nc, S, env):
    wproj_d, wa_d, bcat_d = env["wproj_d"], env["wa_d"], env["bcat_d"]
    br_d, lam_d = env["br_d"], env["lam_d"]
    wproj, wa_sb, bc_sb = env["wproj"], env["wa_sb"], env["bc_sb"]
    lam_b, br_b = env["lam_b"], env["br_b"]
    d0, w6 = S["dma"].n, S["t6"].n

    with nc.Block() as blk, ExitStack() as tst:
        p_bc = tst.enter_context(nc.psum_tensor("p_bc", [128, 512], F32))
        wstg = [tst.enter_context(nc.sbuf_tensor(f"wstg{i}", [128, 16 * 520], F32))
                for i in range(2)]
        lamq = tst.enter_context(nc.sbuf_tensor("lamq", [2, 4 * 64], F32))
        lamc = tst.enter_context(nc.sbuf_tensor("lamc", [2, 8], F32))
        brs = tst.enter_context(nc.sbuf_tensor("brs", [1, 8], F32))
        onerow = tst.enter_context(nc.sbuf_tensor("onerow", [2, 128], F32))

        d1 = S["dmh"].n

        @blk.sync
        def _(sync):
            for gg in range(4):
                pn = "dma" if gg % 2 == 0 else "dmh"
                pb = d0 if gg % 2 == 0 else d1
                if gg >= 2:
                    sync.wait_ge(S["t6"].sem, w6 + gg - 1)
                    sync.wait_ge(S[pn].sem, pb + 16 * (gg // 2))
                S[pn].inc(sync.dma_start(
                    wstg[gg % 2][:, 0:8192].rearrange("p (c f) -> p c f", c=4),
                    wproj_d.rearrange("(c p) f -> p c f", p=128)[:, gg * 4:(gg + 1) * 4, :]), 16)
            sync.wait_ge(S["t6"].sem, w6 + 4)
            sync.wait_ge(S["dma"].sem, d0 + 32)
            S["dma"].inc(sync.dma_start(
                wstg[0][:, 0:16 * 520].rearrange("p (c f) -> p c f", c=16),
                wa_d.rearrange("(c p) f -> p c f", p=128)), 16)
            sync.wait_ge(S["dmh"].sem, d1 + 32)
            S["dmh"].inc(sync.dma_start(
                wstg[1][:, 0:8192].rearrange("p (c f) -> p c f", c=4),
                bcat_d.rearrange("(c p) f -> p c f", p=128)), 16)
            sync.wait_ge(S["dma"].sem, d0 + 48)
            S["dma"].inc(sync.dma_start(brs[:], br_d), 16)
            sync.wait_ge(S["dmh"].sem, d1 + 48)
            S["dmh"].inc(sync.dma_start(
                lamq[:].rearrange("p (c f) -> p c f", c=4),
                lam_d.rearrange("c p f -> p c f")), 16)

        @blk.vector
        def _(v):
            for gg in range(4):
                pn = "dma" if gg % 2 == 0 else "dmh"
                pb = d0 if gg % 2 == 0 else d1
                v.wait_ge(S[pn].sem, pb + 16 * (gg // 2 + 1))
                S["t6"].inc(v.tensor_copy(
                    wproj[:, gg * 4 * 2048:(gg + 1) * 4 * 2048], wstg[gg % 2][:, 0:8192]))
            v.wait_ge(S["dma"].sem, d0 + 48)
            v.wait_ge(S["dmh"].sem, d1 + 48)
            v.tensor_copy(wa_sb[:], wstg[0][:, 0:16 * 520])
            v.tensor_copy(bc_sb[:], wstg[1][:, 0:8192])
            v.wait_ge(S["dma"].sem, d0 + 64)
            v.wait_ge(S["dmh"].sem, d1 + 64)
            sv = SerialEng(v, S["rb"])
            sv.tensor_mul(lamq[:, 0:64], lamq[:, 0:64], lamq[:, 64:128])
            sv.tensor_mul(lamq[:, 128:192], lamq[:, 128:192], lamq[:, 192:256])
            sv.reduce_sum(lamc[:, 0:1], lamq[:, 0:64], axis=AX.X)
            sv.reduce_sum(lamc[:, 1:2], lamq[:, 128:192], axis=AX.X)
            S["t7"].inc(v.tensor_copy(lamc[:, 0:2], lamc[:, 0:2]))

        @blk.scalar
        def _(s):
            s.wait_ge(S["t7"].sem, S["t7"].n)
            ss = SerialEng(s, S["rbs"])
            ss.activation(lamc[:, 2:3], lamc[:, 0:1], AF.Exp)
            ss.activation(lamc[:, 3:4], lamc[:, 1:2], AF.Exp)
            S["t8"].inc(s.activation(lamc[:, 2:4], lamc[:, 2:4], AF.Identity))

        @blk.gpsimd
        def _(g):
            graw = g
            g = SerialEng(g, S["rbp"])
            g.memset(onerow[:], 1.0)
            graw.wait_ge(S["t8"].sem, S["t8"].n)
            g.tensor_sub(lamc[:, 4:5], lamc[:, 2:3], lamc[:, 3:4])
            g.tensor_scalar_add(lamc[:, 4:5], lamc[:, 4:5], LAMBDA_INIT)
            g.tensor_copy(lamc[:, 6:8], lamc[:, 4:5].broadcast_to([2, 2]))
            g.affine_select(
                out=lamc[:, 6:8], in_=lamc[:, 6:8], compare_op=ALU.is_equal,
                fill=0.0, base=0, pattern=[[-1, 2]], channel_multiplier=1)
            S["t9"].inc(graw.tensor_copy(lamc[:, 6:8], lamc[:, 6:8]))

        @blk.tensor
        def _(t):
            t.wait_ge(S["t9"].sem, S["t9"].n)
            t.matmul(p_bc[:, 0:2], onerow[:], lamc[:, 6:8], start=True, stop=True)
            S["t10"].inc(t.matmul(p_bc[:, 8:16], onerow[0:1, :], brs[:],
                                  start=True, stop=True))

        @blk.vector
        def _(v):
            v.wait_ge(S["t10"].sem, S["t10"].n)
            v.tensor_copy(lam_b[:], p_bc[:, 0:2])
            S["t11"].inc(v.tensor_copy(br_b[:], p_bc[:, 8:16]))


def _p1(nc, S, env):
    x_d, qk_d, v_d, g_d = env["x_d"], env["qk_d"], env["v_d"], env["g_d"]
    wproj, cos_tm, sin_tm, ident = env["wproj"], env["cos_tm"], env["sin_tm"], env["ident"]
    consts = env["consts"]
    dbg_aps = env.get("dbg")

    with nc.Block() as blk, ExitStack() as tst:
        p_tpA = tst.enter_context(nc.psum_tensor("p1_tpA", [128, 1024], BF16))
        p_tpB = tst.enter_context(nc.psum_tensor("p1_tpB", [128, 1024], BF16))
        p_qk = tst.enter_context(nc.psum_tensor("p1_qk", [128, 1024], BF16))
        p_prj = [tst.enter_context(nc.psum_tensor(f"p1_prj{i}", [128, 512], F32))
                 for i in range(4)]
        xf32 = [tst.enter_context(nc.sbuf_tensor(f"p1_x{i}", [128, D], F32)) for i in range(3)]
        xn = [tst.enter_context(nc.sbuf_tensor(f"p1_xn{i}", [128, D], BF16)) for i in range(2)]
        xnT = [tst.enter_context(nc.sbuf_tensor(f"p1_xnT{i}", [128, D], BF16)) for i in range(2)]
        junk = tst.enter_context(nc.sbuf_tensor("p1_junk", [128, D], BF16))
        stats = [tst.enter_context(nc.sbuf_tensor(f"p1_st{i}", [128, 8], F32)) for i in range(2)]
        q2 = [tst.enter_context(nc.sbuf_tensor(f"p1_q2{i}", [128, 1024], F32)) for i in range(2)]
        s8 = [tst.enter_context(nc.sbuf_tensor(f"p1_s8{i}", [128, 24], F32)) for i in range(2)]
        qn = [tst.enter_context(nc.sbuf_tensor(f"p1_qn{i}", [128, 1024], BF16)) for i in range(2)]
        qrot = [tst.enter_context(nc.sbuf_tensor(f"p1_qr{i}", [128, 1024], BF16)) for i in range(2)]
        rsc = tst.enter_context(nc.sbuf_tensor("p1_rsc", [128, 1024], BF16))
        qkTs = [tst.enter_context(nc.sbuf_tensor(f"p1_qkT{i}", [128, 1024], BF16)) for i in range(2)]
        vgs = [tst.enter_context(nc.sbuf_tensor(f"p1_vg{i}", [128, 1024], BF16)) for i in range(2)]
        qc = tst.enter_context(nc.sbuf_tensor("p1_qc", [128, 1024], F32))
        gtmp = tst.enter_context(nc.sbuf_tensor("p1_gt", [128, 512], F32))
        esb = tst.enter_context(nc.sbuf_tensor("p1_es", [128, 512], F32))

        base = {k: S[k].n for k in S}

        def spill(sync, t):
            pr = t % 2
            snm = "dmg" if t % 2 == 0 else "dmi"
            sync.wait_ge(S["t8"].sem, base["t8"] + t + 1)
            S[snm].inc(sync.dma_start(
                qk_d.rearrange("a b p x -> p a b x")[:, :, :, t * 128:(t + 1) * 128],
                qkTs[pr][:].rearrange("p (a b x) -> p a b x", a=2, b=4)), 16)
            sync.wait_ge(S["t5"].sem, base["t5"] + t + 1)
            sync.wait_ge(S["u4"].sem, base["u4"] + t + 1)
            S[snm].inc(sync.dma_start(
                v_d.rearrange("a t p x -> p t a x")[:, t],
                vgs[pr][:, 0:512].rearrange("p (a x) -> p a x", a=2)), 16)
            S[snm].inc(sync.dma_start(
                g_d.rearrange("a t p x -> p t a x")[:, t],
                vgs[pr][:, 512:1024].rearrange("p (a x) -> p a x", a=2)), 16)

        @blk.sync
        def _(sync):
            for t in range(NT + 3):
                if t < NT:
                    if t >= 3:
                        sync.wait_ge(S["t15"].sem, base["t15"] + t - 2)
                        sync.wait_ge(S["t14"].sem, base["t14"] + t - 2)
                    S[["dma", "dmh", "dmx"][t % 3]].inc(
                        sync.dma_start(xf32[t % 3][:], x_d[t * 128:(t + 1) * 128, :]), 16)
                if t >= 3:
                    spill(sync, t - 3)
            sync.wait_ge(S["dmg"].sem, S["dmg"].n)
            sync.wait_ge(S["dmi"].sem, S["dmi"].n)
            if dbg_aps:
                pr_l = (NT - 1) % 2
                for nm, tens in [("d_xn", xn[pr_l]), ("d_xnT", xnT[pr_l]),
                                 ("d_qn", qn[pr_l]), ("d_qr", qrot[pr_l]),
                                 ("d_st", stats[pr_l]), ("d_s8", s8[pr_l]),
                                 ("d_vgs", vgs[pr_l])]:
                    S["dmg"].inc(sync.dma_start(dbg_aps[nm], tens[:]), 16)
            sync.wait_ge(S["dmg"].sem, S["dmg"].n)

        @blk.vector
        def _(v):
            for t in range(NT + 2):
                if t < NT:
                    pr3 = t % 3
                    st = stats[t % 2]
                    lnm = ["dma", "dmh", "dmx"][t % 3]
                    v.wait_ge(S[lnm].sem, base[lnm] + 16 * (t // 3 + 1))
                    S["rb"].inc(v.reduce_sum(st[:, 0:1], xf32[pr3][:], axis=AX.X))
                    v.wait_ge(S["rb"].sem, S["rb"].n)
                    S["t14"].inc(v.tensor_scalar_mul(st[:, 2:3], st[:, 0:1], 1.0 / D))
                    v.wait_ge(S["t14"].sem, base["t14"] + t + 1)
                    v.wait_ge(S["t16"].sem, base["t16"] + t + 1)
                    S["rb"].inc(v.tensor_mul(st[:, 3:4], st[:, 2:3], st[:, 2:3]))
                    v.wait_ge(S["rb"].sem, S["rb"].n)
                    S["t17"].inc(v.scalar_tensor_tensor(st[:, 4:5], st[:, 1:2], 1.0 / D,
                                                        st[:, 3:4], ALU.mult, ALU.subtract))
                    v.wait_ge(S["t18"].sem, base["t18"] + t + 1)
                    S["t19"].inc(v.scalar_tensor_tensor(st[:, 6:7], st[:, 2:3], -1.0,
                                                        st[:, 5:6], ALU.mult, ALU.mult))
                    v.wait_ge(S["t20"].sem, base["t20"] + 2 * t + 1)
                    S["t21"].inc(v.tensor_copy(xnT[t % 2][:, 0:1024], p_tpA[:]))
                    v.wait_ge(S["t20"].sem, base["t20"] + 2 * t + 2)
                    S["t21"].inc(v.tensor_copy(xnT[t % 2][:, 1024:2048], p_tpB[:]))
                if 0 <= t - 1 < NT:
                    u = t - 1
                    pu = u % 2
                    v.wait_ge(S["t1"].sem, base["t1"] + u + 1)
                    S["rb"].inc(v.reduce_sum(
                        s8[pu][:, 0:8], q2[pu][:].rearrange("p (s x) -> p s x", s=8),
                        axis=AX.X))
                    v.wait_ge(S["rb"].sem, S["rb"].n)
                    S["t2"].inc(v.tensor_scalar(s8[pu][:, 8:16], s8[pu][:, 0:8],
                                                1.0 / 128.0, EPS_RMS, ALU.mult, ALU.add))
                    v.wait_ge(S["t3"].sem, base["t3"] + u + 1)
                    v.tensor_tensor(
                        qn[pu][:, 0:512].rearrange("p (s x) -> p s x", s=4),
                        qc[:, 0:512].rearrange("p (s x) -> p s x", s=4),
                        s8[pu][:, 16:20].rearrange("p (s o) -> p s o", o=1
                                                   ).broadcast_to([128, 4, 128]),
                        ALU.mult)
                    S["t4"].inc(v.tensor_tensor(
                        qn[pu][:, 512:1024].rearrange("p (s x) -> p s x", s=4),
                        qc[:, 512:1024].rearrange("p (s x) -> p s x", s=4),
                        s8[pu][:, 20:24].rearrange("p (s o) -> p s o", o=1
                                                   ).broadcast_to([128, 4, 128]),
                        ALU.mult))
                    v.wait_ge(S["t5"].sem, base["t5"] + u + 1)
                    S["rb"].inc(v.tensor_scalar_add(esb[:], esb[:], 1.0))
                    v.wait_ge(S["rb"].sem, S["rb"].n)
                    S["rb"].inc(v.reciprocal(esb[:], esb[:]))
                    v.wait_ge(S["rb"].sem, S["rb"].n)
                    S["u4"].inc(v.tensor_mul(vgs[pu][:, 512:1024], gtmp[:], esb[:]))
                if 0 <= t - 2 < NT:
                    w = t - 2
                    v.wait_ge(S["t7"].sem, base["t7"] + w + 1)
                    if w >= 2:
                        c3 = "dmg" if (w - 2) % 2 == 0 else "dmi"
                        v.wait_ge(S[c3].sem, base[c3] + 48 * ((w - 2) // 2 + 1))
                    S["t8"].inc(v.tensor_copy(qkTs[w % 2][:], p_qk[:]))

        @blk.scalar
        def _(s):
            for t in range(NT + 1):
                if t < NT:
                    pr3 = t % 3
                    st = stats[t % 2]
                    lnm = ["dma", "dmh", "dmx"][t % 3]
                    s.wait_ge(S[lnm].sem, base[lnm] + 16 * (t // 3 + 1))
                    S["t16"].inc(s.activation(junk[:], xf32[pr3][:], AF.Square,
                                              accum_out=st[:, 1:2]))
                    s.wait_ge(S["t17"].sem, base["t17"] + t + 1)
                    S["rbs"].inc(s.activation(st[:, 7:8], st[:, 4:5], AF.Ln,
                                              bias=consts[:, 0:1]))
                    s.wait_ge(S["rbs"].sem, S["rbs"].n)
                    S["t18"].inc(s.activation(st[:, 5:6], st[:, 7:8], AF.Exp, scale=-0.5))
                    s.wait_ge(S["t19"].sem, base["t19"] + t + 1)
                    if t >= 2:
                        s.wait_ge(S["t20"].sem, base["t20"] + 2 * (t - 2) + 2)
                    S["t15"].inc(s.activation(xn[t % 2][:], xf32[pr3][:], AF.Identity,
                                              bias=st[:, 6:7], scale=st[:, 5:6]))
                if 0 <= t - 1 < NT:
                    u = t - 1
                    pu = u % 2
                    s.wait_ge(S["t22"].sem, base["t22"] + u + 1)
                    if u >= 1:
                        s.wait_ge(S["t4"].sem, base["t4"] + u)
                        s.wait_ge(S["u4"].sem, base["u4"] + u)
                    if u >= 2:
                        c2 = "dmg" if u % 2 == 0 else "dmi"
                        s.wait_ge(S[c2].sem, base[c2] + 48 * ((u - 2) // 2 + 1))
                    S["rbs"].inc(s.activation(qc[:, 0:512], p_prj[0][:], AF.Copy))
                    S["rbs"].inc(s.activation(qc[:, 512:1024], p_prj[1][:], AF.Copy))
                    s.activation(vgs[pu][:, 0:512], p_prj[2][:], AF.Copy)
                    s.activation(gtmp[:], p_prj[3][:], AF.Copy)
                    S["t5"].inc(s.activation(esb[:], p_prj[3][:], AF.Exp, scale=-1.0))
                    s.wait_ge(S["rbs"].sem, S["rbs"].n)
                    s.activation(q2[pu][:, 0:512], qc[:, 0:512], AF.Square)
                    S["t1"].inc(s.activation(q2[pu][:, 512:1024], qc[:, 512:1024],
                                             AF.Square))
                    s.wait_ge(S["t2"].sem, base["t2"] + u + 1)
                    S["rbs"].inc(s.activation(s8[pu][:, 16:24], s8[pu][:, 8:16], AF.Ln))
                    s.wait_ge(S["rbs"].sem, S["rbs"].n)
                    S["t3"].inc(s.activation(s8[pu][:, 16:24], s8[pu][:, 16:24], AF.Exp,
                                             scale=-0.5))

        @blk.tensor
        def _(t_):
            for t in range(NT + 2):
                if 0 <= t - 2 < NT:
                    w = t - 2
                    t_.wait_ge(S["t6"].sem, base["t6"] + w + 1)
                    if w >= 1:
                        t_.wait_ge(S["t8"].sem, base["t8"] + w)
                    for i, sg in enumerate((0, 2, 1, 3, 4, 6, 5, 7)):
                        ins = t_.transpose(p_qk[:, i * 128:(i + 1) * 128],
                                           qrot[w % 2][:, sg * 128:(sg + 1) * 128], ident[:])
                        if i == 7:
                            S["t7"].inc(ins)
                if t < NT:
                    t_.wait_ge(S["t15"].sem, base["t15"] + t + 1)
                    if t >= 1:
                        t_.wait_ge(S["t21"].sem, base["t21"] + 2 * (t - 1) + 1)
                    for i in range(8):
                        ins = t_.transpose(p_tpA[:, i * 128:(i + 1) * 128],
                                           xn[t % 2][:, i * 128:(i + 1) * 128], ident[:])
                        if i == 7:
                            S["t20"].inc(ins)
                    if t >= 1:
                        t_.wait_ge(S["t21"].sem, base["t21"] + 2 * (t - 1) + 2)
                    for i in range(8):
                        ins = t_.transpose(p_tpB[:, i * 128:(i + 1) * 128],
                                           xn[t % 2][:, (8 + i) * 128:(9 + i) * 128], ident[:])
                        if i == 7:
                            S["t20"].inc(ins)
                    t_.wait_ge(S["t21"].sem, base["t21"] + 2 * t + 2)
                    if t >= 1:
                        t_.wait_ge(S["t5"].sem, base["t5"] + t)
                    for c in range(16):
                        for sx in range(4):
                            ins = t_.matmul(
                                p_prj[sx][:], xnT[t % 2][:, c * 128:(c + 1) * 128],
                                wproj[:, c * 2048 + sx * 512:c * 2048 + (sx + 1) * 512],
                                start=(c == 0), stop=(c == 15))
                            if c == 15 and sx == 3:
                                S["t22"].inc(ins)

        @blk.gpsimd
        def _(g):
            for t in range(1, NT + 1):
                u = t - 1
                pu = u % 2
                g.wait_ge(S["t4"].sem, base["t4"] + u + 1)
                if u >= 1:
                    g.wait_ge(S["t6"].sem, base["t6"] + u)
                if u >= 2:
                    g.wait_ge(S["t7"].sem, base["t7"] + u - 1)
                qv = qn[pu][:].rearrange("p (s h x) -> p s h x", s=8, h=2)
                rv = qrot[pu][:].rearrange("p (s h x) -> p s h x", s=8, h=2)
                sc = rsc[:, 0:512].rearrange("p (s o x) -> p s o x", s=8, o=1)
                sc2 = rsc[:, 512:1024].rearrange("p (s o x) -> p s o x", s=8, o=1)
                c_t = cos_tm[:, u * 64:(u + 1) * 64].rearrange(
                    "p (a o x) -> p a o x", a=1, o=1).broadcast_to([128, 8, 1, 64])
                s_t = sin_tm[:, u * 64:(u + 1) * 64].rearrange(
                    "p (a o x) -> p a o x", a=1, o=1).broadcast_to([128, 8, 1, 64])
                g.tensor_mul(rv[:, :, 0:1, :], qv[:, :, 0:1, :], c_t)
                g.tensor_mul(sc, qv[:, :, 1:2, :], s_t)
                S["rbp"].inc(g.tensor_mul(sc2, qv[:, :, 0:1, :], s_t))
                g.wait_ge(S["rbp"].sem, S["rbp"].n)
                g.tensor_add(rv[:, :, 0:1, :], rv[:, :, 0:1, :], sc)
                S["rbp"].inc(g.tensor_mul(rv[:, :, 1:2, :], qv[:, :, 1:2, :], c_t))
                g.wait_ge(S["rbp"].sem, S["rbp"].n)
                S["t6"].inc(g.tensor_sub(rv[:, :, 1:2, :], rv[:, :, 1:2, :], sc2))


def _p2(nc, S, env):
    qk_d, v_d, g_d, y_d = env["qk_d"], env["v_d"], env["g_d"], env["y_d"]
    maskA, ones_col, lam_b = env["maskA"], env["ones_col"], env["lam_b"]
    consts = env["consts"]

    with ExitStack() as tst:
        qs_ = [tst.enter_context(nc.sbuf_tensor(f"p2_q{i}", [128, T], BF16)) for i in range(2)]
        ks_ = [tst.enter_context(nc.sbuf_tensor(f"p2_k{i}", [128, T], BF16)) for i in range(2)]
        v_sb = tst.enter_context(nc.sbuf_tensor("p2_v", [128, (T // 128) * VD], BF16))
        g_sb = tst.enter_context(nc.sbuf_tensor("p2_g", [128, (T // 128) * VD], BF16))
        P2b = [tst.enter_context(nc.sbuf_tensor(f"p2_P{i}", [128, 512], BF16)) for i in range(2)]
        eR = tst.enter_context(nc.sbuf_tensor("p2_eR", [128, 8], F32))
        ey = tst.enter_context(nc.sbuf_tensor("p2_ey", [128, 256], F32))
        ey2 = tst.enter_context(nc.sbuf_tensor("p2_ey2", [128, 256], F32))
        eyg = tst.enter_context(nc.sbuf_tensor("p2_eyg", [128, 512], F32))
        esq = tst.enter_context(nc.sbuf_tensor("p2_esq", [128, 256], BF16))
        es = tst.enter_context(nc.sbuf_tensor("p2_es", [128, 8], F32))
        ybf = [tst.enter_context(nc.sbuf_tensor(f"p2_ybf{i}", [128, 256], BF16)) for i in range(2)]

        for pr in range(2):
            with nc.Block() as blk, ExitStack() as pst:
                p_s = [pst.enter_context(nc.psum_tensor(f"p2_s{pr}{i}", [128, 512], F32))
                       for i in range(2)]
                p_o = [pst.enter_context(nc.psum_tensor(f"p2_o{pr}{i}", [128, 512], F32))
                       for i in range(4)]
                p_z = [pst.enter_context(nc.psum_tensor(f"p2_z{pr}{i}", [128, 4], F32))
                       for i in range(2)]
                base = {k: S[k].n for k in S}

                iters = []
                for j in range(NQB):
                    for kt in range(2 * j + 2):
                        kind = "full" if kt < 2 * j else ("diag0" if kt == 2 * j else "half")
                        iters.append((j, kt, kind))
                n_it = len(iters)
                cum_it = {}
                c = 0
                nf = []
                nm = []
                cf = cm = 0
                for (j, kt, kind) in iters:
                    c += 1
                    if kind == "full":
                        cf += 1
                    else:
                        cm += 1
                    nf.append(cf)
                    nm.append(cm)
                    if kt == 2 * j + 1:
                        cum_it[j] = c

                @blk.sync
                def _(sync):
                    S["dma"].inc(sync.dma_start(ks_[0][:], qk_d[pr, 2]), 16)
                    S["dma"].inc(sync.dma_start(ks_[1][:], qk_d[pr, 3]), 16)
                    S["dma"].inc(sync.dma_start(qs_[0][:], qk_d[pr, 0]), 16)
                    S["dma"].inc(sync.dma_start(qs_[1][:], qk_d[pr, 1]), 16)
                    S["dmx"].inc(sync.dma_start(
                        v_sb[:].rearrange("p (t f) -> p t f", f=VD),
                        v_d[pr].rearrange("t p f -> p t f")), 16)
                    S["dmc"].inc(sync.dma_start(
                        g_sb[:].rearrange("p (t f) -> p t f", f=VD),
                        g_d[pr].rearrange("t p f -> p t f")), 16)
                    for j in range(NQB):
                        for qs in range(2):
                            u = 2 * j + qs
                            sync.wait_ge(S["t20"].sem, base["t20"] + u + 1)
                            S["dmg" if u % 2 == 0 else "dmi"].inc(sync.dma_start(
                                y_d[j * 256 + qs * 128: j * 256 + (qs + 1) * 128,
                                    pr * 256:(pr + 1) * 256], ybf[u % 2][:]), 16)
                    sync.wait_ge(S["dmg"].sem, S["dmg"].n)
                    sync.wait_ge(S["dmi"].sem, S["dmi"].n)

                @blk.tensor
                def _(t_):
                    t_.wait_ge(S["dma"].sem, base["dma"] + 64)
                    for i, (j, kt, kind) in enumerate(iters):
                        sb = p_s[i % 2]
                        if i >= 2:
                            t_.wait_ge(S["t15"].sem, base["t15"] + nf[i - 2])
                            t_.wait_ge(S["t13"].sem, base["t13"] + nm[i - 2])
                        if kind == "half":
                            qsl = slice(j * 256 + 128, j * 256 + 256)
                            t_.matmul(sb[:, 0:128], ks_[0][:, kt * 128:(kt + 1) * 128],
                                      qs_[0][:, qsl], start=True, stop=True,
                                      skip_group_check=True)
                            ins = t_.matmul(sb[:, 128:256], ks_[1][:, kt * 128:(kt + 1) * 128],
                                            qs_[1][:, qsl], start=True, stop=True,
                                            skip_group_check=True)
                        else:
                            qsl = slice(j * 256, j * 256 + 256)
                            t_.matmul(sb[:, 0:256], ks_[0][:, kt * 128:(kt + 1) * 128],
                                      qs_[0][:, qsl], start=True, stop=True,
                                      skip_group_check=True)
                            ins = t_.matmul(sb[:, 256:512], ks_[1][:, kt * 128:(kt + 1) * 128],
                                            qs_[1][:, qsl], start=True, stop=True,
                                            skip_group_check=True)
                        S["t14"].inc(ins)
                        if i >= 1:
                            _p2_pv(nc, t_, S, base, iters[i - 1], i - 1, P2b, p_o, p_z,
                                   v_sb, ones_col, nf[i - 1], nm[i - 1])
                    _p2_pv(nc, t_, S, base, iters[n_it - 1], n_it - 1, P2b, p_o, p_z,
                           v_sb, ones_col, nf[n_it - 1], nm[n_it - 1])

                @blk.scalar
                def _(s):
                    da = 0
                    for i, (j, kt, kind) in enumerate(iters):
                        s.wait_ge(S["t14"].sem, base["t14"] + i + 1)
                        if i >= 2:
                            s.wait_ge(S["t16"].sem, base["t16"] + i - 1)
                        w = 256 if kind == "half" else 512
                        ins = s.activation(P2b[i % 2][:, 0:w], p_s[i % 2][:, 0:w],
                                           AF.Exp, scale=SCALE)
                        if kind == "full":
                            S["t15"].inc(ins)
                        else:
                            S["t17"].inc(ins)
                            da += 1
                        if kt == min(5, 2 * j + 1) and j >= 1:
                            for qs in range(2):
                                u = 2 * (j - 1) + qs
                                s.wait_ge(S["t18"].sem, base["t18"] + u + 1)
                                S["rbs"].inc(s.activation(
                                    esq[:], eyg[:, qs * 256:(qs + 1) * 256],
                                    AF.Square, accum_out=es[:, qs:qs + 1]))
                                s.wait_ge(S["rbs"].sem, S["rbs"].n)
                                S["rbs"].inc(s.activation(
                                    es[:, 2 + qs:3 + qs], es[:, qs:qs + 1], AF.Ln,
                                    scale=1.0 / 256.0, bias=consts[:, 1:2]))
                                s.wait_ge(S["rbs"].sem, S["rbs"].n)
                                S["t19"].inc(s.activation(
                                    es[:, 4 + qs:5 + qs], es[:, 2 + qs:3 + qs], AF.Exp,
                                    scale=-0.5, bias=consts[:, 3:4]))
                    for qs in range(2):
                        u = 2 * (NQB - 1) + qs
                        s.wait_ge(S["t18"].sem, base["t18"] + u + 1)
                        S["rbs"].inc(s.activation(
                            esq[:], eyg[:, qs * 256:(qs + 1) * 256],
                            AF.Square, accum_out=es[:, qs:qs + 1]))
                        s.wait_ge(S["rbs"].sem, S["rbs"].n)
                        S["rbs"].inc(s.activation(
                            es[:, 2 + qs:3 + qs], es[:, qs:qs + 1], AF.Ln,
                            scale=1.0 / 256.0, bias=consts[:, 1:2]))
                        s.wait_ge(S["rbs"].sem, S["rbs"].n)
                        S["t19"].inc(s.activation(
                            es[:, 4 + qs:5 + qs], es[:, 2 + qs:3 + qs], AF.Exp,
                            scale=-0.5, bias=consts[:, 3:4]))

                @blk.vector
                def _(v):
                    da = 0
                    for i, (j, kt, kind) in enumerate(iters):
                        if kind == "diag0":
                            da += 1
                            v.wait_ge(S["t17"].sem, base["t17"] + da)
                            S["t13"].inc(v.tensor_mul(
                                P2b[i % 2][:].rearrange("p (s x) -> p s x", s=2),
                                P2b[i % 2][:].rearrange("p (s x) -> p s x", s=2),
                                maskA[:].rearrange("p (o x) -> p o x", o=1
                                                   ).broadcast_to([128, 2, 256])))
                        elif kind == "half":
                            da += 1
                            v.wait_ge(S["t17"].sem, base["t17"] + da)
                            S["t13"].inc(v.tensor_mul(
                                P2b[i % 2][:, 0:256].rearrange("p (s x) -> p s x", s=2),
                                P2b[i % 2][:, 0:256].rearrange("p (s x) -> p s x", s=2),
                                maskA[:, 0:128].rearrange("p (o x) -> p o x", o=1
                                                          ).broadcast_to([128, 2, 128])))
                            _p2_epi(nc, v, S, base, j, pr, cum_it[j], p_o, p_z,
                                    eR, ey, ey2, eyg, es, ybf, g_sb, lam_b)

                @blk.gpsimd
                def _(g):
                    pass

    return


def _p2_pv(nc, t_, S, base, it, i, P2b, p_o, p_z, v_sb, ones_col, nfi, nmi):
    j, kt, kind = it
    og = 2 * (j % 2)
    if i == 0:
        t_.wait_ge(S["dmx"].sem, base["dmx"] + 16)
    if kt == 0 and j >= 2:
        t_.wait_ge(S["t21"].sem, base["t21"] + j - 1)
    if kind == "full":
        t_.wait_ge(S["t15"].sem, base["t15"] + nfi)
    else:
        t_.wait_ge(S["t13"].sem, base["t13"] + nmi)
    combos = [(0, 1), (1, 1)] if kind == "half" else [(0, 0), (0, 1), (1, 0), (1, 1)]
    last = None
    for (st, qs) in combos:
        off = st * 128 if kind == "half" else st * 256 + qs * 128
        psl = P2b[i % 2][:, off:off + 128]
        t_.matmul(p_o[og + st][:, qs * 256:(qs + 1) * 256], psl,
                  v_sb[:, kt * VD:(kt + 1) * VD],
                  start=(kt == 0 and qs == 0), stop=False, skip_group_check=True)
        last = t_.matmul(p_z[j % 2][:, st * 2 + qs:st * 2 + qs + 1], psl, ones_col[:],
                         start=(kt == 0 and st == 0 and qs == 0), stop=False,
                         skip_group_check=True)
    S["t16"].inc(last)


def _p2_epi(nc, v, S, base, j, pr, cum, p_o, p_z, eR, ey, ey2, eyg, es, ybf,
            g_sb, lam_b):
    og = 2 * (j % 2)
    if j == 0:
        v.wait_ge(S["dmc"].sem, base["dmc"] + 16)
    v.wait_ge(S["t16"].sem, base["t16"] + cum)
    S["rb"].inc(v.reciprocal(eR[:, 0:4], p_z[j % 2][:]))
    v.wait_ge(S["rb"].sem, S["rb"].n)
    S["rb"].inc(v.tensor_scalar_mul(eR[:, 4:6], eR[:, 2:4], lam_b[:, pr:pr + 1]))
    v.wait_ge(S["rb"].sem, S["rb"].n)
    for qs in range(2):
        S["rb"].inc(v.tensor_scalar_mul(
            ey[:], p_o[og + 1][:, qs * 256:(qs + 1) * 256], eR[:, 4 + qs:5 + qs]))
        v.wait_ge(S["rb"].sem, S["rb"].n)
        S["rb"].inc(v.scalar_tensor_tensor(
            ey2[:], p_o[og][:, qs * 256:(qs + 1) * 256], eR[:, qs:qs + 1], ey[:],
            ALU.mult, ALU.subtract))
        v.wait_ge(S["rb"].sem, S["rb"].n)
        S["t18"].inc(v.tensor_mul(
            eyg[:, qs * 256:(qs + 1) * 256], ey2[:],
            g_sb[:, (2 * j + qs) * VD:(2 * j + qs + 1) * VD]))
        v.wait_ge(S["rb"].sem, S["rb"].n)
    for qs in range(2):
        u = 2 * j + qs
        v.wait_ge(S["t19"].sem, base["t19"] + u + 1)
        if u >= 2:
            cu = "dmg" if u % 2 == 0 else "dmi"
            v.wait_ge(S[cu].sem, base[cu] + 16 * ((u - 2) // 2 + 1))
        S["t20"].inc(v.tensor_scalar_mul(ybf[u % 2][:], eyg[:, qs * 256:(qs + 1) * 256],
                                         es[:, 4 + qs:5 + qs]))
    S["t21"].inc(v.tensor_copy(eR[:, 6:7], eR[:, 0:1]))


def _p4(nc, S, env):
    x_d, cc_d, y_out = env["x_d"], env["cc_d"], env["y_out"]
    wa_sb, bc_sb, br_b, ident = env["wa_sb"], env["bc_sb"], env["br_b"], env["ident"]
    consts, maskA = env["consts"], env["maskA"]
    ccq, xq, xq_done = env["ccq"], env["xq"], env["xq_done"]
    dbg_aps = env.get("dbg")
    NT4 = T // 512
    QT = T // 4

    with nc.Block() as blk, ExitStack() as tst:
        p_tpA = tst.enter_context(nc.psum_tensor("p4_tpA", [128, 1024], BF16))
        p_tpB = tst.enter_context(nc.psum_tensor("p4_tpB", [128, 1024], BF16))
        p_h = tst.enter_context(nc.psum_tensor("p4_h", [128, 512], F32))
        p_lg = tst.enter_context(nc.psum_tensor("p4_lg", [128, 8], F32))
        p_out = tst.enter_context(nc.psum_tensor("p4_out", [128, 2048], F32))
        ytile = [tst.enter_context(nc.sbuf_tensor(f"p4_y{i}", [128, D], BF16)) for i in range(2)]
        xtile = [tst.enter_context(nc.sbuf_tensor(f"p4_x{i}", [128, D], F32)) for i in range(2)]
        lny = tst.enter_context(nc.sbuf_tensor("p4_lny", [128, D], F32))
        x1 = [tst.enter_context(nc.sbuf_tensor(f"p4_x1{i}", [128, D], F32)) for i in range(4)]
        xf = [tst.enter_context(nc.sbuf_tensor(f"p4_xf{i}", [128, D], BF16)) for i in range(2)]
        xfT = tst.enter_context(nc.sbuf_tensor("p4_xfT", [128, D], BF16))
        hmT = tst.enter_context(nc.sbuf_tensor("p4_hmT", [128, 512], BF16))
        hm = tst.enter_context(nc.sbuf_tensor("p4_hm", [128, 512], BF16))
        junk = tst.enter_context(nc.sbuf_tensor("p4_junk", [128, D], BF16))
        st = [tst.enter_context(nc.sbuf_tensor(f"p4_st{i}", [128, 8], F32)) for i in range(2)]
        rr1 = [tst.enter_context(nc.sbuf_tensor(f"p4_rr1{i}", [128, 8], F32)) for i in range(2)]
        rr = tst.enter_context(nc.sbuf_tensor("p4_rr", [128, 32], F32))
        lnf = tst.enter_context(nc.sbuf_tensor("p4_lnf", [128, D], F32))
        sqa = tst.enter_context(nc.sbuf_tensor("p4_sqa", [128, D], F32))
        sqb = tst.enter_context(nc.sbuf_tensor("p4_sqb", [128, D], F32))
        osb = tst.enter_context(nc.sbuf_tensor("p4_osb", [128, D], F32))
        fo = [tst.enter_context(nc.sbuf_tensor(f"p4_fo{i}", [128, D], F32)) for i in range(2)]
        base = {k: S[k].n for k in S}

        def tgt_tpA(t):
            return base["t13"] + 2 * t + 1

        def tgt_tpB(t):
            return base["t13"] + 2 * t + 2

        def tgt_evB(t):
            return base["t22"] + 2 * t + 2

        NIT = NT4 + 4

        @blk.sync
        def _(sync):
            pid = nc.partition_id([ET.SP])
            qoff = nc.snap((pid % 4) * QT, min_val=0, max_val=3 * QT)
            for rk in range(4):
                S["dmc"].inc(sync.dma_start(
                    ccq[rk], cc_d[rk * T:(rk + 1) * T, :][bass.ds(qoff, QT), :]), 16)
            sync.wait_ge(S["dmc"].sem, S["dmc"].n)
            for t in range(NIT):
                if 0 <= t - 2 < NT4:
                    u = t - 2
                    sync.wait_ge(S["t19"].sem, base["t19"] + u + 1)
                    if u >= 1:
                        sync.wait_ge(S["t20"].sem, base["t20"] + u)
                    for i in range(4):
                        S["dmc"].inc(sync.dma_start_transpose(
                            hmT[:, i * 128:(i + 1) * 128],
                            hm[:, i * 128:(i + 1) * 128]), 16)
                if 0 <= t - 4 < NT4:
                    w = t - 4
                    sync.wait_ge(S["u3"].sem, base["u3"] + w + 1)
                    S["dmg" if w % 2 == 0 else "dmi"].inc(sync.dma_start(
                        y_out[w * 128:(w + 1) * 128, :], fo[w % 2][:]), 16)
            if dbg_aps:
                tl = NT4 - 1
                for nm, tens in [("d_p4x1", x1[tl % 4]), ("d_p4xf", xf[tl % 2]),
                                 ("d_p4hm", hm), ("d_p4rr", rr), ("d_p4yt", ytile[tl % 2]),
                                 ("d_p4br", br_b), ("d_p4xfT", xfT),
                                 ("d_p4wa", wa_sb[:, 0:1040])]:
                    S["dmg"].inc(sync.dma_start(dbg_aps[nm], tens[:]), 16)
            sync.wait_ge(S["dmg"].sem, S["dmg"].n)
            sync.wait_ge(S["dmi"].sem, S["dmi"].n)

        @blk.vector
        def _(v):
            for t in range(NIT):
                if 0 <= t - 1 < NT4:
                    w1_ = t - 1
                    v.wait_ge(S["t13"].sem, tgt_tpA(w1_))
                    S["t22"].inc(v.tensor_copy(xfT[:, 0:1024], p_tpA[:]))
                    v.wait_ge(S["t13"].sem, tgt_tpB(w1_))
                    S["t22"].inc(v.tensor_copy(xfT[:, 1024:2048], p_tpB[:]))
                if t < NT4:
                    pr = t % 2
                    lnm = "dma" if t % 2 == 0 else "dmh"
                    v.wait_ge(S[lnm].sem, base[lnm] + 80 * ((t // 2) + 1))
                    S["rb"].inc(v.reduce_sum(st[pr][:, 0:1], ytile[pr][:], axis=AX.X))
                    v.wait_ge(S["rb"].sem, S["rb"].n)
                    S["rb"].inc(v.tensor_scalar_mul(st[pr][:, 2:3], st[pr][:, 0:1], 1.0 / D))
                    v.wait_ge(S["rb"].sem, S["rb"].n)
                    v.wait_ge(S["t1"].sem, base["t1"] + t + 1)
                    S["rb"].inc(v.tensor_mul(st[pr][:, 3:4], st[pr][:, 2:3], st[pr][:, 2:3]))
                    v.wait_ge(S["rb"].sem, S["rb"].n)
                    S["t2"].inc(v.scalar_tensor_tensor(st[pr][:, 4:5], st[pr][:, 1:2],
                                                       1.0 / D, st[pr][:, 3:4],
                                                       ALU.mult, ALU.subtract))
                    v.wait_ge(S["t3"].sem, base["t3"] + t + 1)
                    S["t4"].inc(v.scalar_tensor_tensor(st[pr][:, 6:7], st[pr][:, 2:3],
                                                       -1.0, st[pr][:, 5:6],
                                                       ALU.mult, ALU.mult))
                    v.wait_ge(S["t5"].sem, base["t5"] + t + 1)
                    if t >= 3:
                        v.wait_ge(S["u3"].sem, base["u3"] + t - 3)
                    S["t6"].inc(v.tensor_add(x1[t % 4][:], xtile[pr][:], lny[:]))
                    S["rb"].inc(v.reduce_sum(rr1[pr][:, 0:1], x1[t % 4][:], axis=AX.X))
                    v.wait_ge(S["rb"].sem, S["rb"].n)
                    S["rb"].inc(v.tensor_scalar_mul(rr1[pr][:, 2:3], rr1[pr][:, 0:1], 1.0 / D))
                    v.wait_ge(S["rb"].sem, S["rb"].n)
                    v.wait_ge(S["t7"].sem, base["t7"] + t + 1)
                    S["rb"].inc(v.tensor_mul(rr1[pr][:, 3:4], rr1[pr][:, 2:3], rr1[pr][:, 2:3]))
                    v.wait_ge(S["rb"].sem, S["rb"].n)
                    S["t8"].inc(v.scalar_tensor_tensor(rr1[pr][:, 4:5], rr1[pr][:, 1:2],
                                                       1.0 / D, rr1[pr][:, 3:4],
                                                       ALU.mult, ALU.subtract))
                    v.wait_ge(S["t9"].sem, base["t9"] + t + 1)
                    S["t10"].inc(v.scalar_tensor_tensor(rr1[pr][:, 6:7], rr1[pr][:, 2:3],
                                                        -1.0, rr1[pr][:, 5:6],
                                                        ALU.mult, ALU.mult))
                if 0 <= t - 1 < NT4:
                    u = t - 1
                    v.wait_ge(S["t16"].sem, base["t16"] + u + 1)
                    S["rb"].inc(v.tensor_add(rr[:, 8:16], p_lg[:], br_b[:]))
                    v.wait_ge(S["rb"].sem, S["rb"].n)
                    S["rb"].inc(v.reduce_max(rr[:, 16:17], rr[:, 8:16], axis=AX.X))
                    v.wait_ge(S["rb"].sem, S["rb"].n)
                    S["t17"].inc(v.tensor_scalar_mul(rr[:, 17:18], rr[:, 16:17], -1.0))
                    v.wait_ge(S["t18"].sem, base["t18"] + u + 1)
                    S["rb"].inc(v.reciprocal(rr[:, 19:20], rr[:, 18:19]))
                    v.wait_ge(S["rb"].sem, S["rb"].n)
                    S["rb"].inc(v.tensor_single_scalar(rr[:, 24:32], rr[:, 8:16],
                                                       rr[:, 16:17], ALU.is_ge))
                    v.wait_ge(S["rb"].sem, S["rb"].n)
                    S["rb"].inc(v.tensor_scalar(rr[:, 24:32], rr[:, 24:32], rr[:, 19:20],
                                                1.0 / 64.0, ALU.mult, ALU.mult))
                    v.wait_ge(S["rb"].sem, S["rb"].n)
                    if u >= 1:
                        v.wait_ge(S["dmc"].sem, base["dmc"] + 64 + 64 * u)
                    S["t19"].inc(v.tensor_mul(
                        hm[:].rearrange("p (e r) -> p e r", e=8),
                        p_h[:].rearrange("p (e r) -> p e r", e=8),
                        rr[:, 24:32].rearrange("p (e o) -> p e o", o=1
                                               ).broadcast_to([128, 8, 64])))
                if 0 <= t - 3 < NT4:
                    w = t - 3
                    v.wait_ge(S["u5"].sem, base["u5"] + w + 1)
                    v.wait_ge(S["t21"].sem, base["t21"] + w + 1)
                    S["rb"].inc(v.tensor_scalar_mul(rr[:, 3:4], rr[:, 0:1], 1.0 / D))
                    v.wait_ge(S["rb"].sem, S["rb"].n)
                    S["rb"].inc(v.tensor_mul(rr[:, 20:21], rr[:, 3:4], rr[:, 3:4]))
                    v.wait_ge(S["rb"].sem, S["rb"].n)
                    S["u1"].inc(v.scalar_tensor_tensor(rr[:, 4:5], rr[:, 1:2], 1.0 / D,
                                                       rr[:, 20:21], ALU.mult, ALU.subtract))
                    v.wait_ge(S["u2"].sem, base["u2"] + w + 1)
                    S["rb"].inc(v.scalar_tensor_tensor(rr[:, 6:7], rr[:, 3:4], -1.0,
                                                       rr[:, 5:6], ALU.mult, ALU.mult))
                    v.wait_ge(S["rb"].sem, S["rb"].n)
                    if w >= 1:
                        v.wait_ge(S["u3"].sem, base["u3"] + w)
                    S["t15"].inc(v.tensor_scalar(lnf[:], osb[:], rr[:, 5:6], rr[:, 6:7],
                                                 ALU.mult, ALU.add))

        @blk.scalar
        def _(s):
            def loads(s_):
                lnm = "dma" if s_ % 2 == 0 else "dmh"
                for rk in range(4):
                    S[lnm].inc(s.dma_start(
                        ytile[s_ % 2][:, rk * 512:(rk + 1) * 512],
                        ccq[rk, s_ * 128:(s_ + 1) * 128, :]), 16)
                S[lnm].inc(s.dma_start(
                    xtile[s_ % 2][:], xq[s_ * 128:(s_ + 1) * 128, :]), 16)

            for t in range(NIT):
                if t == 0:
                    s.wait_ge(S["dmc"].sem, base["dmc"] + 64)
                    s.wait_ge(S["dmq"].sem, xq_done)
                    for s_ in (0, 1):
                        if s_ < NT4:
                            loads(s_)
                if 0 <= t - 3 < NT4:
                    w = t - 3
                    if w >= 1:
                        s.wait_ge(S["t15"].sem, base["t15"] + w)
                    S["u5"].inc(s.activation(osb[:], p_out[:], AF.Abs,
                                             accum_out=rr[:, 0:1]))
                if t < NT4:
                    pr = t % 2
                    lnm = "dma" if t % 2 == 0 else "dmh"
                    s.wait_ge(S[lnm].sem, base[lnm] + 80 * ((t // 2) + 1))
                    S["t1"].inc(s.activation(junk[:], ytile[pr][:], AF.Square,
                                             accum_out=st[pr][:, 1:2]))
                    s.wait_ge(S["t2"].sem, base["t2"] + t + 1)
                    S["rbs"].inc(s.activation(st[pr][:, 7:8], st[pr][:, 4:5], AF.Ln,
                                              bias=consts[:, 0:1]))
                    s.wait_ge(S["rbs"].sem, S["rbs"].n)
                    S["t3"].inc(s.activation(st[pr][:, 5:6], st[pr][:, 7:8], AF.Exp,
                                             scale=-0.5))
                    s.wait_ge(S["t4"].sem, base["t4"] + t + 1)
                    if t >= 1:
                        s.wait_ge(S["t6"].sem, base["t6"] + t)
                    S["t5"].inc(s.activation(lny[:], ytile[pr][:], AF.Identity,
                                             bias=st[pr][:, 6:7], scale=st[pr][:, 5:6]))
                    s.wait_ge(S["t6"].sem, base["t6"] + t + 1)
                    S["t7"].inc(s.activation(junk[:], x1[t % 4][:], AF.Square,
                                             accum_out=rr1[pr][:, 1:2]))
                    if t + 2 < NT4:
                        loads(t + 2)
                    s.wait_ge(S["t8"].sem, base["t8"] + t + 1)
                    S["rbs"].inc(s.activation(rr1[pr][:, 7:8], rr1[pr][:, 4:5], AF.Ln,
                                              bias=consts[:, 0:1]))
                    s.wait_ge(S["rbs"].sem, S["rbs"].n)
                    S["t9"].inc(s.activation(rr1[pr][:, 5:6], rr1[pr][:, 7:8], AF.Exp,
                                             scale=-0.5))
                    s.wait_ge(S["t10"].sem, base["t10"] + t + 1)
                    if t >= 2:
                        s.wait_ge(S["t13"].sem, tgt_tpB(t - 2))
                    S["t11"].inc(s.activation(xf[pr][:], x1[t % 4][:], AF.Identity,
                                              bias=rr1[pr][:, 6:7], scale=rr1[pr][:, 5:6]))
                if 0 <= t - 1 < NT4:
                    u = t - 1
                    s.wait_ge(S["t17"].sem, base["t17"] + u + 1)
                    S["t18"].inc(s.activation(junk[:, 0:8], rr[:, 8:16], AF.Exp,
                                              bias=rr[:, 17:18], accum_out=rr[:, 18:19]))
                    s.wait_ge(S["t18"].sem, base["t18"] + u + 1)
                if 0 <= t - 3 < NT4:
                    w = t - 3
                    s.wait_ge(S["u1"].sem, base["u1"] + w + 1)
                    S["rbs"].inc(s.activation(rr[:, 7:8], rr[:, 4:5], AF.Ln,
                                              bias=consts[:, 0:1]))
                    s.wait_ge(S["rbs"].sem, S["rbs"].n)
                    S["u2"].inc(s.activation(rr[:, 5:6], rr[:, 7:8], AF.Exp, scale=-0.5))
                if 0 <= t - 2 < NT4:
                    q = t - 2
                    s.wait_ge(S["t20"].sem, base["t20"] + q + 1)
                    if q >= 1:
                        s.wait_ge(S["u1"].sem, base["u1"] + q)
                    S["t21"].inc(s.activation(sqb[:], p_out[:], AF.Square,
                                              accum_out=rr[:, 1:2]))

        @blk.gpsimd
        def _(g):
            for t in range(3, NIT):
                w = t - 3
                if w >= NT4:
                    continue
                g.wait_ge(S["t15"].sem, base["t15"] + w + 1)
                g.wait_ge(S["t6"].sem, base["t6"] + w + 1)
                if w >= 2:
                    ct = "dmg" if (w - 2) % 2 == 0 else "dmi"
                    g.wait_ge(S[ct].sem, base[ct] + 16 * ((w - 2) // 2 + 1))
                S["u3"].inc(g.tensor_add(fo[w % 2][:], lnf[:], x1[w % 4][:]))

        @blk.tensor
        def _(t_):
            for t in range(NIT):
                if 0 <= t - 1 < NT4:
                    w1_ = t - 1
                    t_.wait_ge(S["t11"].sem, base["t11"] + w1_ + 1)
                    if w1_ >= 1:
                        t_.wait_ge(S["t22"].sem, tgt_evB(w1_ - 1))
                    for i in range(8):
                        ins = t_.transpose(p_tpA[:, i * 128:(i + 1) * 128],
                                           xf[w1_ % 2][:, i * 128:(i + 1) * 128], ident[:])
                        if i == 7:
                            S["t13"].inc(ins)
                    for i in range(8):
                        ins = t_.transpose(p_tpB[:, i * 128:(i + 1) * 128],
                                           xf[w1_ % 2][:, (8 + i) * 128:(9 + i) * 128], ident[:])
                        if i == 7:
                            S["t13"].inc(ins)
                    t_.wait_ge(S["t22"].sem, tgt_evB(w1_))
                    t_.wait_ge(S["t19"].sem, base["t19"] + w1_)
                    for c in range(16):
                        t_.matmul(p_h[:], xfT[:, c * 128:(c + 1) * 128],
                                  wa_sb[:, c * 520:c * 520 + 512],
                                  start=(c == 0), stop=(c == 15), skip_group_check=True)
                    for c in range(16):
                        ins = t_.matmul(p_lg[:], xfT[:, c * 128:(c + 1) * 128],
                                        wa_sb[:, c * 520 + 512:(c + 1) * 520],
                                        start=(c == 0), stop=(c == 15), skip_group_check=True)
                        if c == 15:
                            S["t16"].inc(ins)
                if 0 <= t - 2 < NT4:
                    u = t - 2
                    t_.wait_ge(S["dmc"].sem, base["dmc"] + 64 + 64 * (u + 1))
                    if u >= 1:
                        t_.wait_ge(S["t21"].sem, base["t21"] + u)
                    if u >= 1:
                        t_.wait_ge(S["u5"].sem, base["u5"] + u)
                    for rc in range(4):
                        hc = hmT[:, rc * 128:(rc + 1) * 128]
                        for sx in range(4):
                            t_.matmul(
                                p_out[:, sx * 512:(sx + 1) * 512], hc,
                                bc_sb[:, rc * 2048 + sx * 512:rc * 2048 + (sx + 1) * 512],
                                start=(rc == 0), stop=False, skip_group_check=True)
                    for sx in range(4):
                        for hh in range(2):
                            ins = t_.matmul(
                                p_out[:, sx * 512 + hh * 256:sx * 512 + (hh + 1) * 256],
                                maskA[0:1, 0:128], maskA[0:1, 0:256],
                                start=False, stop=True, skip_group_check=True)
                            if sx == 3 and hh == 1:
                                S["t20"].inc(ins)



def make_in_maps(L0_attn_hidden_states, L0_attn_W_qkv, L0_attn_lambda_q1,
                 L0_attn_lambda_k1, L0_attn_lambda_q2, L0_attn_lambda_k2,
                 L0_attn_W_g, L0_ffn_A, L0_ffn_B, L0_ffn_Wr, L0_ffn_br):
    x = np.asarray(L0_attn_hidden_states)
    wqkv = np.asarray(L0_attn_W_qkv)
    wg = np.asarray(L0_attn_W_g)
    A = np.asarray(L0_ffn_A)
    B = np.asarray(L0_ffn_B)
    Wr = np.asarray(L0_ffn_Wr)
    br = np.asarray(L0_ffn_br)
    lq1, lk1 = np.asarray(L0_attn_lambda_q1), np.asarray(L0_attn_lambda_k1)
    lq2, lk2 = np.asarray(L0_attn_lambda_q2), np.asarray(L0_attn_lambda_k2)

    wa = np.concatenate([A.transpose(2, 0, 1).reshape(D, 512),
                         Wr.T], axis=1).astype(np.float32)
    bcat = np.ascontiguousarray(B.transpose(0, 2, 1).reshape(512, D), dtype=np.float32)

    in_maps = []
    for c in range(8):
        b, pg = c // 4, c % 4
        blocks = []
        for i in range(4):
            h = 4 * pg + i
            blocks.append(wqkv[:, h * 384:h * 384 + 128])
            blocks.append(wqkv[:, h * 384 + 128:h * 384 + 256])
        for i in range(4):
            h = 4 * pg + i
            blocks.append(wqkv[:, h * 384 + 256:h * 384 + 384])
        blocks.append(wg[:, pg * 512:(pg + 1) * 512])
        wproj = np.ascontiguousarray(np.concatenate(blocks, axis=1), dtype=np.float32)
        prs = [2 * pg, 2 * pg + 1]
        lam4 = np.ascontiguousarray(
            np.stack([lq1[prs], lk1[prs], lq2[prs], lk2[prs]]), dtype=np.float32)
        in_maps.append({
            "x": np.ascontiguousarray(x[b], dtype=np.float32),
            "wproj": wproj,
            "wa": np.ascontiguousarray(wa),
            "bcat": bcat,
            "br": np.ascontiguousarray(br.reshape(1, 8), dtype=np.float32),
            "lam4": lam4,
        })
    return in_maps


def kernel(L0_attn_hidden_states, L0_attn_W_qkv, L0_attn_lambda_q1,
           L0_attn_lambda_k1, L0_attn_lambda_q2, L0_attn_lambda_k2,
           L0_attn_W_g, L0_ffn_A, L0_ffn_B, L0_ffn_Wr, L0_ffn_br):
    nc, _ = build_program()
    in_maps = make_in_maps(L0_attn_hidden_states, L0_attn_W_qkv, L0_attn_lambda_q1,
                           L0_attn_lambda_k1, L0_attn_lambda_q2, L0_attn_lambda_k2,
                           L0_attn_W_g, L0_ffn_A, L0_ffn_B, L0_ffn_Wr, L0_ffn_br)
    res = run_bass_kernel_spmd(nc, in_maps, core_ids=list(range(8)))
    out = np.empty((2, T, D), np.float32)
    for c in range(8):
        b, q = c // 4, c % 4
        out[b, q * 2048:(q + 1) * 2048] = res.results[c]["y_out"]
    return out
